# version 12
# baseline (speedup 1.0000x reference)
import math
from contextlib import ExitStack

import numpy as np
import ml_dtypes

import concourse.bass as bass
import concourse.mybir as mybir
from concourse.alu_op_type import AluOpType
from concourse.bass_utils import run_bass_kernel_spmd
from concourse.aot_env import allow_ici_all_to_all

F32 = mybir.dt.float32
BF16 = mybir.dt.bfloat16
I8 = mybir.dt.int8
AF = mybir.ActivationFunctionType
AX = mybir.AxisListType
Alu = AluOpType

B, S, E, H, D, DE = 2, 4096, 2048, 16, 128, 8192
TOK = B * S
NC = 8
TPC = TOK // NC
NT = TOK // 128
NTB = S // 128
ET = E // 128
XT = DE // 128
MASKV = -1.0e5

RC_Q, RC_K, RC_V, RC_O, RC_UP, RC_GATE, RC_DOWN = range(7)
SC_Q, SC_K, SC_V, SC_O, SC_UP, SC_GATE, SC_DOWN = range(7, 14)

DEBUG = False


class DmaGroup:
    __slots__ = ("sem", "target")

    def __init__(self, sem, target):
        self.sem = sem
        self.target = target


class DmaPool:

    def __init__(self, nc, es, n=24):
        self.sems = [es.enter_context(nc.semaphore(f"dpool{i}")) for i in range(n)]
        self.tot = [0] * n
        self.idx = 0

    def issue(self, instrs):
        i = self.idx % len(self.sems)
        self.idx += 1
        for ins in instrs:
            ins.then_inc(self.sems[i], 16)
            self.tot[i] += 16
        return DmaGroup(self.sems[i], self.tot[i])

    def wait(self, eng, grp):
        eng.wait_ge(grp.sem, grp.target)


def build_program():
    nc = bass.Bass("TRN2", target_bir_lowering=False)

    x_full = nc.dram_tensor("x_full", [TOK, E], F32, kind="ExternalInput").ap()
    x_mine = nc.dram_tensor("x_mine", [TPC, E], F32, kind="ExternalInput").ap()
    wq_s = nc.dram_tensor("wq_s", [256, E], F32, kind="ExternalInput").ap()
    wk_s = nc.dram_tensor("wk_s", [256, E], F32, kind="ExternalInput").ap()
    wv_s = nc.dram_tensor("wv_s", [256, E], F32, kind="ExternalInput").ap()
    wo_s = nc.dram_tensor("wo_s", [256, E], F32, kind="ExternalInput").ap()
    up_s = nc.dram_tensor("up_s", [1024, E], F32, kind="ExternalInput").ap()
    gate_s = nc.dram_tensor("gate_s", [1024, E], F32, kind="ExternalInput").ap()
    down_s = nc.dram_tensor("down_s", [256, DE], F32, kind="ExternalInput").ap()
    cmask_in = nc.dram_tensor("cmask_in", [128, 896], F32, kind="ExternalInput").ap()
    ident_in = nc.dram_tensor("ident_in", [128, 128], BF16, kind="ExternalInput").ap()
    out_ext = nc.dram_tensor("out", [TPC, E], F32, kind="ExternalOutput").ap()

    dbg = {}
    if DEBUG:
        for nm, sh, dt in [("dbg_scales", [1, 16], F32), ("dbg_xq", [128, E], I8),
                           ("dbg_sm", [128, 80], F32), ("dbg_partials", [128, 8], F32),
                           ("dbg_qk", [128, 512], BF16), ("dbg_v", [128, 256], BF16),
                           ("dbg_attnout", [128, 256], BF16), ("dbg_oin", [128, 512], BF16),
                           ("dbg_x1", [128, 1024], BF16), ("dbg_h", [128, 512], BF16),
                           ("dbg_y", [128, 1024], BF16), ("dbg_sums", [128, 32], F32)]:
            dbg[nm] = nc.dram_tensor(nm, sh, dt, kind="ExternalOutput").ap()

    pt_hbm = nc.dram_tensor("pt_hbm", [128, 8], F32).ap()
    ar_in = nc.dram_tensor("ar_in", [1, 8], F32)
    ar_out = nc.dram_tensor("ar_out", [1, 8], F32)
    sc_hbm = nc.dram_tensor("sc_hbm", [1, 16], F32).ap()
    ag_wo_in = nc.dram_tensor("ag_wo_in", [E, 256], BF16)
    ag_wo_out = nc.dram_tensor("ag_wo_out", [NC * E, 256], BF16)
    ag_up_in = nc.dram_tensor("ag_up_in", [E, 1024], BF16)
    ag_up_out = nc.dram_tensor("ag_up_out", [NC * E, 1024], BF16)
    ag_gate_in = nc.dram_tensor("ag_gate_in", [E, 1024], BF16)
    ag_gate_out = nc.dram_tensor("ag_gate_out", [NC * E, 1024], BF16)
    ag_dn_in = nc.dram_tensor("ag_dn_in", [DE, 256], BF16)
    ag_dn_out = nc.dram_tensor("ag_dn_out", [NC * DE, 256], BF16)
    a2a_in = nc.dram_tensor("a2a_in", [TOK, 256], BF16)
    a2a_out = nc.dram_tensor("a2a_out", [TOK, 256], BF16)
    sums_bounce = nc.dram_tensor("sums_bounce", [1, S], BF16).ap()
    x1_hbm = nc.dram_tensor("x1_hbm", [TPC, E], F32).ap()

    rg = [list(range(NC))]

    es = ExitStack()
    with es:
        block = es.enter_context(nc.Block())
        sems = {n: es.enter_context(nc.semaphore("sem_" + n))
                for n in ["vector", "scalar", "tensor", "cc",
                          "ag", "out", "x1w", "a2aw"]}
        cnt = {n: 0 for n in sems}
        pool = DmaPool(nc, es)

        def inc(n, ins, k=1):
            ins.then_inc(sems[n], k)
            cnt[n] += k
            return cnt[n]

        def on(name, f):
            getattr(block, name)(f)

        squeue = []

        def sinc(ins, k=1):
            ins.then_inc(sems["scalar"], k)
            return ins

        def sq(fn, incs=1):
            squeue.append(fn)
            cnt["scalar"] += incs
            return cnt["scalar"]

        def sflush():
            if not squeue:
                return
            fns = list(squeue)
            squeue.clear()

            def run(s):
                for f in fns:
                    f(s)
            on("scalar", run)

        SB = lambda name, shape, dt: es.enter_context(nc.sbuf_tensor(name, shape, dt))

        scales_sb = SB("scales_sb", [128, 16], F32)
        eps_t = SB("eps_t", [128, 1], F32)
        neghalf_t = SB("neghalf_t", [128, 1], F32)
        ones_bf = SB("ones_bf", [128, 1], BF16)
        ident = SB("ident", [128, 128], BF16)
        cmask = SB("cmask", [128, 896], F32)
        sm = SB("smalls", [128, 80], F32)

        if DEBUG:
            snap = {
                "sm": SB("sn_sm", [128, 80], F32), "xq": SB("sn_xq", [128, E], I8),
                "ao": SB("sn_ao", [128, 256], BF16),
                "rec": SB("sn_rec", [128, NTB], F32), "oin": SB("sn_oin", [128, 512], BF16),
                "x1": SB("sn_x1", [128, 1024], BF16), "h": SB("sn_h", [128, 512], BF16),
                "y": SB("sn_y", [128, 1024], BF16),
            }

        es_a = ExitStack()
        wqkvT = es_a.enter_context(nc.sbuf_tensor("wqkvT", [128, ET * 768], BF16))

        g_const_h = [None]

        def p0s(sync):
            i1 = sync.dma_start(out=ident[:], in_=ident_in)
            i2 = sync.dma_start(out=cmask[:], in_=cmask_in)
            g_const_h[0] = pool.issue([i1, i2])
        on("sync", p0s)
        g_const = g_const_h[0]

        def p0v(v):
            v.memset(eps_t[:], 1.0e-5)
            v.memset(neghalf_t[:], -0.5)
            inc("vector", v.memset(ones_bf[:], 1.0))
        on("vector", p0v)

        es_p1 = ExitStack()
        wt0 = es_p1.enter_context(nc.sbuf_tensor("wt0", [128, DE], F32))
        wt1 = es_p1.enter_context(nc.sbuf_tensor("wt1", [128, DE], F32))
        wt2 = es_p1.enter_context(nc.sbuf_tensor("wt2", [128, DE], F32))
        partials = es_p1.enter_context(nc.sbuf_tensor("partials", [128, 8], F32))
        ptmp = es_p1.enter_context(nc.sbuf_tensor("ptmp", [128, 1], F32))
        pt7 = es_p1.enter_context(nc.sbuf_tensor("pt7", [8, 128], F32))
        pt7s = es_p1.enter_context(nc.sbuf_tensor("pt7s", [8, 1], F32))
        srow = es_p1.enter_context(nc.sbuf_tensor("srow", [1, 16], F32))
        q8t = es_p1.enter_context(nc.sbuf_tensor("q8t", [128, DE], I8))
        qbft = es_p1.enter_context(nc.sbuf_tensor("qbft", [128, DE], BF16))
        stage0 = es_p1.enter_context(nc.sbuf_tensor("stage0", [128, DE], BF16))
        stage1 = es_p1.enter_context(nc.sbuf_tensor("stage1", [128, DE], BF16))
        stageb = [stage0, stage1]
        ptpw0 = es_p1.enter_context(nc.psum_tensor("ptpw0", [128, 1024], BF16))
        ptpw1 = es_p1.enter_context(nc.psum_tensor("ptpw1", [128, 1024], BF16))
        wt = [wt0, wt1, wt2]
        ptpw = [ptpw0, ptpw1]

        wslices = [
            (wq_s, 256, E, RC_Q), (wk_s, 256, E, RC_K), (wv_s, 256, E, RC_V),
            (wo_s, 256, E, RC_O), (up_s, 1024, E, RC_UP),
            (gate_s, 1024, E, RC_GATE), (down_s, 256, DE, RC_DOWN),
        ]

        vdone = []
        gload = []
        i = 0
        for (ap, rows, cols, rc) in wslices:
            for r0 in range(0, rows, 128):
                buf = i % 3
                ii = i

                def ldw(q, ap=ap, r0=r0, buf=buf, ii=ii, cols=cols):
                    if ii >= 3:
                        q.wait_ge(sems["vector"], vdone[ii - 3])
                    ins = q.dma_start(out=wt[buf][:, 0:cols], in_=ap[r0:r0 + 128, :])
                    gload.append(pool.issue([ins]))
                on("sync" if ii % 2 == 0 else "scalar", ldw)

                def absr(v, rc=rc, buf=buf, cols=cols, ii=ii, first=(r0 == 0)):
                    pool.wait(v, gload[ii])
                    v.tensor_reduce(out=ptmp[:], in_=wt[buf][:, 0:cols],
                                    axis=AX.X, op=Alu.add, apply_absolute_value=True)
                    v.drain()
                    if first:
                        c = inc("vector", v.tensor_copy(out=partials[:, rc:rc + 1], in_=ptmp[:]))
                    else:
                        c = inc("vector", v.tensor_tensor(out=partials[:, rc:rc + 1],
                                                          in0=partials[:, rc:rc + 1],
                                                          in1=ptmp[:], op=Alu.add))
                    vdone.append(c)
                on("vector", absr)
                i += 1
        c_passA = cnt["vector"]

        g_pt = [None, None]

        def pr1(sync):
            sync.wait_ge(sems["vector"], c_passA)
            g_pt[0] = pool.issue([sync.dma_start(out=pt_hbm, in_=partials[:])])
            pool.wait(sync, g_pt[0])
            g_pt[1] = pool.issue([sync.dma_start(
                out=pt7[:], in_=pt_hbm.rearrange("p t -> t p"),
                allow_slow_non_contiguous=True)])
        on("sync", pr1)

        def pr2(v):
            pool.wait(v, g_pt[1])
            inc("vector", v.tensor_reduce(out=pt7s[:], in_=pt7[:], axis=AX.X, op=Alu.add))
        on("vector", pr2)
        c_pr2 = cnt["vector"]

        g_arin_h = [None]

        def pr3(sync):
            sync.wait_ge(sems["vector"], c_pr2)
            dst = ar_in.ap().rearrange("x (t o) -> (x t) o", o=1)[0:8, :]
            g_arin_h[0] = pool.issue([sync.dma_start(out=dst, in_=pt7s[:],
                                                     allow_slow_non_contiguous=True)])
        on("sync", pr3)
        g_arin = g_arin_h[0]

        def cc1(g):
            pool.wait(g, g_arin)
            inc("cc", g.collective_compute("AllReduce", Alu.add, replica_groups=rg,
                                           ins=[ar_in.ap().opt()], outs=[ar_out.ap().opt()]))
        on("gpsimd", cc1)
        c_ar = cnt["cc"]

        numel = {RC_Q: E * E, RC_K: E * E, RC_V: E * E, RC_O: E * E,
                 RC_UP: DE * E, RC_GATE: DE * E, RC_DOWN: E * DE}
        g_srow_h = [None]

        def sc1(sync):
            sync.wait_ge(sems["cc"], c_ar)
            g_srow_h[0] = pool.issue([sync.dma_start(out=srow[0:1, 0:8],
                                                     in_=ar_out.ap()[0:1, 0:8])])
        on("sync", sc1)
        g_srow = g_srow_h[0]

        if DEBUG:
            def dpart(sync):
                sync.wait_ge(sems["vector"], c_passA)
                inc("out", sync.dma_start(out=dbg["dbg_partials"], in_=partials[:]), 16)
            on("sync", dpart)

        def sc2(v):
            pool.wait(v, g_srow)
            for k in range(7):
                v.tensor_scalar(out=srow[0:1, k:k + 1], in0=srow[0:1, k:k + 1],
                                scalar1=1.0 / numel[k], scalar2=None, op0=Alu.mult)
                v.drain()
            for k in (RC_Q, RC_K, RC_V, RC_O):
                v.tensor_scalar(out=srow[0:1, k:k + 1], in0=srow[0:1, k:k + 1],
                                scalar1=1.0e-5, scalar2=None, op0=Alu.max)
                v.drain()
            for k in (RC_UP, RC_GATE, RC_DOWN):
                v.tensor_scalar(out=srow[0:1, k:k + 1], in0=srow[0:1, k:k + 1],
                                scalar1=1.0e-8, scalar2=None, op0=Alu.add)
                v.drain()
            v.tensor_scalar(out=srow[0:1, SC_Q:SC_Q + 1], in0=srow[0:1, RC_Q:RC_Q + 1],
                            scalar1=1.0 / math.sqrt(D), scalar2=None, op0=Alu.mult)
            v.drain()
            for s_c, d_c in ((RC_K, SC_K), (RC_V, SC_V), (RC_O, SC_O),
                             (RC_UP, SC_UP), (RC_GATE, SC_GATE), (RC_DOWN, SC_DOWN)):
                v.tensor_copy(out=srow[0:1, d_c:d_c + 1], in_=srow[0:1, s_c:s_c + 1])
            v.drain()
            inc("vector", v.reciprocal(out=srow[0:1, 0:7], in_=srow[0:1, 0:7]))
        on("vector", sc2)
        c_sc2 = cnt["vector"]

        g_scales_h = [None]

        def sc3(sync):
            sync.wait_ge(sems["vector"], c_sc2)
            g1 = pool.issue([sync.dma_start(out=sc_hbm, in_=srow[:])])
            pool.wait(sync, g1)
            g_scales_h[0] = pool.issue([sync.dma_start(
                out=scales_sb[:], in_=sc_hbm[0:1, 0:16].partition_broadcast(128))])
            if DEBUG:
                inc("out", sync.dma_start(out=dbg["dbg_scales"], in_=srow[:]), 16)
        on("sync", sc3)
        g_scales = g_scales_h[0]

        qkv_off = {id(wq_s): 0, id(wk_s): 256, id(wv_s): 512}
        wjobs = [
            (wq_s, 256, E, RC_Q, None), (wk_s, 256, E, RC_K, None),
            (wv_s, 256, E, RC_V, None), (wo_s, 256, E, RC_O, ag_wo_in),
            (up_s, 1024, E, RC_UP, ag_up_in), (gate_s, 1024, E, RC_GATE, ag_gate_in),
            (down_s, 256, DE, RC_DOWN, ag_dn_in),
        ]
        ji = 0
        qdone = []
        gloadb = []
        tpfree = [0, 0]
        stg_ag = [0, 0]
        stg_vec = [0, 0]
        for (ap, rows, cols, rc, dest) in wjobs:
            nets = cols // 128
            for j in range(rows // 128):
                buf = ji % 3
                jj = ji

                def ldq(q, ap=ap, j=j, buf=buf, jj=jj, cols=cols):
                    pool.wait(q, g_scales)
                    if jj >= 3:
                        q.wait_ge(sems["vector"], qdone[jj - 3])
                    ins = q.dma_start(out=wt[buf][:, 0:cols],
                                      in_=ap[j * 128:(j + 1) * 128, :])
                    gloadb.append(pool.issue([ins]))
                on("sync" if jj % 2 == 0 else "scalar", ldq)

                def qn(v, rc=rc, buf=buf, cols=cols, jj=jj):
                    pool.wait(v, gloadb[jj])
                    pool.wait(v, g_scales)
                    v.tensor_scalar(out=wt[buf][:, 0:cols], in0=wt[buf][:, 0:cols],
                                    scalar1=scales_sb[:, rc:rc + 1], scalar2=1.0,
                                    op0=Alu.mult, op1=Alu.min)
                    v.drain()
                    v.tensor_scalar(out=q8t[:, 0:cols], in0=wt[buf][:, 0:cols],
                                    scalar1=-1.0, scalar2=None, op0=Alu.max)
                    v.drain()
                    c = inc("vector", v.tensor_copy(out=qbft[:, 0:cols], in_=q8t[:, 0:cols]))
                    qdone.append(c)
                on("vector", qn)
                c_q = cnt["vector"]

                for g0 in range(0, nets, 8):
                    gN = min(8, nets - g0)
                    pb = (g0 // 8) % 2

                    def tpw(t, g0=g0, gN=gN, pb=pb, c_q=c_q, sb_=ji % 2):
                        t.wait_ge(sems["vector"], c_q)
                        t.wait_ge(sems["vector"], tpfree[pb])
                        t.wait_ge(sems["ag"], stg_ag[sb_])
                        t.wait_ge(sems["vector"], stg_vec[sb_])
                        pool.wait(t, g_const)
                        for k in range(gN):
                            ins = t.transpose(out=ptpw[pb][:, 128 * k:128 * (k + 1)],
                                              in_=qbft[:, 128 * (g0 + k):128 * (g0 + k + 1)],
                                              identity=ident[:])
                            if k == gN - 1:
                                inc("tensor", ins)
                    on("tensor", tpw)
                    c_tp = cnt["tensor"]

                    def evw(v, g0=g0, gN=gN, pb=pb, c_tp=c_tp, sb_=ji % 2):
                        v.wait_ge(sems["tensor"], c_tp)
                        c = inc("vector", v.tensor_copy(out=stageb[sb_][:, 128 * g0:128 * (g0 + gN)],
                                                        in_=ptpw[pb][:, 0:128 * gN]))
                        tpfree[pb] = c
                    on("vector", evw)
                c_stage = cnt["vector"]

                if dest is None:
                    off = qkv_off[id(ap)] + 128 * j

                    def wqkv(v, off=off, nets=nets, c_stage=c_stage, sb_=ji % 2):
                        v.wait_ge(sems["vector"], c_stage)
                        v.drain()
                        dst = wqkvT.ap().rearrange("p (et c) -> p et c", c=768)[:, :, off:off + 128]
                        src = stageb[sb_].ap().rearrange("p (et c) -> p et c", c=128)[:, 0:nets, :]
                        c = inc("vector", v.tensor_copy(out=dst, in_=src))
                        stg_vec[sb_] = c
                    on("vector", wqkv)
                else:
                    def wag(sync, dest=dest, j=j, nets=nets, c_stage=c_stage, sb_=ji % 2):
                        sync.wait_ge(sems["vector"], c_stage)
                        dst = dest.ap().rearrange("(et p) o -> p et o", p=128)[:, :, 128 * j:128 * (j + 1)]
                        src = stageb[sb_].ap().rearrange("p (et c) -> p et c", c=128)[:, 0:nets, :]
                        inc("ag", sync.dma_start(out=dst, in_=src), 16)
                        stg_ag[sb_] = cnt["ag"]
                    on("sync", wag)
                ji += 1
        c_wqkv_rdy = cnt["vector"]
        c_ag_writes = cnt["ag"]

        def cc2(g):
            g.wait_ge(sems["ag"], c_ag_writes)
            for (i_, o_) in ((ag_wo_in, ag_wo_out), (ag_up_in, ag_up_out),
                             (ag_gate_in, ag_gate_out), (ag_dn_in, ag_dn_out)):
                inc("cc", g.collective_compute("AllGather", Alu.bypass, replica_groups=rg,
                                               ins=[i_.ap().opt()], outs=[o_.ap().opt()]))
        on("gpsimd", cc2)
        cc_w = cnt["cc"]
        c_p1_tensor = cnt["tensor"]

        es_p1.close()

        qT_sb = es_a.enter_context(nc.sbuf_tensor("qT_sb", [128, NT * 256], BF16))
        kT_sb = es_a.enter_context(nc.sbuf_tensor("kT_sb", [128, NT * 256], BF16))
        v_sb = es_a.enter_context(nc.sbuf_tensor("v_sb", [128, NT * 256], BF16))


        (S1, MEAN, EX2, M2, VAR, SQV, RSTD, MX, MN, T1, T2, AMAX, MAXLN, CC_, SXV,
         INVSX, AQ, BQ) = range(18)

        es_p2 = ExitStack()
        xt0 = es_p2.enter_context(nc.sbuf_tensor("xt0", [128, E], F32))
        xt1 = es_p2.enter_context(nc.sbuf_tensor("xt1", [128, E], F32))
        xt2 = es_p2.enter_context(nc.sbuf_tensor("xt2", [128, E], F32))
        sqdump = es_p2.enter_context(nc.sbuf_tensor("sqdump", [128, E], F32))
        xq8a = es_p2.enter_context(nc.sbuf_tensor("xq8a", [128, E], I8))
        xq8b = es_p2.enter_context(nc.sbuf_tensor("xq8b", [128, E], I8))
        xqbfa = es_p2.enter_context(nc.sbuf_tensor("xqbfa", [128, E], BF16))
        xqbfb = es_p2.enter_context(nc.sbuf_tensor("xqbfb", [128, E], BF16))
        xq8 = [xq8a, xq8b]
        xqbf = [xqbfa, xqbfb]
        xqT = es_p2.enter_context(nc.sbuf_tensor("xqT", [128, E], BF16))
        qkstage0 = es_p2.enter_context(nc.sbuf_tensor("qkstage0", [128, 512], BF16))
        qkstage1 = es_p2.enter_context(nc.sbuf_tensor("qkstage1", [128, 512], BF16))
        qkstage = [qkstage0, qkstage1]
        ptpx0 = es_p2.enter_context(nc.psum_tensor("ptpx0", [128, 1024], BF16))
        ptpx1 = es_p2.enter_context(nc.psum_tensor("ptpx1", [128, 1024], BF16))
        pq1 = es_p2.enter_context(nc.psum_tensor("pq1", [128, 512], F32))
        pq2 = es_p2.enter_context(nc.psum_tensor("pq2", [128, 256], F32))
        ptpqk = es_p2.enter_context(nc.psum_tensor("ptpqk", [128, 512], BF16))
        xts = [xt0, xt1, xt2]
        ptpx = [ptpx0, ptpx1]
        c_tile_done = [0] * NT
        gx = [None] * NT

        c_quant_h = [0] * NT
        c_q8_h = [0] * NT
        c_sq_h = [0] * NT
        c_sqv_h = [0] * NT
        c_tpx2_h = [0] * NT
        c_mmq_h = [0] * NT
        c_evq_h = [0] * NT
        c_tqk_h = [0] * NT
        c_evqk_h = [0] * NT
        c_xqT_h = [0] * NT

        def p2_preA(tt):
            buf = tt % 3
            po = 18 * (tt % 4)

            def ldx(sync, tt=tt, buf=buf):
                if tt >= 3:
                    sync.wait_ge(sems["scalar"], c_quant_h[tt - 3])
                    sync.wait_ge(sems["vector"], c_q8_h[tt - 3])
                else:
                    sync.wait_ge(sems["vector"], c_wqkv_rdy)
                    sync.wait_ge(sems["tensor"], c_p1_tensor)
                gx[tt] = pool.issue([sync.dma_start(out=xts[buf][:],
                                                    in_=x_full[tt * 128:(tt + 1) * 128, :])])
            on("sync", ldx)

            def st_v(v, tt=tt, buf=buf, po=po):
                pool.wait(v, gx[tt])
                v.tensor_reduce(out=sm[:, po + S1:po + S1 + 1], in_=xts[buf][:],
                                axis=AX.X, op=Alu.add)
                v.tensor_reduce(out=sm[:, po + MX:po + MX + 1], in_=xts[buf][:],
                                axis=AX.X, op=Alu.max)
                inc("vector", v.tensor_reduce(out=sm[:, po + MN:po + MN + 1], in_=xts[buf][:],
                                              axis=AX.X, op=Alu.min))
            on("vector", st_v)
            c_sum = cnt["vector"]

            def st_s(s, buf=buf, po=po, c_sum=c_sum):
                s.wait_ge(sems["vector"], c_sum)
                sinc(s.activation(out=sqdump[:], in_=xts[buf][:], func=AF.Square,
                                  accum_out=sm[:, po + EX2:po + EX2 + 1]))
            c_sq_h[tt] = sq(st_s)

        def p2_preB(tt):
            po = 18 * (tt % 4)
            c_sq = c_sq_h[tt]

            def sml_v(v, po=po, c_sq=c_sq):
                v.wait_ge(sems["scalar"], c_sq)
                v.tensor_scalar(out=sm[:, po + MEAN:po + MEAN + 1], in0=sm[:, po + S1:po + S1 + 1],
                                scalar1=1.0 / E, scalar2=None, op0=Alu.mult)
                v.tensor_scalar(out=sm[:, po + EX2:po + EX2 + 1], in0=sm[:, po + EX2:po + EX2 + 1],
                                scalar1=1.0 / E, scalar2=None, op0=Alu.mult)
                v.drain()
                v.tensor_tensor(out=sm[:, po + M2:po + M2 + 1], in0=sm[:, po + MEAN:po + MEAN + 1],
                                in1=sm[:, po + MEAN:po + MEAN + 1], op=Alu.mult)
                v.drain()
                inc("vector", v.tensor_tensor(out=sm[:, po + VAR:po + VAR + 1],
                                              in0=sm[:, po + EX2:po + EX2 + 1],
                                              in1=sm[:, po + M2:po + M2 + 1], op=Alu.subtract))
            on("vector", sml_v)
            c_var = cnt["vector"]

            def sml_s(s, po=po, c_var=c_var):
                s.wait_ge(sems["vector"], c_var)
                s.activation(out=sm[:, po + SQV:po + SQV + 1],
                             in_=sm[:, po + VAR:po + VAR + 1],
                             func=AF.Ln, bias=eps_t[:, 0:1], scale=1.0)
                s.drain()
                sinc(s.activation(out=sm[:, po + RSTD:po + RSTD + 1],
                                  in_=sm[:, po + SQV:po + SQV + 1],
                                  func=AF.Exp, scale=neghalf_t[:, 0:1]))
            c_sqv_h[tt] = sq(sml_s)

        def p2_preC(tt):
            buf = tt % 3
            po = 18 * (tt % 4)
            c_sqv = c_sqv_h[tt]

            def qv(v, buf=buf, po=po, tt=tt, c_sqv=c_sqv):
                v.wait_ge(sems["scalar"], c_sqv)
                if tt >= 2:
                    v.wait_ge(sems["scalar"], c_quant_h[tt - 2])
                P = lambda k: sm[:, po + k:po + k + 1]
                v.tensor_tensor(out=P(T1), in0=P(MX), in1=P(MEAN), op=Alu.subtract)
                v.tensor_tensor(out=P(T2), in0=P(MEAN), in1=P(MN), op=Alu.subtract)
                v.drain()
                v.tensor_tensor(out=P(AMAX), in0=P(T1), in1=P(T2), op=Alu.max)
                v.drain()
                v.tensor_tensor(out=P(MAXLN), in0=P(AMAX), in1=P(RSTD), op=Alu.mult)
                v.drain()
                v.tensor_scalar(out=P(CC_), in0=P(MAXLN), scalar1=1.0e-5, scalar2=None, op0=Alu.max)
                v.drain()
                v.reciprocal(out=P(SXV), in_=P(CC_))
                v.drain()
                v.tensor_scalar(out=P(SXV), in0=P(SXV), scalar1=127.0, scalar2=None, op0=Alu.mult)
                v.tensor_scalar(out=P(INVSX), in0=P(CC_), scalar1=1.0 / 127.0, scalar2=None, op0=Alu.mult)
                v.drain()
                v.tensor_tensor(out=P(AQ), in0=P(RSTD), in1=P(SXV), op=Alu.mult)
                v.drain()
                v.tensor_scalar(out=P(BQ), in0=P(MEAN), scalar1=P(AQ), scalar2=-1.0,
                                op0=Alu.mult, op1=Alu.mult)
                v.drain()
                inc("vector", v.tensor_scalar(out=xq8[tt % 2][:], in0=xts[buf][:],
                                              scalar1=P(AQ), scalar2=P(BQ),
                                              op0=Alu.mult, op1=Alu.add))
            on("vector", qv)
            c_q8 = cnt["vector"]
            c_q8_h[tt] = c_q8

            def qcast(s, tt=tt, c_q8=c_q8):
                s.wait_ge(sems["vector"], c_q8)
                if tt >= 2:
                    s.wait_ge(sems["tensor"], c_tpx2_h[tt - 2])
                sinc(s.copy(out=xqbf[tt % 2][:], in_=xq8[tt % 2][:]))
            c_quant_h[tt] = sq(qcast)

            if DEBUG and tt == 0:
                def dxq(v, c_q8=c_q8):
                    v.wait_ge(sems["vector"], c_q8)
                    v.tensor_copy(out=snap["xq"][:], in_=xq8[0][:])
                    inc("vector", v.tensor_copy(out=snap["sm"][:], in_=sm[:]))
                on("vector", dxq)

        def p2_pe(tt):
            buf = tt % 3
            po = 18 * (tt % 4)
            c_quant = c_quant_h[tt]
            c_xqT = 0
            for g0 in (0, 8):
                pb = g0 // 8

                def tpx(t, g0=g0, pb=pb, tt=tt, buf=buf, c_quant=c_quant):
                    t.wait_ge(sems["scalar"], c_quant)
                    if tt > 0:
                        t.wait_ge(sems["scalar"], c_xqT_h[tt - 1])
                    pool.wait(t, g_const)
                    for k in range(8):
                        ins = t.transpose(out=ptpx[pb][:, 128 * k:128 * (k + 1)],
                                          in_=xqbf[tt % 2][:, 128 * (g0 + k):128 * (g0 + k + 1)],
                                          identity=ident[:])
                        if k == 7:
                            inc("tensor", ins)
                on("tensor", tpx)
                c_tpx = cnt["tensor"]
                c_tpx2_h[tt] = c_tpx

                def evx(s, g0=g0, pb=pb, c_tpx=c_tpx, tt=tt):
                    s.wait_ge(sems["tensor"], c_tpx)
                    if tt > 0:
                        s.wait_ge(sems["tensor"], c_mmq_h[tt - 1])
                    sinc(s.copy(out=xqT[:, 128 * g0:128 * (g0 + 8)], in_=ptpx[pb][:]))
                c_xqT = sq(evx)
                c_xqT_h[tt] = c_xqT

            def mmq(t, c_xqT=c_xqT, tt=tt):
                t.wait_ge(sems["scalar"], c_xqT)
                t.wait_ge(sems["vector"], c_wqkv_rdy)
                if tt > 0:
                    t.wait_ge(sems["vector"], c_evq_h[tt - 1])
                for et in range(ET):
                    t.matmul(out=pq1[:], lhsT=xqT[:, 128 * et:128 * (et + 1)],
                             rhs=wqkvT[:, 768 * et:768 * et + 512],
                             start=(et == 0), stop=(et == ET - 1))
                for et in range(ET):
                    ins = t.matmul(out=pq2[:], lhsT=xqT[:, 128 * et:128 * (et + 1)],
                                   rhs=wqkvT[:, 768 * et + 512:768 * (et + 1)],
                                   start=(et == 0), stop=(et == ET - 1))
                    if et == ET - 1:
                        inc("tensor", ins)
            on("tensor", mmq)
            c_mmq_h[tt] = cnt["tensor"]
            c_mmq = cnt["tensor"]

            def evq(v, tt=tt, po=po, c_mmq=c_mmq, buf=tt % 2):
                v.wait_ge(sems["tensor"], c_mmq)
                if tt > 1:
                    v.wait_ge(sems["tensor"], c_tqk_h[tt - 2])
                P = lambda k: sm[:, po + k:po + k + 1]
                v.tensor_scalar(out=qkstage[buf][:, 0:256], in0=pq1[:, 0:256],
                                scalar1=P(INVSX), scalar2=scales_sb[:, SC_Q:SC_Q + 1],
                                op0=Alu.mult, op1=Alu.mult)
                v.tensor_scalar(out=qkstage[buf][:, 256:512], in0=pq1[:, 256:512],
                                scalar1=P(INVSX), scalar2=scales_sb[:, SC_K:SC_K + 1],
                                op0=Alu.mult, op1=Alu.mult)
                inc("vector", v.tensor_scalar(out=v_sb[:, 256 * tt:256 * (tt + 1)],
                                              in0=pq2[:], scalar1=P(INVSX),
                                              scalar2=scales_sb[:, SC_V:SC_V + 1],
                                              op0=Alu.mult, op1=Alu.mult))
            on("vector", evq)
            c_evq_h[tt] = cnt["vector"]
            c_evq = cnt["vector"]


        def p2_pe_b(tt):
            buf = tt % 2

            def tqk(t, tt=tt, buf=buf):
                t.wait_ge(sems["vector"], c_evq_h[tt])
                if tt > 0:
                    t.wait_ge(sems["scalar"], c_evqk_h[tt - 1])
                for k in range(4):
                    ins = t.transpose(out=ptpqk[:, 128 * k:128 * (k + 1)],
                                      in_=qkstage[buf][:, 128 * k:128 * (k + 1)],
                                      identity=ident[:])
                    if k == 3:
                        inc("tensor", ins)
            on("tensor", tqk)
            c_tqk_h[tt] = cnt["tensor"]
            c_tqk = cnt["tensor"]

            def evqk(s, tt=tt, c_tqk=c_tqk):
                s.wait_ge(sems["tensor"], c_tqk)
                s.copy(out=qT_sb[:, 256 * tt:256 * (tt + 1)], in_=ptpqk[:, 0:256])
                sinc(s.copy(out=kT_sb[:, 256 * tt:256 * (tt + 1)], in_=ptpqk[:, 256:512]))
            c = sq(evqk)
            c_evqk_h[tt] = c
            c_tile_done[tt] = c

        for step in range(NT + 4):
            if step < NT:
                p2_preA(step)
            if 1 <= step and step - 1 < NT:
                p2_preB(step - 1)
            if 2 <= step and step - 2 < NT:
                p2_preC(step - 2)
            if 3 <= step and step - 3 < NT:
                p2_pe(step - 3)
            if 4 <= step:
                p2_pe_b(step - 4)

        sflush()
        es_p2.close()

        es_p3 = ExitStack()
        attnout_sb = es_p3.enter_context(nc.sbuf_tensor("attnout_sb", [128, NT * 256], BF16))
        attnT0 = es_p3.enter_context(nc.sbuf_tensor("attnT0", [128, 512], BF16))
        attnT1 = es_p3.enter_context(nc.sbuf_tensor("attnT1", [128, 512], BF16))
        scm0 = es_p3.enter_context(nc.sbuf_tensor("scm0", [128, 512], F32))
        scm1 = es_p3.enter_context(nc.sbuf_tensor("scm1", [128, 512], F32))
        outT = es_p3.enter_context(nc.sbuf_tensor("outT", [128, 4096], BF16))
        sumsrow = es_p3.enter_context(nc.sbuf_tensor("sumsrow", [1, S], F32))
        recips = es_p3.enter_context(nc.sbuf_tensor("recips", [128, NTB], F32))
        sumsbf = es_p3.enter_context(nc.sbuf_tensor("sumsbf", [1, S], BF16))
        sums32 = es_p3.enter_context(nc.sbuf_tensor("sums32", [32, 128], BF16))
        psc0 = es_p3.enter_context(nc.psum_tensor("psc0", [128, 512], F32))
        psc1 = es_p3.enter_context(nc.psum_tensor("psc1", [128, 512], F32))
        pout = es_p3.enter_context(nc.psum_tensor("pout", [128, 512], F32))
        psum1 = es_p3.enter_context(nc.psum_tensor("psum1", [1, 512], F32))
        ptpo = es_p3.enter_context(nc.psum_tensor("ptpo", [128, 512], BF16))
        ptps = es_p3.enter_context(nc.psum_tensor("ptps", [128, 32], BF16))
        psc = [psc0, psc1]
        c_qkv_all = cnt["vector"]
        g_sums_prev = [None]

        attnTb = [attnT0, attnT1]
        scmb = [scm0, scm1]
        for bh in range(4):
            b, ds = bh // 2, bh % 2
            tiles = [(sc, tl) for sc in range(8) for tl in range(4 * sc + 4)]
            NTI = len(tiles)
            c_exp_h = [0] * NTI
            c_ov_h = [0] * NTI
            psc_cons = [None, None]
            c_ev_sc_map = {}
            c_ev_last = 0

            def emit_mm_sc(idx):
                sc, tl = tiles[idx]
                tg = NTB * b + tl
                pb = idx % 2

                def mm_sc(t, tg=tg, pb=pb, sc=sc, cons=psc_cons[pb]):
                    t.wait_ge(sems["vector"], c_qkv_all)
                    if cons is not None:
                        t.wait_ge(sems[cons[0]], cons[1])
                    qv_ = qT_sb.ap().rearrange("p (t c) -> p t c", c=256)
                    rhs = qv_[:, NTB * b + 4 * sc:NTB * b + 4 * sc + 4,
                              128 * ds:128 * (ds + 1)]
                    inc("tensor", t.matmul(
                        out=psc[pb][:],
                        lhsT=kT_sb[:, 256 * tg + 128 * ds:256 * tg + 128 * (ds + 1)],
                        rhs=rhs, start=True, stop=True))
                on("tensor", mm_sc)
                return cnt["tensor"]

            def emit_exp(idx, c_mmsc):
                sc, tl = tiles[idx]
                pb = idx % 2
                diag = tl >= 4 * sc
                c_at_free = c_ov_h[idx - 2] if idx >= 2 else 0
                if diag:
                    kk = tl - 4 * sc

                    def mskv(v, kk=kk, pb=pb, c_mmsc=c_mmsc):
                        v.wait_ge(sems["tensor"], c_mmsc)
                        if c_exp_h[idx - 2] if idx >= 2 else 0:
                            v.wait_ge(sems["scalar"], c_exp_h[idx - 2])
                        inc("vector", v.tensor_tensor(
                            out=scmb[pb][:], in0=psc[pb][:],
                            in1=cmask[:, 384 - 128 * kk:896 - 128 * kk], op=Alu.add))
                    on("vector", mskv)
                    c_mv = cnt["vector"]
                    psc_cons[pb] = ("vector", c_mv)

                    def expd(s, pb=pb, c_mv=c_mv, c_at_free=c_at_free):
                        s.wait_ge(sems["vector"], c_mv)
                        if c_at_free:
                            s.wait_ge(sems["tensor"], c_at_free)
                        sinc(s.activation(out=attnTb[pb][:], in_=scmb[pb][:], func=AF.Exp))
                    sq(expd)
                else:
                    def expn(s, pb=pb, c_mmsc=c_mmsc, c_at_free=c_at_free):
                        s.wait_ge(sems["tensor"], c_mmsc)
                        if c_at_free:
                            s.wait_ge(sems["tensor"], c_at_free)
                        sinc(s.activation(out=attnTb[pb][:], in_=psc[pb][:], func=AF.Exp))
                    sq(expn)
                    psc_cons[pb] = ("scalar", cnt["scalar"])
                c_exp_h[idx] = cnt["scalar"]

            def emit_mm_ov(idx):
                sc, tl = tiles[idx]
                tg = NTB * b + tl
                pb = idx % 2
                first = tl == 0
                last = tl == 4 * sc + 3
                c_exp = c_exp_h[idx]
                c_prev_ev = c_ev_sc_map.get(sc - 1, 0)

                def mm_ov(t, tg=tg, pb=pb, first=first, last=last,
                          c_exp=c_exp, c_prev_ev=c_prev_ev):
                    t.wait_ge(sems["scalar"], c_exp)
                    if first and c_prev_ev:
                        t.wait_ge(sems["vector"], c_prev_ev)
                    t.matmul(out=pout[:],
                             lhsT=v_sb[:, 256 * tg + 128 * ds:256 * tg + 128 * (ds + 1)],
                             rhs=attnTb[pb][:], start=first, stop=last)
                    inc("tensor", t.matmul(out=psum1[:], lhsT=ones_bf[:], rhs=attnTb[pb][:],
                                           start=first, stop=last))
                on("tensor", mm_ov)
                c_ov_h[idx] = cnt["tensor"]
                if last:
                    def ev_sc(v, sc=sc, c_last_mm=cnt["tensor"]):
                        v.wait_ge(sems["tensor"], c_last_mm)
                        if sc == 0 and g_sums_prev[0] is not None:
                            pool.wait(v, g_sums_prev[0])
                        v.tensor_copy(out=outT[:, 512 * sc:512 * (sc + 1)], in_=pout[:])
                        inc("vector", v.tensor_copy(out=sumsrow[0:1, 512 * sc:512 * (sc + 1)],
                                                    in_=psum1[0:1, :]))
                    on("vector", ev_sc)
                    c_ev_sc_map[sc] = cnt["vector"]

            for idx in range(NTI):
                c_mmsc = emit_mm_sc(idx)
                emit_exp(idx, c_mmsc)
                if idx >= 1:
                    emit_mm_ov(idx - 1)
            emit_mm_ov(NTI - 1)
            sflush()
            c_ev_sc = c_ev_sc_map[7]

            def sums_bf(v, bh=bh):
                v.drain()
                inc("vector", v.tensor_copy(out=sumsbf[0:1, :], in_=sumsrow[0:1, :]))
            on("vector", sums_bf)
            c_sbf = cnt["vector"]

            g_sums_h = [None]

            def sums_mv(sync, c_sbf=c_sbf):
                sync.wait_ge(sems["vector"], c_sbf)
                g1 = pool.issue([sync.dma_start(out=sums_bounce, in_=sumsbf[0:1, :])])
                pool.wait(sync, g1)
                src = sums_bounce.rearrange("x (t p) -> (x t) p", t=32)
                g_sums_h[0] = pool.issue([sync.dma_start(out=sums32[:], in_=src)])
            on("sync", sums_mv)
            g_sums = g_sums_h[0]
            g_sums_prev[0] = g_sums

            def sums_tp(t, g_sums=g_sums):
                pool.wait(t, g_sums)
                inc("tensor", t.transpose(out=ptps[:], in_=sums32[:], identity=ident[0:32, 0:32]))
            on("tensor", sums_tp)
            c_stp = cnt["tensor"]

            def recv(v, c_stp=c_stp, bh=bh):
                v.wait_ge(sems["tensor"], c_stp)
                v.tensor_copy(out=recips[:], in_=ptps[:])
                v.drain()
                inc("vector", v.reciprocal(out=recips[:], in_=recips[:]))
                v.drain()
                if DEBUG and bh == 0:
                    inc("vector", v.tensor_copy(out=snap["rec"][:], in_=recips[:]))
            on("vector", recv)
            c_rec = cnt["vector"]

            c_ev_tp = 0
            for sc in range(8):
                def tpo(t, sc=sc, c_rec=c_rec, c_ev_tp=c_ev_tp):
                    t.wait_ge(sems["vector"], c_rec)
                    if c_ev_tp:
                        t.wait_ge(sems["vector"], c_ev_tp)
                    for j in range(4):
                        ins = t.transpose(out=ptpo[:, 128 * j:128 * (j + 1)],
                                          in_=outT[:, 512 * sc + 128 * j:512 * sc + 128 * (j + 1)],
                                          identity=ident[:])
                        if j == 3:
                            inc("tensor", ins)
                on("tensor", tpo)
                c_tpo = cnt["tensor"]

                def evtp(v, sc=sc, b=b, ds=ds, c_tpo=c_tpo):
                    v.wait_ge(sems["tensor"], c_tpo)
                    for j in range(4):
                        st = 4 * sc + j
                        ins = v.tensor_scalar(
                            out=attnout_sb[:, 256 * (NTB * b + st) + 128 * ds:
                                           256 * (NTB * b + st) + 128 * (ds + 1)],
                            in0=ptpo[:, 128 * j:128 * (j + 1)],
                            scalar1=recips[:, st:st + 1], scalar2=None, op0=Alu.mult)
                        if j == 3:
                            inc("vector", ins)
                on("vector", evtp)
                c_ev_tp = cnt["vector"]

        if DEBUG:
            def dato(v):
                v.wait_ge(sems["vector"], cnt["vector"])
                inc("vector", v.tensor_copy(out=snap["ao"][:], in_=attnout_sb[:, 0:256]))
            on("vector", dato)

        def at_out(sync):
            sync.wait_ge(sems["vector"], cnt["vector"])
            dst = a2a_in.ap().rearrange("(t p) f -> p t f", p=128)
            src_ = attnout_sb.ap().rearrange("p (t f) -> p t f", f=256)
            inc("a2aw", sync.dma_start(out=dst, in_=src_), 16)
        on("sync", at_out)
        c_a2a_in = cnt["a2aw"]

        def cc3(g):
            g.wait_ge(sems["a2aw"], c_a2a_in)
            with allow_ici_all_to_all():
                inc("cc", g.collective_compute("AllToAll", Alu.bypass, replica_groups=rg,
                                               ins=[a2a_in.ap().opt()], outs=[a2a_out.ap().opt()]))
        on("gpsimd", cc3)
        cc_at = cnt["cc"]
        c_p3_v = cnt["vector"]
        c_p3_t = cnt["tensor"]
        c_p3_s = cnt["scalar"]

        es_p3.close()
        es_a.close()

        OS1, OMEAN, OEX2, OM2, OVAR, OSQV, ORSTD, OCC, OSX, OINVSX, OBT = range(20, 31)

        es_p4 = ExitStack()
        woT_sb = es_p4.enter_context(nc.sbuf_tensor("woT_sb", [128, ET * 2048], BF16))
        oin0 = es_p4.enter_context(nc.sbuf_tensor("oin0", [128, E], BF16))
        oin1 = es_p4.enter_context(nc.sbuf_tensor("oin1", [128, E], BF16))
        oinb = [oin0, oin1]
        xres0 = es_p4.enter_context(nc.sbuf_tensor("xres0", [128, E], F32))
        xres1 = es_p4.enter_context(nc.sbuf_tensor("xres1", [128, E], F32))
        xresb = [xres0, xres1]
        oq8 = es_p4.enter_context(nc.sbuf_tensor("oq8", [128, E], I8))
        oqbf = es_p4.enter_context(nc.sbuf_tensor("oqbf", [128, E], BF16))
        oqT = es_p4.enter_context(nc.sbuf_tensor("oqT", [128, E], BF16))
        oproj = es_p4.enter_context(nc.sbuf_tensor("oproj", [128, E], F32))
        osq = es_p4.enter_context(nc.sbuf_tensor("osq", [128, E], F32))
        lnres = es_p4.enter_context(nc.sbuf_tensor("lnres", [128, E], F32))
        ptpo2a = es_p4.enter_context(nc.psum_tensor("ptpo2a", [128, 1024], BF16))
        ptpo2b = es_p4.enter_context(nc.psum_tensor("ptpo2b", [128, 1024], BF16))
        po0 = es_p4.enter_context(nc.psum_tensor("po0", [128, 512], F32))
        po1 = es_p4.enter_context(nc.psum_tensor("po1", [128, 512], F32))
        pob = [po0, po1]
        ptpo2 = [ptpo2a, ptpo2b]

        g_wo_h = [None]

        def ldwo(sync):
            sync.wait_ge(sems["cc"], cc_w)
            sync.wait_ge(sems["vector"], c_p3_v)
            sync.wait_ge(sems["tensor"], c_p3_t)
            sync.wait_ge(sems["scalar"], c_p3_s)
            ins = []
            for r in range(NC):
                src = ag_wo_out.ap()[E * r:E * (r + 1), :].rearrange("(et p) o -> p et o", p=128)
                dst = woT_sb.ap().rearrange("p (et c) -> p et c", c=2048)[:, :, 256 * r:256 * (r + 1)]
                ins.append(sync.dma_start(out=dst, in_=src))
            g_wo_h[0] = pool.issue(ins)
        on("sync", ldwo)
        g_wo = g_wo_h[0]

        c_oq_h = [0, 0]
        c_x1_h = [0, 0]
        for j in range(TPC // 128):
            ob = j % 2
            g_oin_h = [None]

            def ldoin(sync, j=j, ob=ob):
                sync.wait_ge(sems["cc"], cc_at)
                if c_oq_h[ob]:
                    sync.wait_ge(sems["vector"], c_oq_h[ob])
                if c_x1_h[ob]:
                    sync.wait_ge(sems["vector"], c_x1_h[ob])
                ins = []
                for r in range(NC):
                    ins.append(sync.dma_start(out=oinb[ob][:, 256 * r:256 * (r + 1)],
                                              in_=a2a_out.ap()[TPC * r + 128 * j:
                                                               TPC * r + 128 * (j + 1), :]))
                ins.append(sync.dma_start(out=xresb[ob][:], in_=x_mine[128 * j:128 * (j + 1), :]))
                g_oin_h[0] = pool.issue(ins)
            on("sync", ldoin)
            g_oin = g_oin_h[0]
            oin = oinb[ob]

            def oquant(v, g_oin=g_oin, j=j, oin=oin):
                pool.wait(v, g_oin)
                if DEBUG and j == 0:
                    v.tensor_copy(out=snap["oin"][:], in_=oin[:, 0:512])
                P = lambda k: sm[:, k:k + 1]
                v.tensor_reduce(out=P(OCC), in_=oin[:], axis=AX.X, op=Alu.max,
                                apply_absolute_value=True)
                v.drain()
                v.tensor_scalar(out=P(OCC), in0=P(OCC), scalar1=1.0e-5, scalar2=None, op0=Alu.max)
                v.drain()
                v.reciprocal(out=P(OSX), in_=P(OCC))
                v.drain()
                v.tensor_scalar(out=P(OSX), in0=P(OSX), scalar1=127.0, scalar2=None, op0=Alu.mult)
                v.tensor_scalar(out=P(OINVSX), in0=P(OCC), scalar1=1.0 / 127.0,
                                scalar2=None, op0=Alu.mult)
                v.drain()
                v.tensor_scalar(out=oq8[:], in0=oin[:], scalar1=P(OSX), scalar2=None, op0=Alu.mult)
                v.drain()
                inc("vector", v.tensor_copy(out=oqbf[:], in_=oq8[:]))
            on("vector", oquant)
            c_oq = cnt["vector"]
            c_oq_h[ob] = c_oq

            for g0 in (0, 8):
                pb = g0 // 8

                def tpoq(t, g0=g0, pb=pb, c_oq=c_oq):
                    t.wait_ge(sems["vector"], c_oq)
                    t.wait_ge(sems["vector"], cnt["vector"])
                    pool.wait(t, g_const)
                    for k in range(8):
                        ins = t.transpose(out=ptpo2[pb][:, 128 * k:128 * (k + 1)],
                                          in_=oqbf[:, 128 * (g0 + k):128 * (g0 + k + 1)],
                                          identity=ident[:])
                        if k == 7:
                            inc("tensor", ins)
                on("tensor", tpoq)
                c_t = cnt["tensor"]

                def evoq(v, g0=g0, pb=pb, c_t=c_t):
                    v.wait_ge(sems["tensor"], c_t)
                    inc("vector", v.tensor_copy(out=oqT[:, 128 * g0:128 * (g0 + 8)],
                                                in_=ptpo2[pb][:]))
                on("vector", evoq)
            c_oqT = cnt["vector"]

            c_evo = [0, 0]
            for ch in range(4):
                def mm_o(t, ch=ch, c_oqT=c_oqT, c_free=c_evo[ch % 2]):
                    t.wait_ge(sems["vector"], c_oqT)
                    pool.wait(t, g_wo)
                    if c_free:
                        t.wait_ge(sems["vector"], c_free)
                    for et in range(ET):
                        ins = t.matmul(out=pob[ch % 2][:], lhsT=oqT[:, 128 * et:128 * (et + 1)],
                                       rhs=woT_sb[:, 2048 * et + 512 * ch:2048 * et + 512 * (ch + 1)],
                                       start=(et == 0), stop=(et == ET - 1))
                        if et == ET - 1:
                            inc("tensor", ins)
                on("tensor", mm_o)
                c_mo = cnt["tensor"]

                def ev_o(v, ch=ch, c_mo=c_mo):
                    v.wait_ge(sems["tensor"], c_mo)
                    inc("vector", v.tensor_scalar(out=oproj[:, 512 * ch:512 * (ch + 1)],
                                                  in0=pob[ch % 2][:], scalar1=sm[:, OINVSX:OINVSX + 1],
                                                  scalar2=scales_sb[:, SC_O:SC_O + 1],
                                                  op0=Alu.mult, op1=Alu.mult))
                on("vector", ev_o)
                c_evo[ch % 2] = cnt["vector"]
            c_op = cnt["vector"]

            def oln_v(v, c_op=c_op):
                v.wait_ge(sems["vector"], c_op)
                inc("vector", v.tensor_reduce(out=sm[:, OS1:OS1 + 1], in_=oproj[:],
                                              axis=AX.X, op=Alu.add))
            on("vector", oln_v)
            c_os1 = cnt["vector"]

            def oln_s(s, c_os1=c_os1):
                s.wait_ge(sems["vector"], c_os1)
                sinc(s.activation(out=osq[:], in_=oproj[:], func=AF.Square,
                                  accum_out=sm[:, OEX2:OEX2 + 1]))
            c_osq = sq(oln_s)

            def oln_v2(v, c_osq=c_osq):
                v.wait_ge(sems["scalar"], c_osq)
                P = lambda k: sm[:, k:k + 1]
                v.tensor_scalar(out=P(OMEAN), in0=P(OS1), scalar1=1.0 / E, scalar2=None, op0=Alu.mult)
                v.tensor_scalar(out=P(OEX2), in0=P(OEX2), scalar1=1.0 / E, scalar2=None, op0=Alu.mult)
                v.drain()
                v.tensor_tensor(out=P(OM2), in0=P(OMEAN), in1=P(OMEAN), op=Alu.mult)
                v.drain()
                inc("vector", v.tensor_tensor(out=P(OVAR), in0=P(OEX2), in1=P(OM2), op=Alu.subtract))
            on("vector", oln_v2)
            c_ov = cnt["vector"]

            def oln_s2(s, c_ov=c_ov):
                s.wait_ge(sems["vector"], c_ov)
                s.activation(out=sm[:, OSQV:OSQV + 1], in_=sm[:, OVAR:OVAR + 1],
                             func=AF.Ln, bias=eps_t[:, 0:1], scale=1.0)
                s.drain()
                sinc(s.activation(out=sm[:, ORSTD:ORSTD + 1],
                                  in_=sm[:, OSQV:OSQV + 1],
                                  func=AF.Exp, scale=neghalf_t[:, 0:1]))
            c_osqv = sq(oln_s2)

            def oln_v3(v, j=j, c_osqv=c_osqv, ob=ob):
                s_ = sems
                v.wait_ge(s_["scalar"], c_osqv)
                P = lambda k: sm[:, k:k + 1]
                v.tensor_scalar(out=P(OBT), in0=P(OMEAN), scalar1=P(ORSTD),
                                scalar2=-1.0, op0=Alu.mult, op1=Alu.mult)
                v.drain()
                v.tensor_scalar(out=lnres[:], in0=oproj[:],
                                scalar1=P(ORSTD), scalar2=P(OBT),
                                op0=Alu.mult, op1=Alu.add)
                v.drain()
                inc("vector", v.tensor_tensor(out=lnres[:], in0=lnres[:], in1=xresb[ob][:], op=Alu.add))
                if DEBUG and j == 0:
                    inc("vector", v.tensor_copy(out=snap["x1"][:], in_=lnres[:, 0:1024]))
            on("vector", oln_v3)
            c_x1 = cnt["vector"]

            c_x1_h[ob] = c_x1

            def stx1(sync, j=j, c_x1=c_x1):
                sync.wait_ge(sems["vector"], c_x1)
                inc("x1w", sync.dma_start(out=x1_hbm[128 * j:128 * (j + 1), :], in_=lnres[:]), 16)
            on("sync", stx1)
        sflush()
        c_x1_all = cnt["x1w"]
        c_p4_v = cnt["vector"]
        c_p4_t = cnt["tensor"]
        c_p4_s = cnt["scalar"]
        es_p4.close()

        FS1, FMEAN, FEX2, FM2, FVAR, FSQV, FRSTD, FBT = range(31, 39)

        es_p5 = ExitStack()
        x1t = es_p5.enter_context(nc.sbuf_tensor("x1t", [128, E], F32))
        fsq = es_p5.enter_context(nc.sbuf_tensor("fsq", [128, E], F32))
        hin = es_p5.enter_context(nc.sbuf_tensor("hin", [128, E], BF16))
        hinT = es_p5.enter_context(nc.sbuf_tensor("hinT", [128, ET * 512], BF16))
        wug0 = es_p5.enter_context(nc.sbuf_tensor("wug0", [128, 2 * 2048], BF16))
        wug1 = es_p5.enter_context(nc.sbuf_tensor("wug1", [128, 2 * 2048], BF16))
        silu0 = es_p5.enter_context(nc.sbuf_tensor("silu0", [128, 512], F32))
        silu1 = es_p5.enter_context(nc.sbuf_tensor("silu1", [128, 512], F32))
        silub = [silu0, silu1]
        hT = es_p5.enter_context(nc.sbuf_tensor("hT", [128, 32 * 512], BF16))
        wdn0 = es_p5.enter_context(nc.sbuf_tensor("wdn0", [128, 32 * 512], BF16))
        wdn1 = es_p5.enter_context(nc.sbuf_tensor("wdn1", [128, 32 * 512], BF16))
        y0 = es_p5.enter_context(nc.sbuf_tensor("y0", [128, E], F32))
        y1 = es_p5.enter_context(nc.sbuf_tensor("y1", [128, E], F32))
        y2 = es_p5.enter_context(nc.sbuf_tensor("y2", [128, E], F32))
        y3 = es_p5.enter_context(nc.sbuf_tensor("y3", [128, E], F32))
        ptpf = es_p5.enter_context(nc.psum_tensor("ptpf", [128, 1024], BF16))
        pupa = es_p5.enter_context(nc.psum_tensor("pupa", [128, 512], F32))
        pupb = es_p5.enter_context(nc.psum_tensor("pupb", [128, 512], F32))
        pgta = es_p5.enter_context(nc.psum_tensor("pgta", [128, 512], F32))
        pgtb = es_p5.enter_context(nc.psum_tensor("pgtb", [128, 512], F32))
        pup = [pupa, pupb]
        pgt = [pgta, pgtb]
        pd0 = es_p5.enter_context(nc.psum_tensor("pd0", [128, 512], F32))
        pd1 = es_p5.enter_context(nc.psum_tensor("pd1", [128, 512], F32))
        wug = [wug0, wug1]
        wdn = [wdn0, wdn1]
        ysb = [y0, y1, y2, y3]
        pd = [pd0, pd1]

        for grp in range(2):
            c_hinT = 0
            for j2 in range(4):
                j = 4 * grp + j2
                g_x1_h = [None]

                def ldx1(sync, j=j, j2=j2, grp=grp):
                    sync.wait_ge(sems["x1w"], c_x1_all)
                    if grp == 0 and j2 == 0:
                        sync.wait_ge(sems["vector"], c_p4_v)
                        sync.wait_ge(sems["tensor"], c_p4_t)
                        sync.wait_ge(sems["scalar"], c_p4_s)
                    else:
                        sync.wait_ge(sems["vector"], cnt["vector"])
                    g_x1_h[0] = pool.issue([sync.dma_start(out=x1t[:],
                                                           in_=x1_hbm[128 * j:128 * (j + 1), :])])
                on("sync", ldx1)
                g_x1 = g_x1_h[0]

                def fst_v(v, g_x1=g_x1):
                    pool.wait(v, g_x1)
                    inc("vector", v.tensor_reduce(out=sm[:, FS1:FS1 + 1], in_=x1t[:],
                                                  axis=AX.X, op=Alu.add))
                on("vector", fst_v)
                c_fs1 = cnt["vector"]

                def fst_s(s, c_fs1=c_fs1):
                    s.wait_ge(sems["vector"], c_fs1)
                    sinc(s.activation(out=fsq[:], in_=x1t[:], func=AF.Square,
                                      accum_out=sm[:, FEX2:FEX2 + 1]))
                c_fsq = sq(fst_s)

                def fsm(v, c_fsq=c_fsq):
                    v.wait_ge(sems["scalar"], c_fsq)
                    P = lambda k: sm[:, k:k + 1]
                    v.tensor_scalar(out=P(FMEAN), in0=P(FS1), scalar1=1.0 / E, scalar2=None, op0=Alu.mult)
                    v.tensor_scalar(out=P(FEX2), in0=P(FEX2), scalar1=1.0 / E, scalar2=None, op0=Alu.mult)
                    v.drain()
                    v.tensor_tensor(out=P(FM2), in0=P(FMEAN), in1=P(FMEAN), op=Alu.mult)
                    v.drain()
                    inc("vector", v.tensor_tensor(out=P(FVAR), in0=P(FEX2), in1=P(FM2), op=Alu.subtract))
                on("vector", fsm)
                c_fv = cnt["vector"]

                def fsm_s(s, c_fv=c_fv):
                    s.wait_ge(sems["vector"], c_fv)
                    s.activation(out=sm[:, FSQV:FSQV + 1], in_=sm[:, FVAR:FVAR + 1],
                                 func=AF.Ln, bias=eps_t[:, 0:1], scale=1.0)
                    s.drain()
                    sinc(s.activation(out=sm[:, FRSTD:FRSTD + 1],
                                      in_=sm[:, FSQV:FSQV + 1],
                                      func=AF.Exp, scale=neghalf_t[:, 0:1]))
                c_fsqv = sq(fsm_s)

                def fnorm(v, c_fsqv=c_fsqv):
                    v.wait_ge(sems["scalar"], c_fsqv)
                    P = lambda k: sm[:, k:k + 1]
                    v.tensor_scalar(out=P(FBT), in0=P(FMEAN), scalar1=P(FRSTD),
                                    scalar2=-1.0, op0=Alu.mult, op1=Alu.mult)
                    v.drain()
                    inc("vector", v.tensor_scalar(out=hin[:], in0=x1t[:],
                                                  scalar1=P(FRSTD), scalar2=P(FBT),
                                                  op0=Alu.mult, op1=Alu.add))
                on("vector", fnorm)
                c_hin = cnt["vector"]

                for g0 in (0, 8):
                    def tph(t, g0=g0, c_hin=c_hin):
                        t.wait_ge(sems["vector"], c_hin)
                        t.wait_ge(sems["vector"], cnt["vector"])
                        pool.wait(t, g_const)
                        for k in range(8):
                            ins = t.transpose(out=ptpf[:, 128 * k:128 * (k + 1)],
                                              in_=hin[:, 128 * (g0 + k):128 * (g0 + k + 1)],
                                              identity=ident[:])
                            if k == 7:
                                inc("tensor", ins)
                    on("tensor", tph)
                    c_tph = cnt["tensor"]

                    def evh(v, g0=g0, j2=j2, c_tph=c_tph):
                        v.wait_ge(sems["tensor"], c_tph)
                        dst = hinT.ap().rearrange("p (et c) -> p et c", c=512)[:, g0:g0 + 8,
                                                                               128 * j2:128 * (j2 + 1)]
                        src = ptpf.ap().rearrange("p (et c) -> p et c", c=128)
                        inc("vector", v.tensor_copy(out=dst, in_=src))
                    on("vector", evh)
                c_hinT = cnt["vector"]

            for xh in range(2):
                c_silu_done = [cnt["scalar"], cnt["scalar"]]
                c_doh_done = [cnt["vector"], cnt["vector"]]
                c_mug_h = [cnt["tensor"], cnt["tensor"]]
                for ebl in range(32):
                    eb = 32 * xh + ebl
                    wb = eb % 2
                    r, ebr = eb // 8, eb % 8
                    g_ug_h = [None]

                    def ldug(sync, r=r, ebr=ebr, wb=wb, eb=eb, ebl=ebl):
                        sync.wait_ge(sems["cc"], cc_w)
                        if c_mug_h[ebl % 2]:
                            sync.wait_ge(sems["tensor"], c_mug_h[ebl % 2])
                        ins = []
                        for idx, agt in enumerate((ag_up_out, ag_gate_out)):
                            src = agt.ap()[E * r:E * (r + 1), 128 * ebr:128 * (ebr + 1)] \
                                .rearrange("(et p) o -> p et o", p=128)
                            dst = wug[wb].ap().rearrange("p (et c) -> p et c", c=128) \
                                [:, 16 * idx:16 * idx + 16, :]
                            ins.append(sync.dma_start(out=dst, in_=src))
                        g_ug_h[0] = pool.issue(ins)
                    on("sync", ldug)
                    g_ug = g_ug_h[0]

                    ps = ebl % 2
                    def mm_ug(t, wb=wb, g_ug=g_ug, c_hinT=c_hinT, eb=eb, ps=ps,
                              c_sfree=c_silu_done[ebl % 2], c_vfree=c_doh_done[ebl % 2]):
                        pool.wait(t, g_ug)
                        t.wait_ge(sems["vector"], c_hinT)
                        if c_sfree:
                            t.wait_ge(sems["scalar"], c_sfree)
                        if c_vfree:
                            t.wait_ge(sems["vector"], c_vfree)
                        for et in range(ET):
                            t.matmul(out=pup[ps][:], lhsT=wug[wb][:, 128 * et:128 * (et + 1)],
                                     rhs=hinT[:, 512 * et:512 * (et + 1)],
                                     start=(et == 0), stop=(et == ET - 1))
                        for et in range(ET):
                            ins = t.matmul(out=pgt[ps][:], lhsT=wug[wb][:, 2048 + 128 * et:2048 + 128 * (et + 1)],
                                           rhs=hinT[:, 512 * et:512 * (et + 1)],
                                           start=(et == 0), stop=(et == ET - 1))
                            if et == ET - 1:
                                inc("tensor", ins)
                    on("tensor", mm_ug)
                    c_mug = cnt["tensor"]
                    c_mug_h[ebl % 2] = c_mug

                    def dosilu(s, c_mug=c_mug, ps=ps, sb_=ebl % 2):
                        s.wait_ge(sems["tensor"], c_mug)
                        sinc(s.activation(out=silub[sb_][:], in_=pup[ps][:], func=AF.Silu,
                                          scale=scales_sb[:, SC_UP:SC_UP + 1]))
                    c_sil = sq(dosilu)
                    c_silu_done[ebl % 2] = c_sil

                    def doh(v, ebl=ebl, c_sil=c_sil, ps=ps, sb_=ebl % 2, grp=grp, eb=eb):
                        v.wait_ge(sems["scalar"], c_sil)
                        inc("vector", v.scalar_tensor_tensor(
                            out=hT[:, 512 * ebl:512 * (ebl + 1)], in0=pgt[ps][:],
                            scalar=scales_sb[:, SC_GATE:SC_GATE + 1], in1=silub[sb_][:],
                            op0=Alu.mult, op1=Alu.mult))
                        if DEBUG and grp == 0 and eb == 0:
                            v.drain()
                            inc("vector", v.tensor_copy(out=snap["h"][:], in_=hT[:, 0:512]))
                    on("vector", doh)
                    c_doh_done[ebl % 2] = cnt["vector"]
                c_h_ready = cnt["vector"]

                c_evd_done = [cnt["vector"], cnt["vector"]]
                c_md_h = [cnt["tensor"], cnt["tensor"]]
                for ch in range(4):
                    wb = ch % 2
                    g_dn_h = [None]

                    def lddn(sync, ch=ch, xh=xh, wb=wb):
                        sync.wait_ge(sems["cc"], cc_w)
                        if c_md_h[ch % 2]:
                            sync.wait_ge(sems["tensor"], c_md_h[ch % 2])
                        ins = []
                        for half in range(2):
                            rr = 2 * ch + half
                            src = ag_dn_out.ap()[DE * rr + 4096 * xh:DE * rr + 4096 * (xh + 1), :] \
                                .rearrange("(xt p) d -> p xt d", p=128)
                            dst = wdn[wb].ap().rearrange("p (xt c) -> p xt c", c=512) \
                                [:, :, 256 * half:256 * (half + 1)]
                            ins.append(sync.dma_start(out=dst, in_=src))
                        g_dn_h[0] = pool.issue(ins)
                    on("sync", lddn)
                    g_dn = g_dn_h[0]

                    for j2 in range(4):
                        pb = j2 % 2

                        def mm_d(t, j2=j2, pb=pb, wb=wb, g_dn=g_dn, c_h_ready=c_h_ready,
                                 c_free=c_evd_done[pb]):
                            pool.wait(t, g_dn)
                            t.wait_ge(sems["vector"], c_h_ready)
                            if c_free:
                                t.wait_ge(sems["vector"], c_free)
                            for xt in range(32):
                                ins = t.matmul(out=pd[pb][:],
                                               lhsT=hT[:, 512 * xt + 128 * j2:512 * xt + 128 * (j2 + 1)],
                                               rhs=wdn[wb][:, 512 * xt:512 * (xt + 1)],
                                               start=(xt == 0), stop=(xt == 31))
                                if xt == 31:
                                    inc("tensor", ins)
                        on("tensor", mm_d)
                        c_md = cnt["tensor"]
                        c_md_h[ch % 2] = c_md

                        def ev_d(v, j2=j2, pb=pb, ch=ch, xh=xh, c_md=c_md):
                            v.wait_ge(sems["tensor"], c_md)
                            if xh == 0:
                                ins = v.tensor_scalar(out=ysb[j2][:, 512 * ch:512 * (ch + 1)],
                                                      in0=pd[pb][:],
                                                      scalar1=scales_sb[:, SC_DOWN:SC_DOWN + 1],
                                                      scalar2=None, op0=Alu.mult)
                            else:
                                ins = v.scalar_tensor_tensor(out=ysb[j2][:, 512 * ch:512 * (ch + 1)],
                                                             in0=pd[pb][:],
                                                             scalar=scales_sb[:, SC_DOWN:SC_DOWN + 1],
                                                             in1=ysb[j2][:, 512 * ch:512 * (ch + 1)],
                                                             op0=Alu.mult, op1=Alu.add)
                            inc("vector", ins)
                        on("vector", ev_d)
                        c_evd_done[pb] = cnt["vector"]
            c_y = cnt["vector"]

            for j2 in range(4):
                j = 4 * grp + j2
                yt = ysb[j2]

                def yln_v(v, yt=yt, c_y=c_y):
                    v.wait_ge(sems["vector"], c_y)
                    inc("vector", v.tensor_reduce(out=sm[:, FS1:FS1 + 1], in_=yt[:],
                                                  axis=AX.X, op=Alu.add))
                on("vector", yln_v)
                c_ys1 = cnt["vector"]

                def yln_s(s, yt=yt, c_ys1=c_ys1):
                    s.wait_ge(sems["vector"], c_ys1)
                    sinc(s.activation(out=fsq[:], in_=yt[:], func=AF.Square,
                                      accum_out=sm[:, FEX2:FEX2 + 1]))
                c_ysq = sq(yln_s)

                def yln_v2(v, c_ysq=c_ysq):
                    v.wait_ge(sems["scalar"], c_ysq)
                    P = lambda k: sm[:, k:k + 1]
                    v.tensor_scalar(out=P(FMEAN), in0=P(FS1), scalar1=1.0 / E, scalar2=None, op0=Alu.mult)
                    v.tensor_scalar(out=P(FEX2), in0=P(FEX2), scalar1=1.0 / E, scalar2=None, op0=Alu.mult)
                    v.drain()
                    v.tensor_tensor(out=P(FM2), in0=P(FMEAN), in1=P(FMEAN), op=Alu.mult)
                    v.drain()
                    inc("vector", v.tensor_tensor(out=P(FVAR), in0=P(FEX2), in1=P(FM2), op=Alu.subtract))
                on("vector", yln_v2)
                c_yv = cnt["vector"]

                def yln_s2(s, c_yv=c_yv):
                    s.wait_ge(sems["vector"], c_yv)
                    s.activation(out=sm[:, FSQV:FSQV + 1], in_=sm[:, FVAR:FVAR + 1],
                                 func=AF.Ln, bias=eps_t[:, 0:1], scale=1.0)
                    s.drain()
                    sinc(s.activation(out=sm[:, FRSTD:FRSTD + 1],
                                      in_=sm[:, FSQV:FSQV + 1],
                                      func=AF.Exp, scale=neghalf_t[:, 0:1]))
                c_ysqv = sq(yln_s2)

                def ynorm(v, yt=yt, c_ysqv=c_ysqv):
                    v.wait_ge(sems["scalar"], c_ysqv)
                    P = lambda k: sm[:, k:k + 1]
                    v.tensor_scalar(out=P(FBT), in0=P(FMEAN), scalar1=P(FRSTD),
                                    scalar2=-1.0, op0=Alu.mult, op1=Alu.mult)
                    v.drain()
                    inc("vector", v.tensor_scalar(out=fsq[:], in0=yt[:],
                                                  scalar1=P(FRSTD), scalar2=P(FBT),
                                                  op0=Alu.mult, op1=Alu.add))
                on("vector", ynorm)
                c_yn = cnt["vector"]

                g_x1b_h = [None]

                def ldx1b(sync, j=j, c_yn=c_yn):
                    sync.wait_ge(sems["vector"], c_yn)
                    g_x1b_h[0] = pool.issue([sync.dma_start(out=x1t[:],
                                                            in_=x1_hbm[128 * j:128 * (j + 1), :])])
                on("sync", ldx1b)
                g_x1b = g_x1b_h[0]

                def yres(v, yt=yt, j=j, g_x1b=g_x1b):
                    pool.wait(v, g_x1b)
                    inc("vector", v.tensor_tensor(out=yt[:], in0=fsq[:], in1=x1t[:], op=Alu.add))
                    if DEBUG and j == 0:
                        inc("vector", v.tensor_copy(out=snap["y"][:], in_=yt[:, 0:1024]))
                on("vector", yres)
                c_out = cnt["vector"]

                def stout(sync, j=j, yt=yt, c_out=c_out):
                    sync.wait_ge(sems["vector"], c_out)
                    inc("out", sync.dma_start(out=out_ext[128 * j:128 * (j + 1), :], in_=yt[:]), 16)
                on("sync", stout)
        sflush()
        es_p5.close()

        if DEBUG:
            def ddump(sync):
                sync.wait_ge(sems["vector"], cnt["vector"])
                for nm_s, nm_d in [("sm", "dbg_sm"), ("xq", "dbg_xq"),
                                   ("ao", "dbg_attnout"),
                                   ("rec", "dbg_sums"), ("oin", "dbg_oin"),
                                   ("x1", "dbg_x1"), ("h", "dbg_h"), ("y", "dbg_y")]:
                    inc("out", sync.dma_start(out=dbg[nm_d], in_=snap[nm_s][:]), 16)
            on("sync", ddump)

        def fin(sync):
            sync.wait_ge(sems["out"], cnt["out"])
        on("sync", fin)

    return nc


_cache = {}


def _host_consts():
    t = np.arange(128)[:, None]
    j = np.arange(896)[None, :]
    cmask = np.where((j - 384) >= t, 0.0, MASKV).astype(np.float32)
    ident = np.eye(128, dtype=ml_dtypes.bfloat16)
    return cmask, ident


def kernel(L0_attn_x, L0_attn_w_q, L0_attn_w_k, L0_attn_w_v, L0_attn_w_o,
           L0_ffn_W_up, L0_ffn_W_gate, L0_ffn_W_down):
    if "nc" not in _cache:
        _cache["nc"] = build_program()
    nc = _cache["nc"]

    x2d = np.ascontiguousarray(np.asarray(L0_attn_x, dtype=np.float32).reshape(TOK, E))
    wq = np.ascontiguousarray(np.asarray(L0_attn_w_q, dtype=np.float32))
    wk = np.ascontiguousarray(np.asarray(L0_attn_w_k, dtype=np.float32))
    wv = np.ascontiguousarray(np.asarray(L0_attn_w_v, dtype=np.float32))
    wo = np.ascontiguousarray(np.asarray(L0_attn_w_o, dtype=np.float32))
    wup = np.ascontiguousarray(np.asarray(L0_ffn_W_up, dtype=np.float32))
    wgate = np.ascontiguousarray(np.asarray(L0_ffn_W_gate, dtype=np.float32))
    wdown = np.ascontiguousarray(np.asarray(L0_ffn_W_down, dtype=np.float32))
    cmask, ident = _host_consts()

    in_maps = []
    for c in range(NC):
        in_maps.append({
            "x_full": x2d,
            "x_mine": np.ascontiguousarray(x2d[TPC * c:TPC * (c + 1)]),
            "wq_s": np.ascontiguousarray(wq[256 * c:256 * (c + 1)]),
            "wk_s": np.ascontiguousarray(wk[256 * c:256 * (c + 1)]),
            "wv_s": np.ascontiguousarray(wv[256 * c:256 * (c + 1)]),
            "wo_s": np.ascontiguousarray(wo[256 * c:256 * (c + 1)]),
            "up_s": np.ascontiguousarray(wup[1024 * c:1024 * (c + 1)]),
            "gate_s": np.ascontiguousarray(wgate[1024 * c:1024 * (c + 1)]),
            "down_s": np.ascontiguousarray(wdown[256 * c:256 * (c + 1)]),
            "cmask_in": cmask,
            "ident_in": ident,
        })
    res = run_bass_kernel_spmd(nc, in_maps, core_ids=list(range(NC)))
    if DEBUG:
        _cache["results"] = res
    out = np.concatenate([np.asarray(res.results[c]["out"]) for c in range(NC)], axis=0)
    return out.reshape(B, S, E).astype(np.float32)


# version 13
# speedup vs baseline: 1.0763x; 1.0763x over previous
import math
from contextlib import ExitStack

import numpy as np
import ml_dtypes

import concourse.bass as bass
import concourse.mybir as mybir
from concourse.alu_op_type import AluOpType
from concourse.bass_utils import run_bass_kernel_spmd
from concourse.aot_env import allow_ici_all_to_all

F32 = mybir.dt.float32
BF16 = mybir.dt.bfloat16
I8 = mybir.dt.int8
AF = mybir.ActivationFunctionType
AX = mybir.AxisListType
Alu = AluOpType

B, S, E, H, D, DE = 2, 4096, 2048, 16, 128, 8192
TOK = B * S
NC = 8
TPC = TOK // NC
NT = TOK // 128
NTB = S // 128
ET = E // 128
XT = DE // 128
MASKV = -1.0e5

RC_Q, RC_K, RC_V, RC_O, RC_UP, RC_GATE, RC_DOWN = range(7)
SC_Q, SC_K, SC_V, SC_O, SC_UP, SC_GATE, SC_DOWN = range(7, 14)

DEBUG = False


class DmaGroup:
    __slots__ = ("sem", "target")

    def __init__(self, sem, target):
        self.sem = sem
        self.target = target


class DmaPool:

    def __init__(self, nc, es, n=24):
        self.sems = [es.enter_context(nc.semaphore(f"dpool{i}")) for i in range(n)]
        self.tot = [0] * n
        self.idx = 0

    def issue(self, instrs):
        i = self.idx % len(self.sems)
        self.idx += 1
        for ins in instrs:
            ins.then_inc(self.sems[i], 16)
            self.tot[i] += 16
        return DmaGroup(self.sems[i], self.tot[i])

    def wait(self, eng, grp):
        eng.wait_ge(grp.sem, grp.target)


def build_program():
    nc = bass.Bass("TRN2", target_bir_lowering=False)

    x_full = nc.dram_tensor("x_full", [TOK, E], F32, kind="ExternalInput").ap()
    x_mine = nc.dram_tensor("x_mine", [TPC, E], F32, kind="ExternalInput").ap()
    wq_s = nc.dram_tensor("wq_s", [256, E], F32, kind="ExternalInput").ap()
    wk_s = nc.dram_tensor("wk_s", [256, E], F32, kind="ExternalInput").ap()
    wv_s = nc.dram_tensor("wv_s", [256, E], F32, kind="ExternalInput").ap()
    wo_s = nc.dram_tensor("wo_s", [256, E], F32, kind="ExternalInput").ap()
    up_s = nc.dram_tensor("up_s", [1024, E], F32, kind="ExternalInput").ap()
    gate_s = nc.dram_tensor("gate_s", [1024, E], F32, kind="ExternalInput").ap()
    down_s = nc.dram_tensor("down_s", [256, DE], F32, kind="ExternalInput").ap()
    cmask_in = nc.dram_tensor("cmask_in", [128, 896], F32, kind="ExternalInput").ap()
    ident_in = nc.dram_tensor("ident_in", [128, 128], BF16, kind="ExternalInput").ap()
    out_ext = nc.dram_tensor("out", [TPC, E], F32, kind="ExternalOutput").ap()

    dbg = {}
    if DEBUG:
        for nm, sh, dt in [("dbg_scales", [1, 16], F32), ("dbg_xq", [128, E], I8),
                           ("dbg_sm", [128, 44], F32), ("dbg_partials", [128, 8], F32),
                           ("dbg_qk", [128, 512], BF16), ("dbg_v", [128, 256], BF16),
                           ("dbg_attnout", [128, 256], BF16), ("dbg_oin", [128, 512], BF16),
                           ("dbg_x1", [128, 1024], BF16), ("dbg_h", [128, 512], BF16),
                           ("dbg_y", [128, 1024], BF16), ("dbg_sums", [128, 32], F32)]:
            dbg[nm] = nc.dram_tensor(nm, sh, dt, kind="ExternalOutput").ap()

    pt_hbm = nc.dram_tensor("pt_hbm", [128, 8], F32).ap()
    ar_in = nc.dram_tensor("ar_in", [1, 8], F32)
    ar_out = nc.dram_tensor("ar_out", [1, 8], F32)
    sc_hbm = nc.dram_tensor("sc_hbm", [1, 16], F32).ap()
    ag_wo_in = nc.dram_tensor("ag_wo_in", [E, 256], BF16)
    ag_wo_out = nc.dram_tensor("ag_wo_out", [NC * E, 256], BF16)
    ag_up_in = nc.dram_tensor("ag_up_in", [E, 1024], BF16)
    ag_up_out = nc.dram_tensor("ag_up_out", [NC * E, 1024], BF16)
    ag_gate_in = nc.dram_tensor("ag_gate_in", [E, 1024], BF16)
    ag_gate_out = nc.dram_tensor("ag_gate_out", [NC * E, 1024], BF16)
    ag_dn_in = nc.dram_tensor("ag_dn_in", [DE, 256], BF16)
    ag_dn_out = nc.dram_tensor("ag_dn_out", [NC * DE, 256], BF16)
    a2a_in = nc.dram_tensor("a2a_in", [TOK, 256], BF16)
    a2a_out = nc.dram_tensor("a2a_out", [TOK, 256], BF16)
    sums_bounce = nc.dram_tensor("sums_bounce", [1, S], BF16).ap()
    x1_hbm = nc.dram_tensor("x1_hbm", [TPC, E], F32).ap()

    rg = [list(range(NC))]

    es = ExitStack()
    with es:
        block = es.enter_context(nc.Block())
        sems = {n: es.enter_context(nc.semaphore("sem_" + n))
                for n in ["vector", "scalar", "tensor", "cc",
                          "ag", "out", "x1w", "a2aw"]}
        cnt = {n: 0 for n in sems}
        pool = DmaPool(nc, es)

        def inc(n, ins, k=1):
            ins.then_inc(sems[n], k)
            cnt[n] += k
            return cnt[n]

        def on(name, f):
            getattr(block, name)(f)

        squeue = []

        def sinc(ins, k=1):
            ins.then_inc(sems["scalar"], k)
            return ins

        def sq(fn, incs=1):
            squeue.append(fn)
            cnt["scalar"] += incs
            return cnt["scalar"]

        def sflush():
            if not squeue:
                return
            fns = list(squeue)
            squeue.clear()

            def run(s):
                for f in fns:
                    f(s)
            on("scalar", run)

        SB = lambda name, shape, dt: es.enter_context(nc.sbuf_tensor(name, shape, dt))

        scales_sb = SB("scales_sb", [128, 16], F32)
        eps_t = SB("eps_t", [128, 1], F32)
        neghalf_t = SB("neghalf_t", [128, 1], F32)
        ones_bf = SB("ones_bf", [128, 1], BF16)
        ident = SB("ident", [128, 128], BF16)
        cmask = SB("cmask", [128, 896], F32)
        sm = SB("smalls", [128, 44], F32)

        if DEBUG:
            snap = {
                "sm": SB("sn_sm", [128, 44], F32), "xq": SB("sn_xq", [128, E], I8),
                "ao": SB("sn_ao", [128, 256], BF16),
                "rec": SB("sn_rec", [128, NTB], F32), "oin": SB("sn_oin", [128, 512], BF16),
                "x1": SB("sn_x1", [128, 1024], BF16), "h": SB("sn_h", [128, 512], BF16),
                "y": SB("sn_y", [128, 1024], BF16),
            }

        es_a = ExitStack()
        wqkvT = es_a.enter_context(nc.sbuf_tensor("wqkvT", [128, ET * 768], BF16))

        g_const_h = [None]

        def p0s(sync):
            i1 = sync.dma_start(out=ident[:], in_=ident_in)
            i2 = sync.dma_start(out=cmask[:], in_=cmask_in)
            g_const_h[0] = pool.issue([i1, i2])
        on("sync", p0s)
        g_const = g_const_h[0]

        def p0v(v):
            v.memset(eps_t[:], 1.0e-5)
            v.memset(neghalf_t[:], -0.5)
            inc("vector", v.memset(ones_bf[:], 1.0))
        on("vector", p0v)

        es_p1 = ExitStack()
        wt0 = es_p1.enter_context(nc.sbuf_tensor("wt0", [128, DE], F32))
        wt1 = es_p1.enter_context(nc.sbuf_tensor("wt1", [128, DE], F32))
        wt2 = es_p1.enter_context(nc.sbuf_tensor("wt2", [128, DE], F32))
        partials = es_p1.enter_context(nc.sbuf_tensor("partials", [128, 8], F32))
        ptmp = es_p1.enter_context(nc.sbuf_tensor("ptmp", [128, 1], F32))
        pt7 = es_p1.enter_context(nc.sbuf_tensor("pt7", [8, 128], F32))
        pt7s = es_p1.enter_context(nc.sbuf_tensor("pt7s", [8, 1], F32))
        srow = es_p1.enter_context(nc.sbuf_tensor("srow", [1, 16], F32))
        q8t = es_p1.enter_context(nc.sbuf_tensor("q8t", [128, DE], I8))
        qbft = es_p1.enter_context(nc.sbuf_tensor("qbft", [128, DE], BF16))
        stage0 = es_p1.enter_context(nc.sbuf_tensor("stage0", [128, DE], BF16))
        stage1 = es_p1.enter_context(nc.sbuf_tensor("stage1", [128, DE], BF16))
        stageb = [stage0, stage1]
        ptpw0 = es_p1.enter_context(nc.psum_tensor("ptpw0", [128, 1024], BF16))
        ptpw1 = es_p1.enter_context(nc.psum_tensor("ptpw1", [128, 1024], BF16))
        wt = [wt0, wt1, wt2]
        ptpw = [ptpw0, ptpw1]

        wslices = [
            (wq_s, 256, E, RC_Q), (wk_s, 256, E, RC_K), (wv_s, 256, E, RC_V),
            (wo_s, 256, E, RC_O), (up_s, 1024, E, RC_UP),
            (gate_s, 1024, E, RC_GATE), (down_s, 256, DE, RC_DOWN),
        ]

        vdone = []
        gload = []
        i = 0
        for (ap, rows, cols, rc) in wslices:
            for r0 in range(0, rows, 128):
                buf = i % 3
                ii = i

                def ldw(q, ap=ap, r0=r0, buf=buf, ii=ii, cols=cols):
                    if ii >= 3:
                        q.wait_ge(sems["vector"], vdone[ii - 3])
                    ins = q.dma_start(out=wt[buf][:, 0:cols], in_=ap[r0:r0 + 128, :])
                    gload.append(pool.issue([ins]))
                on("sync" if ii % 2 == 0 else "scalar", ldw)

                def absr(v, rc=rc, buf=buf, cols=cols, ii=ii, first=(r0 == 0)):
                    pool.wait(v, gload[ii])
                    v.tensor_reduce(out=ptmp[:], in_=wt[buf][:, 0:cols],
                                    axis=AX.X, op=Alu.add, apply_absolute_value=True)
                    v.drain()
                    if first:
                        c = inc("vector", v.tensor_copy(out=partials[:, rc:rc + 1], in_=ptmp[:]))
                    else:
                        c = inc("vector", v.tensor_tensor(out=partials[:, rc:rc + 1],
                                                          in0=partials[:, rc:rc + 1],
                                                          in1=ptmp[:], op=Alu.add))
                    vdone.append(c)
                on("vector", absr)
                i += 1
        c_passA = cnt["vector"]

        g_pt = [None, None]

        def pr1(sync):
            sync.wait_ge(sems["vector"], c_passA)
            g_pt[0] = pool.issue([sync.dma_start(out=pt_hbm, in_=partials[:])])
            pool.wait(sync, g_pt[0])
            g_pt[1] = pool.issue([sync.dma_start(
                out=pt7[:], in_=pt_hbm.rearrange("p t -> t p"),
                allow_slow_non_contiguous=True)])
        on("sync", pr1)

        def pr2(v):
            pool.wait(v, g_pt[1])
            inc("vector", v.tensor_reduce(out=pt7s[:], in_=pt7[:], axis=AX.X, op=Alu.add))
        on("vector", pr2)
        c_pr2 = cnt["vector"]

        g_arin_h = [None]

        def pr3(sync):
            sync.wait_ge(sems["vector"], c_pr2)
            dst = ar_in.ap().rearrange("x (t o) -> (x t) o", o=1)[0:8, :]
            g_arin_h[0] = pool.issue([sync.dma_start(out=dst, in_=pt7s[:],
                                                     allow_slow_non_contiguous=True)])
        on("sync", pr3)
        g_arin = g_arin_h[0]

        def cc1(g):
            pool.wait(g, g_arin)
            inc("cc", g.collective_compute("AllReduce", Alu.add, replica_groups=rg,
                                           ins=[ar_in.ap().opt()], outs=[ar_out.ap().opt()]))
        on("gpsimd", cc1)
        c_ar = cnt["cc"]

        numel = {RC_Q: E * E, RC_K: E * E, RC_V: E * E, RC_O: E * E,
                 RC_UP: DE * E, RC_GATE: DE * E, RC_DOWN: E * DE}
        g_srow_h = [None]

        def sc1(sync):
            sync.wait_ge(sems["cc"], c_ar)
            g_srow_h[0] = pool.issue([sync.dma_start(out=srow[0:1, 0:8],
                                                     in_=ar_out.ap()[0:1, 0:8])])
        on("sync", sc1)
        g_srow = g_srow_h[0]

        if DEBUG:
            def dpart(sync):
                sync.wait_ge(sems["vector"], c_passA)
                inc("out", sync.dma_start(out=dbg["dbg_partials"], in_=partials[:]), 16)
            on("sync", dpart)

        def sc2(v):
            pool.wait(v, g_srow)
            for k in range(7):
                v.tensor_scalar(out=srow[0:1, k:k + 1], in0=srow[0:1, k:k + 1],
                                scalar1=1.0 / numel[k], scalar2=None, op0=Alu.mult)
                v.drain()
            for k in (RC_Q, RC_K, RC_V, RC_O):
                v.tensor_scalar(out=srow[0:1, k:k + 1], in0=srow[0:1, k:k + 1],
                                scalar1=1.0e-5, scalar2=None, op0=Alu.max)
                v.drain()
            for k in (RC_UP, RC_GATE, RC_DOWN):
                v.tensor_scalar(out=srow[0:1, k:k + 1], in0=srow[0:1, k:k + 1],
                                scalar1=1.0e-8, scalar2=None, op0=Alu.add)
                v.drain()
            v.tensor_scalar(out=srow[0:1, SC_Q:SC_Q + 1], in0=srow[0:1, RC_Q:RC_Q + 1],
                            scalar1=1.0 / math.sqrt(D), scalar2=None, op0=Alu.mult)
            v.drain()
            for s_c, d_c in ((RC_K, SC_K), (RC_V, SC_V), (RC_O, SC_O),
                             (RC_UP, SC_UP), (RC_GATE, SC_GATE), (RC_DOWN, SC_DOWN)):
                v.tensor_copy(out=srow[0:1, d_c:d_c + 1], in_=srow[0:1, s_c:s_c + 1])
            v.drain()
            inc("vector", v.reciprocal(out=srow[0:1, 0:7], in_=srow[0:1, 0:7]))
        on("vector", sc2)
        c_sc2 = cnt["vector"]

        g_scales_h = [None]

        def sc3(sync):
            sync.wait_ge(sems["vector"], c_sc2)
            g1 = pool.issue([sync.dma_start(out=sc_hbm, in_=srow[:])])
            pool.wait(sync, g1)
            g_scales_h[0] = pool.issue([sync.dma_start(
                out=scales_sb[:], in_=sc_hbm[0:1, 0:16].partition_broadcast(128))])
            if DEBUG:
                inc("out", sync.dma_start(out=dbg["dbg_scales"], in_=srow[:]), 16)
        on("sync", sc3)
        g_scales = g_scales_h[0]

        qkv_off = {id(wq_s): 0, id(wk_s): 256, id(wv_s): 512}
        wjobs = [
            (wq_s, 256, E, RC_Q, None), (wk_s, 256, E, RC_K, None),
            (wv_s, 256, E, RC_V, None), (wo_s, 256, E, RC_O, ag_wo_in),
            (up_s, 1024, E, RC_UP, ag_up_in), (gate_s, 1024, E, RC_GATE, ag_gate_in),
            (down_s, 256, DE, RC_DOWN, ag_dn_in),
        ]
        ji = 0
        qdone = []
        gloadb = []
        tpfree = [0, 0]
        stg_ag = [0, 0]
        stg_vec = [0, 0]
        for (ap, rows, cols, rc, dest) in wjobs:
            nets = cols // 128
            for j in range(rows // 128):
                buf = ji % 3
                jj = ji

                def ldq(q, ap=ap, j=j, buf=buf, jj=jj, cols=cols):
                    pool.wait(q, g_scales)
                    if jj >= 3:
                        q.wait_ge(sems["vector"], qdone[jj - 3])
                    ins = q.dma_start(out=wt[buf][:, 0:cols],
                                      in_=ap[j * 128:(j + 1) * 128, :])
                    gloadb.append(pool.issue([ins]))
                on("sync" if jj % 2 == 0 else "scalar", ldq)

                def qn(v, rc=rc, buf=buf, cols=cols, jj=jj):
                    pool.wait(v, gloadb[jj])
                    pool.wait(v, g_scales)
                    v.tensor_scalar(out=wt[buf][:, 0:cols], in0=wt[buf][:, 0:cols],
                                    scalar1=scales_sb[:, rc:rc + 1], scalar2=1.0,
                                    op0=Alu.mult, op1=Alu.min)
                    v.drain()
                    v.tensor_scalar(out=q8t[:, 0:cols], in0=wt[buf][:, 0:cols],
                                    scalar1=-1.0, scalar2=None, op0=Alu.max)
                    v.drain()
                    c = inc("vector", v.tensor_copy(out=qbft[:, 0:cols], in_=q8t[:, 0:cols]))
                    qdone.append(c)
                on("vector", qn)
                c_q = cnt["vector"]

                for g0 in range(0, nets, 8):
                    gN = min(8, nets - g0)
                    pb = (g0 // 8) % 2

                    def tpw(t, g0=g0, gN=gN, pb=pb, c_q=c_q, sb_=ji % 2):
                        t.wait_ge(sems["vector"], c_q)
                        t.wait_ge(sems["vector"], tpfree[pb])
                        t.wait_ge(sems["ag"], stg_ag[sb_])
                        t.wait_ge(sems["vector"], stg_vec[sb_])
                        pool.wait(t, g_const)
                        for k in range(gN):
                            ins = t.transpose(out=ptpw[pb][:, 128 * k:128 * (k + 1)],
                                              in_=qbft[:, 128 * (g0 + k):128 * (g0 + k + 1)],
                                              identity=ident[:])
                            if k == gN - 1:
                                inc("tensor", ins)
                    on("tensor", tpw)
                    c_tp = cnt["tensor"]

                    def evw(v, g0=g0, gN=gN, pb=pb, c_tp=c_tp, sb_=ji % 2):
                        v.wait_ge(sems["tensor"], c_tp)
                        c = inc("vector", v.tensor_copy(out=stageb[sb_][:, 128 * g0:128 * (g0 + gN)],
                                                        in_=ptpw[pb][:, 0:128 * gN]))
                        tpfree[pb] = c
                    on("vector", evw)
                c_stage = cnt["vector"]

                if dest is None:
                    off = qkv_off[id(ap)] + 128 * j

                    def wqkv(v, off=off, nets=nets, c_stage=c_stage, sb_=ji % 2):
                        v.wait_ge(sems["vector"], c_stage)
                        v.drain()
                        dst = wqkvT.ap().rearrange("p (et c) -> p et c", c=768)[:, :, off:off + 128]
                        src = stageb[sb_].ap().rearrange("p (et c) -> p et c", c=128)[:, 0:nets, :]
                        c = inc("vector", v.tensor_copy(out=dst, in_=src))
                        stg_vec[sb_] = c
                    on("vector", wqkv)
                else:
                    def wag(sync, dest=dest, j=j, nets=nets, c_stage=c_stage, sb_=ji % 2):
                        sync.wait_ge(sems["vector"], c_stage)
                        dst = dest.ap().rearrange("(et p) o -> p et o", p=128)[:, :, 128 * j:128 * (j + 1)]
                        src = stageb[sb_].ap().rearrange("p (et c) -> p et c", c=128)[:, 0:nets, :]
                        inc("ag", sync.dma_start(out=dst, in_=src), 16)
                        stg_ag[sb_] = cnt["ag"]
                    on("sync", wag)
                ji += 1
        c_wqkv_rdy = cnt["vector"]
        c_ag_writes = cnt["ag"]

        def cc2(g):
            g.wait_ge(sems["ag"], c_ag_writes)
            for (i_, o_) in ((ag_wo_in, ag_wo_out), (ag_up_in, ag_up_out),
                             (ag_gate_in, ag_gate_out), (ag_dn_in, ag_dn_out)):
                inc("cc", g.collective_compute("AllGather", Alu.bypass, replica_groups=rg,
                                               ins=[i_.ap().opt()], outs=[o_.ap().opt()]))
        on("gpsimd", cc2)
        cc_w = cnt["cc"]
        c_p1_tensor = cnt["tensor"]

        es_p1.close()

        qT_sb = es_a.enter_context(nc.sbuf_tensor("qT_sb", [128, NT * 256], BF16))
        kT_sb = es_a.enter_context(nc.sbuf_tensor("kT_sb", [128, NT * 256], BF16))
        v_sb = es_a.enter_context(nc.sbuf_tensor("v_sb", [128, NT * 256], BF16))
        attnout_sb = es_a.enter_context(nc.sbuf_tensor("attnout_sb", [128, NT * 256], BF16))

        (S1, MEAN, EX2, M2, VAR, SQV, RSTD, MX, MN, T1, T2, AMAX, MAXLN, CC_, SXV,
         INVSX, AQ, BQ) = range(18)

        es_p2 = ExitStack()
        xt0 = es_p2.enter_context(nc.sbuf_tensor("xt0", [128, E], F32))
        xt1 = es_p2.enter_context(nc.sbuf_tensor("xt1", [128, E], F32))
        sqdump = es_p2.enter_context(nc.sbuf_tensor("sqdump", [128, E], F32))
        xq8a = es_p2.enter_context(nc.sbuf_tensor("xq8a", [128, E], I8))
        xq8b = es_p2.enter_context(nc.sbuf_tensor("xq8b", [128, E], I8))
        xqbfa = es_p2.enter_context(nc.sbuf_tensor("xqbfa", [128, E], BF16))
        xqbfb = es_p2.enter_context(nc.sbuf_tensor("xqbfb", [128, E], BF16))
        xq8 = [xq8a, xq8b]
        xqbf = [xqbfa, xqbfb]
        xqT = es_p2.enter_context(nc.sbuf_tensor("xqT", [128, E], BF16))
        qkstage0 = es_p2.enter_context(nc.sbuf_tensor("qkstage0", [128, 512], BF16))
        qkstage1 = es_p2.enter_context(nc.sbuf_tensor("qkstage1", [128, 512], BF16))
        qkstage = [qkstage0, qkstage1]
        ptpx0 = es_p2.enter_context(nc.psum_tensor("ptpx0", [128, 1024], BF16))
        ptpx1 = es_p2.enter_context(nc.psum_tensor("ptpx1", [128, 1024], BF16))
        pq1 = es_p2.enter_context(nc.psum_tensor("pq1", [128, 512], F32))
        pq2 = es_p2.enter_context(nc.psum_tensor("pq2", [128, 256], F32))
        ptpqk = es_p2.enter_context(nc.psum_tensor("ptpqk", [128, 512], BF16))
        xts = [xt0, xt1]
        ptpx = [ptpx0, ptpx1]
        c_tile_done = [0] * NT
        gx = [None] * NT

        c_quant_h = [0] * NT
        c_mmq_h = [0] * NT
        c_evq_h = [0] * NT
        c_tqk_h = [0] * NT
        c_evqk_h = [0] * NT
        c_xqT_h = [0] * NT

        def p2_pre(tt):
            buf = tt % 2
            po = 20 * (tt % 2)

            def ldx(sync, tt=tt, buf=buf):
                if tt >= 2:
                    sync.wait_ge(sems["scalar"], c_tile_done[tt - 2])
                    sync.wait_ge(sems["vector"], c_quant_h[tt - 2] if tt >= 2 else 0)
                else:
                    sync.wait_ge(sems["vector"], c_wqkv_rdy)
                    sync.wait_ge(sems["tensor"], c_p1_tensor)
                gx[tt] = pool.issue([sync.dma_start(out=xts[buf][:],
                                                    in_=x_full[tt * 128:(tt + 1) * 128, :])])
            on("sync", ldx)

            def st_v(v, tt=tt, buf=buf, po=po):
                pool.wait(v, gx[tt])
                v.tensor_reduce(out=sm[:, po + S1:po + S1 + 1], in_=xts[buf][:],
                                axis=AX.X, op=Alu.add)
                v.tensor_reduce(out=sm[:, po + MX:po + MX + 1], in_=xts[buf][:],
                                axis=AX.X, op=Alu.max)
                inc("vector", v.tensor_reduce(out=sm[:, po + MN:po + MN + 1], in_=xts[buf][:],
                                              axis=AX.X, op=Alu.min))
            on("vector", st_v)
            c_sum = cnt["vector"]

            def st_s(s, buf=buf, po=po, c_sum=c_sum):
                s.wait_ge(sems["vector"], c_sum)
                sinc(s.activation(out=sqdump[:], in_=xts[buf][:], func=AF.Square,
                                  accum_out=sm[:, po + EX2:po + EX2 + 1]))
            c_sq = sq(st_s)

            def sml_v(v, po=po, c_sq=c_sq):
                v.wait_ge(sems["scalar"], c_sq)
                v.tensor_scalar(out=sm[:, po + MEAN:po + MEAN + 1], in0=sm[:, po + S1:po + S1 + 1],
                                scalar1=1.0 / E, scalar2=None, op0=Alu.mult)
                v.tensor_scalar(out=sm[:, po + EX2:po + EX2 + 1], in0=sm[:, po + EX2:po + EX2 + 1],
                                scalar1=1.0 / E, scalar2=None, op0=Alu.mult)
                v.drain()
                v.tensor_tensor(out=sm[:, po + M2:po + M2 + 1], in0=sm[:, po + MEAN:po + MEAN + 1],
                                in1=sm[:, po + MEAN:po + MEAN + 1], op=Alu.mult)
                v.drain()
                inc("vector", v.tensor_tensor(out=sm[:, po + VAR:po + VAR + 1],
                                              in0=sm[:, po + EX2:po + EX2 + 1],
                                              in1=sm[:, po + M2:po + M2 + 1], op=Alu.subtract))
            on("vector", sml_v)
            c_var = cnt["vector"]

            def sml_s(s, po=po, c_var=c_var):
                s.wait_ge(sems["vector"], c_var)
                s.activation(out=sm[:, po + SQV:po + SQV + 1],
                             in_=sm[:, po + VAR:po + VAR + 1],
                             func=AF.Ln, bias=eps_t[:, 0:1], scale=1.0)
                s.drain()
                sinc(s.activation(out=sm[:, po + RSTD:po + RSTD + 1],
                                  in_=sm[:, po + SQV:po + SQV + 1],
                                  func=AF.Exp, scale=neghalf_t[:, 0:1]))
            c_sqv = sq(sml_s)

            def qv(v, buf=buf, po=po, tt=tt, c_sqv=c_sqv):
                v.wait_ge(sems["scalar"], c_sqv)
                P = lambda k: sm[:, po + k:po + k + 1]
                v.tensor_tensor(out=P(T1), in0=P(MX), in1=P(MEAN), op=Alu.subtract)
                v.tensor_tensor(out=P(T2), in0=P(MEAN), in1=P(MN), op=Alu.subtract)
                v.drain()
                v.tensor_tensor(out=P(AMAX), in0=P(T1), in1=P(T2), op=Alu.max)
                v.drain()
                v.tensor_tensor(out=P(MAXLN), in0=P(AMAX), in1=P(RSTD), op=Alu.mult)
                v.drain()
                v.tensor_scalar(out=P(CC_), in0=P(MAXLN), scalar1=1.0e-5, scalar2=None, op0=Alu.max)
                v.drain()
                v.reciprocal(out=P(SXV), in_=P(CC_))
                v.drain()
                v.tensor_scalar(out=P(SXV), in0=P(SXV), scalar1=127.0, scalar2=None, op0=Alu.mult)
                v.tensor_scalar(out=P(INVSX), in0=P(CC_), scalar1=1.0 / 127.0, scalar2=None, op0=Alu.mult)
                v.drain()
                v.tensor_tensor(out=P(AQ), in0=P(RSTD), in1=P(SXV), op=Alu.mult)
                v.drain()
                v.tensor_scalar(out=P(BQ), in0=P(MEAN), scalar1=P(AQ), scalar2=-1.0,
                                op0=Alu.mult, op1=Alu.mult)
                v.drain()
                v.tensor_scalar(out=xq8[buf][:], in0=xts[buf][:], scalar1=P(AQ), scalar2=P(BQ),
                                op0=Alu.mult, op1=Alu.add)
                v.drain()
                inc("vector", v.tensor_copy(out=xqbf[buf][:], in_=xq8[buf][:]))
            on("vector", qv)
            c_quant_h[tt] = cnt["vector"]

            if DEBUG and tt == 0:
                def dxq(v):
                    v.tensor_copy(out=snap["xq"][:], in_=xq8[0][:])
                    inc("vector", v.tensor_copy(out=snap["sm"][:], in_=sm[:]))
                on("vector", dxq)
                c_quant_h[tt] = cnt["vector"]

        def p2_pe(tt):
            buf = tt % 2
            po = 20 * (tt % 2)
            c_quant = c_quant_h[tt]
            c_xqT = 0
            for g0 in (0, 8):
                pb = g0 // 8

                def tpx(t, g0=g0, pb=pb, tt=tt, buf=buf, c_quant=c_quant):
                    t.wait_ge(sems["vector"], c_quant)
                    if tt > 0:
                        t.wait_ge(sems["scalar"], c_xqT_h[tt - 1])
                    pool.wait(t, g_const)
                    for k in range(8):
                        ins = t.transpose(out=ptpx[pb][:, 128 * k:128 * (k + 1)],
                                          in_=xqbf[buf][:, 128 * (g0 + k):128 * (g0 + k + 1)],
                                          identity=ident[:])
                        if k == 7:
                            inc("tensor", ins)
                on("tensor", tpx)
                c_tpx = cnt["tensor"]

                def evx(s, g0=g0, pb=pb, c_tpx=c_tpx, tt=tt):
                    s.wait_ge(sems["tensor"], c_tpx)
                    if tt > 0:
                        s.wait_ge(sems["tensor"], c_mmq_h[tt - 1])
                    sinc(s.copy(out=xqT[:, 128 * g0:128 * (g0 + 8)], in_=ptpx[pb][:]))
                c_xqT = sq(evx)
                c_xqT_h[tt] = c_xqT

            def mmq(t, c_xqT=c_xqT, tt=tt):
                t.wait_ge(sems["scalar"], c_xqT)
                t.wait_ge(sems["vector"], c_wqkv_rdy)
                if tt > 0:
                    t.wait_ge(sems["vector"], c_evq_h[tt - 1])
                for et in range(ET):
                    t.matmul(out=pq1[:], lhsT=xqT[:, 128 * et:128 * (et + 1)],
                             rhs=wqkvT[:, 768 * et:768 * et + 512],
                             start=(et == 0), stop=(et == ET - 1))
                for et in range(ET):
                    ins = t.matmul(out=pq2[:], lhsT=xqT[:, 128 * et:128 * (et + 1)],
                                   rhs=wqkvT[:, 768 * et + 512:768 * (et + 1)],
                                   start=(et == 0), stop=(et == ET - 1))
                    if et == ET - 1:
                        inc("tensor", ins)
            on("tensor", mmq)
            c_mmq_h[tt] = cnt["tensor"]
            c_mmq = cnt["tensor"]

            def evq(v, tt=tt, po=po, c_mmq=c_mmq, buf=buf):
                v.wait_ge(sems["tensor"], c_mmq)
                if tt > 1:
                    v.wait_ge(sems["tensor"], c_tqk_h[tt - 2])
                P = lambda k: sm[:, po + k:po + k + 1]
                v.tensor_scalar(out=qkstage[buf][:, 0:256], in0=pq1[:, 0:256],
                                scalar1=P(INVSX), scalar2=scales_sb[:, SC_Q:SC_Q + 1],
                                op0=Alu.mult, op1=Alu.mult)
                v.tensor_scalar(out=qkstage[buf][:, 256:512], in0=pq1[:, 256:512],
                                scalar1=P(INVSX), scalar2=scales_sb[:, SC_K:SC_K + 1],
                                op0=Alu.mult, op1=Alu.mult)
                inc("vector", v.tensor_scalar(out=v_sb[:, 256 * tt:256 * (tt + 1)],
                                              in0=pq2[:], scalar1=P(INVSX),
                                              scalar2=scales_sb[:, SC_V:SC_V + 1],
                                              op0=Alu.mult, op1=Alu.mult))
            on("vector", evq)
            c_evq_h[tt] = cnt["vector"]
            c_evq = cnt["vector"]


        def p2_pe_b(tt):
            buf = tt % 2

            def tqk(t, tt=tt, buf=buf):
                t.wait_ge(sems["vector"], c_evq_h[tt])
                if tt > 0:
                    t.wait_ge(sems["scalar"], c_evqk_h[tt - 1])
                for k in range(4):
                    ins = t.transpose(out=ptpqk[:, 128 * k:128 * (k + 1)],
                                      in_=qkstage[buf][:, 128 * k:128 * (k + 1)],
                                      identity=ident[:])
                    if k == 3:
                        inc("tensor", ins)
            on("tensor", tqk)
            c_tqk_h[tt] = cnt["tensor"]
            c_tqk = cnt["tensor"]

            def evqk(s, tt=tt, c_tqk=c_tqk):
                s.wait_ge(sems["tensor"], c_tqk)
                s.copy(out=qT_sb[:, 256 * tt:256 * (tt + 1)], in_=ptpqk[:, 0:256])
                sinc(s.copy(out=kT_sb[:, 256 * tt:256 * (tt + 1)], in_=ptpqk[:, 256:512]))
            c = sq(evqk)
            c_evqk_h[tt] = c
            c_tile_done[tt] = c

        for step in range(NT + 2):
            if step < NT:
                p2_pre(step)
            if step >= 1 and step - 1 < NT:
                p2_pe(step - 1)
            if step >= 2:
                p2_pe_b(step - 2)

        sflush()
        es_p2.close()

        es_p3 = ExitStack()
        attnT0 = es_p3.enter_context(nc.sbuf_tensor("attnT0", [128, 512], BF16))
        attnT1 = es_p3.enter_context(nc.sbuf_tensor("attnT1", [128, 512], BF16))
        scm0 = es_p3.enter_context(nc.sbuf_tensor("scm0", [128, 512], F32))
        scm1 = es_p3.enter_context(nc.sbuf_tensor("scm1", [128, 512], F32))
        outT = es_p3.enter_context(nc.sbuf_tensor("outT", [128, 4096], BF16))
        sumsrow = es_p3.enter_context(nc.sbuf_tensor("sumsrow", [1, S], F32))
        recips = es_p3.enter_context(nc.sbuf_tensor("recips", [128, NTB], F32))
        sumsbf = es_p3.enter_context(nc.sbuf_tensor("sumsbf", [1, S], BF16))
        sums32 = es_p3.enter_context(nc.sbuf_tensor("sums32", [32, 128], BF16))
        psc0 = es_p3.enter_context(nc.psum_tensor("psc0", [128, 512], F32))
        psc1 = es_p3.enter_context(nc.psum_tensor("psc1", [128, 512], F32))
        pout = es_p3.enter_context(nc.psum_tensor("pout", [128, 512], F32))
        psum1 = es_p3.enter_context(nc.psum_tensor("psum1", [1, 512], F32))
        ptpo = es_p3.enter_context(nc.psum_tensor("ptpo", [128, 512], BF16))
        ptps = es_p3.enter_context(nc.psum_tensor("ptps", [128, 32], BF16))
        psc = [psc0, psc1]
        c_qkv_all = cnt["vector"]
        g_sums_prev = [None]

        attnTb = [attnT0, attnT1]
        scmb = [scm0, scm1]
        for bh in range(4):
            b, ds = bh // 2, bh % 2
            tiles = [(sc, tl) for sc in range(8) for tl in range(4 * sc + 4)]
            NTI = len(tiles)
            c_exp_h = [0] * NTI
            c_ov_h = [0] * NTI
            psc_cons = [None, None]
            c_ev_sc_map = {}
            c_ev_last = 0

            def emit_mm_sc(idx):
                sc, tl = tiles[idx]
                tg = NTB * b + tl
                pb = idx % 2

                def mm_sc(t, tg=tg, pb=pb, sc=sc, cons=psc_cons[pb]):
                    t.wait_ge(sems["vector"], c_qkv_all)
                    if cons is not None:
                        t.wait_ge(sems[cons[0]], cons[1])
                    qv_ = qT_sb.ap().rearrange("p (t c) -> p t c", c=256)
                    rhs = qv_[:, NTB * b + 4 * sc:NTB * b + 4 * sc + 4,
                              128 * ds:128 * (ds + 1)]
                    inc("tensor", t.matmul(
                        out=psc[pb][:],
                        lhsT=kT_sb[:, 256 * tg + 128 * ds:256 * tg + 128 * (ds + 1)],
                        rhs=rhs, start=True, stop=True))
                on("tensor", mm_sc)
                return cnt["tensor"]

            def emit_exp(idx, c_mmsc):
                sc, tl = tiles[idx]
                pb = idx % 2
                diag = tl >= 4 * sc
                c_at_free = c_ov_h[idx - 2] if idx >= 2 else 0
                if diag:
                    kk = tl - 4 * sc

                    def mskv(v, kk=kk, pb=pb, c_mmsc=c_mmsc):
                        v.wait_ge(sems["tensor"], c_mmsc)
                        if c_exp_h[idx - 2] if idx >= 2 else 0:
                            v.wait_ge(sems["scalar"], c_exp_h[idx - 2])
                        inc("vector", v.tensor_tensor(
                            out=scmb[pb][:], in0=psc[pb][:],
                            in1=cmask[:, 384 - 128 * kk:896 - 128 * kk], op=Alu.add))
                    on("vector", mskv)
                    c_mv = cnt["vector"]
                    psc_cons[pb] = ("vector", c_mv)

                    def expd(s, pb=pb, c_mv=c_mv, c_at_free=c_at_free):
                        s.wait_ge(sems["vector"], c_mv)
                        if c_at_free:
                            s.wait_ge(sems["tensor"], c_at_free)
                        sinc(s.activation(out=attnTb[pb][:], in_=scmb[pb][:], func=AF.Exp))
                    sq(expd)
                else:
                    def expn(s, pb=pb, c_mmsc=c_mmsc, c_at_free=c_at_free):
                        s.wait_ge(sems["tensor"], c_mmsc)
                        if c_at_free:
                            s.wait_ge(sems["tensor"], c_at_free)
                        sinc(s.activation(out=attnTb[pb][:], in_=psc[pb][:], func=AF.Exp))
                    sq(expn)
                    psc_cons[pb] = ("scalar", cnt["scalar"])
                c_exp_h[idx] = cnt["scalar"]

            def emit_mm_ov(idx):
                sc, tl = tiles[idx]
                tg = NTB * b + tl
                pb = idx % 2
                first = tl == 0
                last = tl == 4 * sc + 3
                c_exp = c_exp_h[idx]
                c_prev_ev = c_ev_sc_map.get(sc - 1, 0)

                def mm_ov(t, tg=tg, pb=pb, first=first, last=last,
                          c_exp=c_exp, c_prev_ev=c_prev_ev):
                    t.wait_ge(sems["scalar"], c_exp)
                    if first and c_prev_ev:
                        t.wait_ge(sems["vector"], c_prev_ev)
                    t.matmul(out=pout[:],
                             lhsT=v_sb[:, 256 * tg + 128 * ds:256 * tg + 128 * (ds + 1)],
                             rhs=attnTb[pb][:], start=first, stop=last)
                    inc("tensor", t.matmul(out=psum1[:], lhsT=ones_bf[:], rhs=attnTb[pb][:],
                                           start=first, stop=last))
                on("tensor", mm_ov)
                c_ov_h[idx] = cnt["tensor"]
                if last:
                    def ev_sc(v, sc=sc, c_last_mm=cnt["tensor"]):
                        v.wait_ge(sems["tensor"], c_last_mm)
                        if sc == 0 and g_sums_prev[0] is not None:
                            pool.wait(v, g_sums_prev[0])
                        v.tensor_copy(out=outT[:, 512 * sc:512 * (sc + 1)], in_=pout[:])
                        inc("vector", v.tensor_copy(out=sumsrow[0:1, 512 * sc:512 * (sc + 1)],
                                                    in_=psum1[0:1, :]))
                    on("vector", ev_sc)
                    c_ev_sc_map[sc] = cnt["vector"]

            for idx in range(NTI):
                c_mmsc = emit_mm_sc(idx)
                emit_exp(idx, c_mmsc)
                if idx >= 1:
                    emit_mm_ov(idx - 1)
            emit_mm_ov(NTI - 1)
            sflush()
            c_ev_sc = c_ev_sc_map[7]

            def sums_bf(v, bh=bh):
                v.drain()
                inc("vector", v.tensor_copy(out=sumsbf[0:1, :], in_=sumsrow[0:1, :]))
            on("vector", sums_bf)
            c_sbf = cnt["vector"]

            g_sums_h = [None]

            def sums_mv(sync, c_sbf=c_sbf):
                sync.wait_ge(sems["vector"], c_sbf)
                g1 = pool.issue([sync.dma_start(out=sums_bounce, in_=sumsbf[0:1, :])])
                pool.wait(sync, g1)
                src = sums_bounce.rearrange("x (t p) -> (x t) p", t=32)
                g_sums_h[0] = pool.issue([sync.dma_start(out=sums32[:], in_=src)])
            on("sync", sums_mv)
            g_sums = g_sums_h[0]
            g_sums_prev[0] = g_sums

            def sums_tp(t, g_sums=g_sums):
                pool.wait(t, g_sums)
                inc("tensor", t.transpose(out=ptps[:], in_=sums32[:], identity=ident[0:32, 0:32]))
            on("tensor", sums_tp)
            c_stp = cnt["tensor"]

            def recv(v, c_stp=c_stp, bh=bh):
                v.wait_ge(sems["tensor"], c_stp)
                v.tensor_copy(out=recips[:], in_=ptps[:])
                v.drain()
                inc("vector", v.reciprocal(out=recips[:], in_=recips[:]))
                v.drain()
                if DEBUG and bh == 0:
                    inc("vector", v.tensor_copy(out=snap["rec"][:], in_=recips[:]))
            on("vector", recv)
            c_rec = cnt["vector"]

            c_ev_tp = 0
            for sc in range(8):
                def tpo(t, sc=sc, c_rec=c_rec, c_ev_tp=c_ev_tp):
                    t.wait_ge(sems["vector"], c_rec)
                    if c_ev_tp:
                        t.wait_ge(sems["vector"], c_ev_tp)
                    for j in range(4):
                        ins = t.transpose(out=ptpo[:, 128 * j:128 * (j + 1)],
                                          in_=outT[:, 512 * sc + 128 * j:512 * sc + 128 * (j + 1)],
                                          identity=ident[:])
                        if j == 3:
                            inc("tensor", ins)
                on("tensor", tpo)
                c_tpo = cnt["tensor"]

                def evtp(v, sc=sc, b=b, ds=ds, c_tpo=c_tpo):
                    v.wait_ge(sems["tensor"], c_tpo)
                    for j in range(4):
                        st = 4 * sc + j
                        ins = v.tensor_scalar(
                            out=attnout_sb[:, 256 * (NTB * b + st) + 128 * ds:
                                           256 * (NTB * b + st) + 128 * (ds + 1)],
                            in0=ptpo[:, 128 * j:128 * (j + 1)],
                            scalar1=recips[:, st:st + 1], scalar2=None, op0=Alu.mult)
                        if j == 3:
                            inc("vector", ins)
                on("vector", evtp)
                c_ev_tp = cnt["vector"]

        if DEBUG:
            def dato(v):
                v.wait_ge(sems["vector"], cnt["vector"])
                inc("vector", v.tensor_copy(out=snap["ao"][:], in_=attnout_sb[:, 0:256]))
            on("vector", dato)

        def at_out(sync):
            sync.wait_ge(sems["vector"], cnt["vector"])
            dst = a2a_in.ap().rearrange("(t p) f -> p t f", p=128)
            src_ = attnout_sb.ap().rearrange("p (t f) -> p t f", f=256)
            inc("a2aw", sync.dma_start(out=dst, in_=src_), 16)
        on("sync", at_out)
        c_a2a_in = cnt["a2aw"]

        def cc3(g):
            g.wait_ge(sems["a2aw"], c_a2a_in)
            with allow_ici_all_to_all():
                inc("cc", g.collective_compute("AllToAll", Alu.bypass, replica_groups=rg,
                                               ins=[a2a_in.ap().opt()], outs=[a2a_out.ap().opt()]))
        on("gpsimd", cc3)
        cc_at = cnt["cc"]
        c_p3_v = cnt["vector"]
        c_p3_t = cnt["tensor"]
        c_p3_s = cnt["scalar"]

        es_p3.close()
        es_a.close()

        OS1, OMEAN, OEX2, OM2, OVAR, OSQV, ORSTD, OCC, OSX, OINVSX, OBT = range(11)

        es_p4 = ExitStack()
        woT_sb = es_p4.enter_context(nc.sbuf_tensor("woT_sb", [128, ET * 2048], BF16))
        oin0 = es_p4.enter_context(nc.sbuf_tensor("oin0", [128, E], BF16))
        oin1 = es_p4.enter_context(nc.sbuf_tensor("oin1", [128, E], BF16))
        oinb = [oin0, oin1]
        xres0 = es_p4.enter_context(nc.sbuf_tensor("xres0", [128, E], F32))
        xres1 = es_p4.enter_context(nc.sbuf_tensor("xres1", [128, E], F32))
        xresb = [xres0, xres1]
        oq8a = es_p4.enter_context(nc.sbuf_tensor("oq8a", [128, E], I8))
        oq8b = es_p4.enter_context(nc.sbuf_tensor("oq8b", [128, E], I8))
        oq8b_ = [oq8a, oq8b]
        oqbfa = es_p4.enter_context(nc.sbuf_tensor("oqbfa", [128, E], BF16))
        oqbfb = es_p4.enter_context(nc.sbuf_tensor("oqbfb", [128, E], BF16))
        oqbfb_ = [oqbfa, oqbfb]
        oqT = es_p4.enter_context(nc.sbuf_tensor("oqT", [128, E], BF16))
        oproj0 = es_p4.enter_context(nc.sbuf_tensor("oproj0", [128, E], F32))
        oproj1 = es_p4.enter_context(nc.sbuf_tensor("oproj1", [128, E], F32))
        oprojb = [oproj0, oproj1]
        osq = es_p4.enter_context(nc.sbuf_tensor("osq", [128, E], F32))
        lnres0 = es_p4.enter_context(nc.sbuf_tensor("lnres0", [128, E], F32))
        lnres1 = es_p4.enter_context(nc.sbuf_tensor("lnres1", [128, E], F32))
        lnresb = [lnres0, lnres1]
        ptpo2a = es_p4.enter_context(nc.psum_tensor("ptpo2a", [128, 1024], BF16))
        ptpo2b = es_p4.enter_context(nc.psum_tensor("ptpo2b", [128, 1024], BF16))
        po0 = es_p4.enter_context(nc.psum_tensor("po0", [128, 512], F32))
        po1 = es_p4.enter_context(nc.psum_tensor("po1", [128, 512], F32))
        po2 = es_p4.enter_context(nc.psum_tensor("po2", [128, 512], F32))
        po3 = es_p4.enter_context(nc.psum_tensor("po3", [128, 512], F32))
        pob = [po0, po1, po2, po3]
        ptpo2 = [ptpo2a, ptpo2b]

        g_wo_h = [None]

        def ldwo(sync):
            sync.wait_ge(sems["cc"], cc_w)
            sync.wait_ge(sems["vector"], c_p3_v)
            sync.wait_ge(sems["tensor"], c_p3_t)
            sync.wait_ge(sems["scalar"], c_p3_s)
            ins = []
            for r in range(NC):
                src = ag_wo_out.ap()[E * r:E * (r + 1), :].rearrange("(et p) o -> p et o", p=128)
                dst = woT_sb.ap().rearrange("p (et c) -> p et c", c=2048)[:, :, 256 * r:256 * (r + 1)]
                ins.append(sync.dma_start(out=dst, in_=src))
            g_wo_h[0] = pool.issue(ins)
        on("sync", ldwo)
        g_wo = g_wo_h[0]

        NJ = TPC // 128
        c_oq_h = [0] * NJ
        c_evoq_h = [0] * NJ
        c_evo_h = [[0] * 4 for _ in range(NJ)]
        c_x1v_h = [0] * NJ

        def p4_pre(j):
            ob = j % 2
            OP = 11 * (j % 2)
            g_oin_h = [None]

            def ldoin(sync, j=j, ob=ob):
                sync.wait_ge(sems["cc"], cc_at)
                if j >= 2:
                    sync.wait_ge(sems["vector"], c_oq_h[j - 2])
                ins = []
                for r in range(NC):
                    ins.append(sync.dma_start(out=oinb[ob][:, 256 * r:256 * (r + 1)],
                                              in_=a2a_out.ap()[TPC * r + 128 * j:
                                                               TPC * r + 128 * (j + 1), :]))
                g_oin_h[0] = pool.issue(ins)
            on("sync", ldoin)
            g_oin = g_oin_h[0]

            def oquant(v, g_oin=g_oin, j=j, ob=ob, OP=OP):
                pool.wait(v, g_oin)
                oin = oinb[ob]
                if DEBUG and j == 0:
                    v.tensor_copy(out=snap["oin"][:], in_=oin[:, 0:512])
                P = lambda k: sm[:, OP + k:OP + k + 1]
                v.tensor_reduce(out=P(OCC), in_=oin[:], axis=AX.X, op=Alu.max,
                                apply_absolute_value=True)
                v.drain()
                v.tensor_scalar(out=P(OCC), in0=P(OCC), scalar1=1.0e-5, scalar2=None, op0=Alu.max)
                v.drain()
                v.reciprocal(out=P(OSX), in_=P(OCC))
                v.drain()
                v.tensor_scalar(out=P(OSX), in0=P(OSX), scalar1=127.0, scalar2=None, op0=Alu.mult)
                v.tensor_scalar(out=P(OINVSX), in0=P(OCC), scalar1=1.0 / 127.0,
                                scalar2=None, op0=Alu.mult)
                v.drain()
                v.tensor_scalar(out=oq8b_[ob][:], in0=oin[:], scalar1=P(OSX),
                                scalar2=None, op0=Alu.mult)
                v.drain()
                inc("vector", v.tensor_copy(out=oqbfb_[ob][:], in_=oq8b_[ob][:]))
            on("vector", oquant)
            c_oq_h[j] = cnt["vector"]

        def p4_mid(j):
            ob = j % 2
            c_oq = c_oq_h[j]
            for g0 in (0, 8):
                pb = g0 // 8

                def tpoq(t, g0=g0, pb=pb, c_oq=c_oq, j=j, ob=ob):
                    t.wait_ge(sems["vector"], c_oq)
                    if j > 0:
                        t.wait_ge(sems["scalar"], c_evoq_h[j - 1])
                    pool.wait(t, g_const)
                    for k in range(8):
                        ins = t.transpose(out=ptpo2[pb][:, 128 * k:128 * (k + 1)],
                                          in_=oqbfb_[ob][:, 128 * (g0 + k):128 * (g0 + k + 1)],
                                          identity=ident[:])
                        if k == 7:
                            inc("tensor", ins)
                on("tensor", tpoq)
                c_t = cnt["tensor"]

                def evoq(s, g0=g0, pb=pb, c_t=c_t):
                    s.wait_ge(sems["tensor"], c_t)
                    sinc(s.copy(out=oqT[:, 128 * g0:128 * (g0 + 8)], in_=ptpo2[pb][:]))
                c_evoq_h[j] = sq(evoq)
            c_evoq = c_evoq_h[j]

            for ch in range(4):
                def mm_o(t, ch=ch, j=j, c_evoq=c_evoq):
                    t.wait_ge(sems["scalar"], c_evoq)
                    pool.wait(t, g_wo)
                    if j > 0:
                        t.wait_ge(sems["vector"], c_evo_h[j - 1][ch])
                    for et in range(ET):
                        ins = t.matmul(out=pob[ch][:], lhsT=oqT[:, 128 * et:128 * (et + 1)],
                                       rhs=woT_sb[:, 2048 * et + 512 * ch:2048 * et + 512 * (ch + 1)],
                                       start=(et == 0), stop=(et == ET - 1))
                        if et == ET - 1:
                            inc("tensor", ins)
                on("tensor", mm_o)
                c_mo = cnt["tensor"]

                def ev_o(v, ch=ch, j=j, c_mo=c_mo, OP=11 * (j % 2)):
                    v.wait_ge(sems["tensor"], c_mo)
                    inc("vector", v.tensor_scalar(out=oprojb[j % 2][:, 512 * ch:512 * (ch + 1)],
                                                  in0=pob[ch][:],
                                                  scalar1=sm[:, OP + OINVSX:OP + OINVSX + 1],
                                                  scalar2=scales_sb[:, SC_O:SC_O + 1],
                                                  op0=Alu.mult, op1=Alu.mult))
                on("vector", ev_o)
                c_evo_h[j][ch] = cnt["vector"]

        def p4_post(j):
            ob = j % 2
            OP = 11 * (j % 2)
            oproj = oprojb[ob]
            c_op = c_evo_h[j][3]
            g_xr_h = [None]

            def ldxr(sync, j=j, ob=ob):
                if j >= 2:
                    sync.wait_ge(sems["vector"], c_x1v_h[j - 2])
                g_xr_h[0] = pool.issue([sync.dma_start(out=xresb[ob][:],
                                                       in_=x_mine[128 * j:128 * (j + 1), :])])
            on("sync", ldxr)
            g_xr = g_xr_h[0]

            def oln_v(v, c_op=c_op, OP=OP, oproj=oproj):
                v.wait_ge(sems["vector"], c_op)
                inc("vector", v.tensor_reduce(out=sm[:, OP + OS1:OP + OS1 + 1], in_=oproj[:],
                                              axis=AX.X, op=Alu.add))
            on("vector", oln_v)
            c_os1 = cnt["vector"]

            def oln_s(s, c_os1=c_os1, OP=OP, oproj=oproj):
                s.wait_ge(sems["vector"], c_os1)
                sinc(s.activation(out=osq[:], in_=oproj[:], func=AF.Square,
                                  accum_out=sm[:, OP + OEX2:OP + OEX2 + 1]))
            c_osq = sq(oln_s)

            def oln_v2(v, c_osq=c_osq, OP=OP):
                v.wait_ge(sems["scalar"], c_osq)
                P = lambda k: sm[:, OP + k:OP + k + 1]
                v.tensor_scalar(out=P(OMEAN), in0=P(OS1), scalar1=1.0 / E, scalar2=None, op0=Alu.mult)
                v.tensor_scalar(out=P(OEX2), in0=P(OEX2), scalar1=1.0 / E, scalar2=None, op0=Alu.mult)
                v.drain()
                v.tensor_tensor(out=P(OM2), in0=P(OMEAN), in1=P(OMEAN), op=Alu.mult)
                v.drain()
                inc("vector", v.tensor_tensor(out=P(OVAR), in0=P(OEX2), in1=P(OM2), op=Alu.subtract))
            on("vector", oln_v2)
            c_ov = cnt["vector"]

            def oln_s2(s, c_ov=c_ov, OP=OP):
                s.wait_ge(sems["vector"], c_ov)
                s.activation(out=sm[:, OP + OSQV:OP + OSQV + 1], in_=sm[:, OP + OVAR:OP + OVAR + 1],
                             func=AF.Ln, bias=eps_t[:, 0:1], scale=1.0)
                s.drain()
                sinc(s.activation(out=sm[:, OP + ORSTD:OP + ORSTD + 1],
                                  in_=sm[:, OP + OSQV:OP + OSQV + 1],
                                  func=AF.Exp, scale=neghalf_t[:, 0:1]))
            c_osqv = sq(oln_s2)

            def oln_v3(v, j=j, c_osqv=c_osqv, ob=ob, OP=OP, oproj=oproj, g_xr=g_xr,
                       c_x1w=cnt["x1w"]):
                v.wait_ge(sems["scalar"], c_osqv)
                v.wait_ge(sems["x1w"], c_x1w)
                pool.wait(v, g_xr)
                P = lambda k: sm[:, OP + k:OP + k + 1]
                v.tensor_scalar(out=P(OBT), in0=P(OMEAN), scalar1=P(ORSTD),
                                scalar2=-1.0, op0=Alu.mult, op1=Alu.mult)
                v.drain()
                v.tensor_scalar(out=lnresb[ob][:], in0=oproj[:],
                                scalar1=P(ORSTD), scalar2=P(OBT),
                                op0=Alu.mult, op1=Alu.add)
                v.drain()
                inc("vector", v.tensor_tensor(out=lnresb[ob][:], in0=lnresb[ob][:],
                                              in1=xresb[ob][:], op=Alu.add))
                if DEBUG and j == 0:
                    inc("vector", v.tensor_copy(out=snap["x1"][:], in_=lnresb[0][:, 0:1024]))
            on("vector", oln_v3)
            c_x1 = cnt["vector"]
            c_x1v_h[j] = c_x1

            def stx1(sync, j=j, c_x1=c_x1, ob=ob):
                sync.wait_ge(sems["vector"], c_x1)
                inc("x1w", sync.dma_start(out=x1_hbm[128 * j:128 * (j + 1), :], in_=lnresb[ob][:]), 16)
            on("sync", stx1)

        for step in range(NJ + 2):
            if step < NJ:
                p4_pre(step)
            if 1 <= step <= NJ:
                p4_mid(step - 1)
            if step >= 2:
                p4_post(step - 2)
        sflush()
        c_x1_all = cnt["x1w"]
        c_p4_v = cnt["vector"]
        c_p4_t = cnt["tensor"]
        c_p4_s = cnt["scalar"]
        es_p4.close()

        FS1, FMEAN, FEX2, FM2, FVAR, FSQV, FRSTD, FBT = range(31, 39)

        es_p5 = ExitStack()
        x1t = es_p5.enter_context(nc.sbuf_tensor("x1t", [128, E], F32))
        fsq = es_p5.enter_context(nc.sbuf_tensor("fsq", [128, E], F32))
        hin = es_p5.enter_context(nc.sbuf_tensor("hin", [128, E], BF16))
        hinT = es_p5.enter_context(nc.sbuf_tensor("hinT", [128, ET * 512], BF16))
        wug0 = es_p5.enter_context(nc.sbuf_tensor("wug0", [128, 2 * 2048], BF16))
        wug1 = es_p5.enter_context(nc.sbuf_tensor("wug1", [128, 2 * 2048], BF16))
        silu0 = es_p5.enter_context(nc.sbuf_tensor("silu0", [128, 512], F32))
        silu1 = es_p5.enter_context(nc.sbuf_tensor("silu1", [128, 512], F32))
        silub = [silu0, silu1]
        hT = es_p5.enter_context(nc.sbuf_tensor("hT", [128, 32 * 512], BF16))
        wdn0 = es_p5.enter_context(nc.sbuf_tensor("wdn0", [128, 32 * 512], BF16))
        wdn1 = es_p5.enter_context(nc.sbuf_tensor("wdn1", [128, 32 * 512], BF16))
        y0 = es_p5.enter_context(nc.sbuf_tensor("y0", [128, E], F32))
        y1 = es_p5.enter_context(nc.sbuf_tensor("y1", [128, E], F32))
        y2 = es_p5.enter_context(nc.sbuf_tensor("y2", [128, E], F32))
        y3 = es_p5.enter_context(nc.sbuf_tensor("y3", [128, E], F32))
        ptpf = es_p5.enter_context(nc.psum_tensor("ptpf", [128, 1024], BF16))
        pupa = es_p5.enter_context(nc.psum_tensor("pupa", [128, 512], F32))
        pupb = es_p5.enter_context(nc.psum_tensor("pupb", [128, 512], F32))
        pgta = es_p5.enter_context(nc.psum_tensor("pgta", [128, 512], F32))
        pgtb = es_p5.enter_context(nc.psum_tensor("pgtb", [128, 512], F32))
        pup = [pupa, pupb]
        pgt = [pgta, pgtb]
        pd0 = es_p5.enter_context(nc.psum_tensor("pd0", [128, 512], F32))
        pd1 = es_p5.enter_context(nc.psum_tensor("pd1", [128, 512], F32))
        wug = [wug0, wug1]
        wdn = [wdn0, wdn1]
        ysb = [y0, y1, y2, y3]
        pd = [pd0, pd1]

        for grp in range(2):
            c_hinT = 0
            for j2 in range(4):
                j = 4 * grp + j2
                g_x1_h = [None]

                def ldx1(sync, j=j, j2=j2, grp=grp):
                    sync.wait_ge(sems["x1w"], c_x1_all)
                    if grp == 0 and j2 == 0:
                        sync.wait_ge(sems["vector"], c_p4_v)
                        sync.wait_ge(sems["tensor"], c_p4_t)
                        sync.wait_ge(sems["scalar"], c_p4_s)
                    else:
                        sync.wait_ge(sems["vector"], cnt["vector"])
                    g_x1_h[0] = pool.issue([sync.dma_start(out=x1t[:],
                                                           in_=x1_hbm[128 * j:128 * (j + 1), :])])
                on("sync", ldx1)
                g_x1 = g_x1_h[0]

                def fst_v(v, g_x1=g_x1):
                    pool.wait(v, g_x1)
                    inc("vector", v.tensor_reduce(out=sm[:, FS1:FS1 + 1], in_=x1t[:],
                                                  axis=AX.X, op=Alu.add))
                on("vector", fst_v)
                c_fs1 = cnt["vector"]

                def fst_s(s, c_fs1=c_fs1):
                    s.wait_ge(sems["vector"], c_fs1)
                    sinc(s.activation(out=fsq[:], in_=x1t[:], func=AF.Square,
                                      accum_out=sm[:, FEX2:FEX2 + 1]))
                c_fsq = sq(fst_s)

                def fsm(v, c_fsq=c_fsq):
                    v.wait_ge(sems["scalar"], c_fsq)
                    P = lambda k: sm[:, k:k + 1]
                    v.tensor_scalar(out=P(FMEAN), in0=P(FS1), scalar1=1.0 / E, scalar2=None, op0=Alu.mult)
                    v.tensor_scalar(out=P(FEX2), in0=P(FEX2), scalar1=1.0 / E, scalar2=None, op0=Alu.mult)
                    v.drain()
                    v.tensor_tensor(out=P(FM2), in0=P(FMEAN), in1=P(FMEAN), op=Alu.mult)
                    v.drain()
                    inc("vector", v.tensor_tensor(out=P(FVAR), in0=P(FEX2), in1=P(FM2), op=Alu.subtract))
                on("vector", fsm)
                c_fv = cnt["vector"]

                def fsm_s(s, c_fv=c_fv):
                    s.wait_ge(sems["vector"], c_fv)
                    s.activation(out=sm[:, FSQV:FSQV + 1], in_=sm[:, FVAR:FVAR + 1],
                                 func=AF.Ln, bias=eps_t[:, 0:1], scale=1.0)
                    s.drain()
                    sinc(s.activation(out=sm[:, FRSTD:FRSTD + 1],
                                      in_=sm[:, FSQV:FSQV + 1],
                                      func=AF.Exp, scale=neghalf_t[:, 0:1]))
                c_fsqv = sq(fsm_s)

                def fnorm(v, c_fsqv=c_fsqv):
                    v.wait_ge(sems["scalar"], c_fsqv)
                    P = lambda k: sm[:, k:k + 1]
                    v.tensor_scalar(out=P(FBT), in0=P(FMEAN), scalar1=P(FRSTD),
                                    scalar2=-1.0, op0=Alu.mult, op1=Alu.mult)
                    v.drain()
                    inc("vector", v.tensor_scalar(out=hin[:], in0=x1t[:],
                                                  scalar1=P(FRSTD), scalar2=P(FBT),
                                                  op0=Alu.mult, op1=Alu.add))
                on("vector", fnorm)
                c_hin = cnt["vector"]

                for g0 in (0, 8):
                    def tph(t, g0=g0, c_hin=c_hin):
                        t.wait_ge(sems["vector"], c_hin)
                        t.wait_ge(sems["vector"], cnt["vector"])
                        pool.wait(t, g_const)
                        for k in range(8):
                            ins = t.transpose(out=ptpf[:, 128 * k:128 * (k + 1)],
                                              in_=hin[:, 128 * (g0 + k):128 * (g0 + k + 1)],
                                              identity=ident[:])
                            if k == 7:
                                inc("tensor", ins)
                    on("tensor", tph)
                    c_tph = cnt["tensor"]

                    def evh(v, g0=g0, j2=j2, c_tph=c_tph):
                        v.wait_ge(sems["tensor"], c_tph)
                        dst = hinT.ap().rearrange("p (et c) -> p et c", c=512)[:, g0:g0 + 8,
                                                                               128 * j2:128 * (j2 + 1)]
                        src = ptpf.ap().rearrange("p (et c) -> p et c", c=128)
                        inc("vector", v.tensor_copy(out=dst, in_=src))
                    on("vector", evh)
                c_hinT = cnt["vector"]

            for xh in range(2):
                c_silu_done = [cnt["scalar"], cnt["scalar"]]
                c_doh_done = [cnt["vector"], cnt["vector"]]
                c_mug_h = [cnt["tensor"], cnt["tensor"]]
                for ebl in range(32):
                    eb = 32 * xh + ebl
                    wb = eb % 2
                    r, ebr = eb // 8, eb % 8
                    g_ug_h = [None]

                    def ldug(sync, r=r, ebr=ebr, wb=wb, eb=eb, ebl=ebl):
                        sync.wait_ge(sems["cc"], cc_w)
                        if c_mug_h[ebl % 2]:
                            sync.wait_ge(sems["tensor"], c_mug_h[ebl % 2])
                        ins = []
                        for idx, agt in enumerate((ag_up_out, ag_gate_out)):
                            src = agt.ap()[E * r:E * (r + 1), 128 * ebr:128 * (ebr + 1)] \
                                .rearrange("(et p) o -> p et o", p=128)
                            dst = wug[wb].ap().rearrange("p (et c) -> p et c", c=128) \
                                [:, 16 * idx:16 * idx + 16, :]
                            ins.append(sync.dma_start(out=dst, in_=src))
                        g_ug_h[0] = pool.issue(ins)
                    on("sync", ldug)
                    g_ug = g_ug_h[0]

                    ps = ebl % 2
                    def mm_ug(t, wb=wb, g_ug=g_ug, c_hinT=c_hinT, eb=eb, ps=ps,
                              c_sfree=c_silu_done[ebl % 2], c_vfree=c_doh_done[ebl % 2]):
                        pool.wait(t, g_ug)
                        t.wait_ge(sems["vector"], c_hinT)
                        if c_sfree:
                            t.wait_ge(sems["scalar"], c_sfree)
                        if c_vfree:
                            t.wait_ge(sems["vector"], c_vfree)
                        for et in range(ET):
                            t.matmul(out=pup[ps][:], lhsT=wug[wb][:, 128 * et:128 * (et + 1)],
                                     rhs=hinT[:, 512 * et:512 * (et + 1)],
                                     start=(et == 0), stop=(et == ET - 1))
                        for et in range(ET):
                            ins = t.matmul(out=pgt[ps][:], lhsT=wug[wb][:, 2048 + 128 * et:2048 + 128 * (et + 1)],
                                           rhs=hinT[:, 512 * et:512 * (et + 1)],
                                           start=(et == 0), stop=(et == ET - 1))
                            if et == ET - 1:
                                inc("tensor", ins)
                    on("tensor", mm_ug)
                    c_mug = cnt["tensor"]
                    c_mug_h[ebl % 2] = c_mug

                    def dosilu(s, c_mug=c_mug, ps=ps, sb_=ebl % 2):
                        s.wait_ge(sems["tensor"], c_mug)
                        sinc(s.activation(out=silub[sb_][:], in_=pup[ps][:], func=AF.Silu,
                                          scale=scales_sb[:, SC_UP:SC_UP + 1]))
                    c_sil = sq(dosilu)
                    c_silu_done[ebl % 2] = c_sil

                    def doh(v, ebl=ebl, c_sil=c_sil, ps=ps, sb_=ebl % 2, grp=grp, eb=eb):
                        v.wait_ge(sems["scalar"], c_sil)
                        inc("vector", v.scalar_tensor_tensor(
                            out=hT[:, 512 * ebl:512 * (ebl + 1)], in0=pgt[ps][:],
                            scalar=scales_sb[:, SC_GATE:SC_GATE + 1], in1=silub[sb_][:],
                            op0=Alu.mult, op1=Alu.mult))
                        if DEBUG and grp == 0 and eb == 0:
                            v.drain()
                            inc("vector", v.tensor_copy(out=snap["h"][:], in_=hT[:, 0:512]))
                    on("vector", doh)
                    c_doh_done[ebl % 2] = cnt["vector"]
                c_h_ready = cnt["vector"]

                c_evd_done = [cnt["vector"], cnt["vector"]]
                c_md_h = [cnt["tensor"], cnt["tensor"]]
                for ch in range(4):
                    wb = ch % 2
                    g_dn_h = [None]

                    def lddn(sync, ch=ch, xh=xh, wb=wb):
                        sync.wait_ge(sems["cc"], cc_w)
                        if c_md_h[ch % 2]:
                            sync.wait_ge(sems["tensor"], c_md_h[ch % 2])
                        ins = []
                        for half in range(2):
                            rr = 2 * ch + half
                            src = ag_dn_out.ap()[DE * rr + 4096 * xh:DE * rr + 4096 * (xh + 1), :] \
                                .rearrange("(xt p) d -> p xt d", p=128)
                            dst = wdn[wb].ap().rearrange("p (xt c) -> p xt c", c=512) \
                                [:, :, 256 * half:256 * (half + 1)]
                            ins.append(sync.dma_start(out=dst, in_=src))
                        g_dn_h[0] = pool.issue(ins)
                    on("sync", lddn)
                    g_dn = g_dn_h[0]

                    for j2 in range(4):
                        pb = j2 % 2

                        def mm_d(t, j2=j2, pb=pb, wb=wb, g_dn=g_dn, c_h_ready=c_h_ready,
                                 c_free=c_evd_done[pb]):
                            pool.wait(t, g_dn)
                            t.wait_ge(sems["vector"], c_h_ready)
                            if c_free:
                                t.wait_ge(sems["vector"], c_free)
                            for xt in range(32):
                                ins = t.matmul(out=pd[pb][:],
                                               lhsT=hT[:, 512 * xt + 128 * j2:512 * xt + 128 * (j2 + 1)],
                                               rhs=wdn[wb][:, 512 * xt:512 * (xt + 1)],
                                               start=(xt == 0), stop=(xt == 31))
                                if xt == 31:
                                    inc("tensor", ins)
                        on("tensor", mm_d)
                        c_md = cnt["tensor"]
                        c_md_h[ch % 2] = c_md

                        def ev_d(v, j2=j2, pb=pb, ch=ch, xh=xh, c_md=c_md):
                            v.wait_ge(sems["tensor"], c_md)
                            if xh == 0:
                                ins = v.tensor_scalar(out=ysb[j2][:, 512 * ch:512 * (ch + 1)],
                                                      in0=pd[pb][:],
                                                      scalar1=scales_sb[:, SC_DOWN:SC_DOWN + 1],
                                                      scalar2=None, op0=Alu.mult)
                            else:
                                ins = v.scalar_tensor_tensor(out=ysb[j2][:, 512 * ch:512 * (ch + 1)],
                                                             in0=pd[pb][:],
                                                             scalar=scales_sb[:, SC_DOWN:SC_DOWN + 1],
                                                             in1=ysb[j2][:, 512 * ch:512 * (ch + 1)],
                                                             op0=Alu.mult, op1=Alu.add)
                            inc("vector", ins)
                        on("vector", ev_d)
                        c_evd_done[pb] = cnt["vector"]
            c_y = cnt["vector"]

            for j2 in range(4):
                j = 4 * grp + j2
                yt = ysb[j2]

                def yln_v(v, yt=yt, c_y=c_y):
                    v.wait_ge(sems["vector"], c_y)
                    inc("vector", v.tensor_reduce(out=sm[:, FS1:FS1 + 1], in_=yt[:],
                                                  axis=AX.X, op=Alu.add))
                on("vector", yln_v)
                c_ys1 = cnt["vector"]

                def yln_s(s, yt=yt, c_ys1=c_ys1):
                    s.wait_ge(sems["vector"], c_ys1)
                    sinc(s.activation(out=fsq[:], in_=yt[:], func=AF.Square,
                                      accum_out=sm[:, FEX2:FEX2 + 1]))
                c_ysq = sq(yln_s)

                def yln_v2(v, c_ysq=c_ysq):
                    v.wait_ge(sems["scalar"], c_ysq)
                    P = lambda k: sm[:, k:k + 1]
                    v.tensor_scalar(out=P(FMEAN), in0=P(FS1), scalar1=1.0 / E, scalar2=None, op0=Alu.mult)
                    v.tensor_scalar(out=P(FEX2), in0=P(FEX2), scalar1=1.0 / E, scalar2=None, op0=Alu.mult)
                    v.drain()
                    v.tensor_tensor(out=P(FM2), in0=P(FMEAN), in1=P(FMEAN), op=Alu.mult)
                    v.drain()
                    inc("vector", v.tensor_tensor(out=P(FVAR), in0=P(FEX2), in1=P(FM2), op=Alu.subtract))
                on("vector", yln_v2)
                c_yv = cnt["vector"]

                def yln_s2(s, c_yv=c_yv):
                    s.wait_ge(sems["vector"], c_yv)
                    s.activation(out=sm[:, FSQV:FSQV + 1], in_=sm[:, FVAR:FVAR + 1],
                                 func=AF.Ln, bias=eps_t[:, 0:1], scale=1.0)
                    s.drain()
                    sinc(s.activation(out=sm[:, FRSTD:FRSTD + 1],
                                      in_=sm[:, FSQV:FSQV + 1],
                                      func=AF.Exp, scale=neghalf_t[:, 0:1]))
                c_ysqv = sq(yln_s2)

                def ynorm(v, yt=yt, c_ysqv=c_ysqv):
                    v.wait_ge(sems["scalar"], c_ysqv)
                    P = lambda k: sm[:, k:k + 1]
                    v.tensor_scalar(out=P(FBT), in0=P(FMEAN), scalar1=P(FRSTD),
                                    scalar2=-1.0, op0=Alu.mult, op1=Alu.mult)
                    v.drain()
                    inc("vector", v.tensor_scalar(out=fsq[:], in0=yt[:],
                                                  scalar1=P(FRSTD), scalar2=P(FBT),
                                                  op0=Alu.mult, op1=Alu.add))
                on("vector", ynorm)
                c_yn = cnt["vector"]

                g_x1b_h = [None]

                def ldx1b(sync, j=j, c_yn=c_yn):
                    sync.wait_ge(sems["vector"], c_yn)
                    g_x1b_h[0] = pool.issue([sync.dma_start(out=x1t[:],
                                                            in_=x1_hbm[128 * j:128 * (j + 1), :])])
                on("sync", ldx1b)
                g_x1b = g_x1b_h[0]

                def yres(v, yt=yt, j=j, g_x1b=g_x1b):
                    pool.wait(v, g_x1b)
                    inc("vector", v.tensor_tensor(out=yt[:], in0=fsq[:], in1=x1t[:], op=Alu.add))
                    if DEBUG and j == 0:
                        inc("vector", v.tensor_copy(out=snap["y"][:], in_=yt[:, 0:1024]))
                on("vector", yres)
                c_out = cnt["vector"]

                def stout(sync, j=j, yt=yt, c_out=c_out):
                    sync.wait_ge(sems["vector"], c_out)
                    inc("out", sync.dma_start(out=out_ext[128 * j:128 * (j + 1), :], in_=yt[:]), 16)
                on("sync", stout)
        sflush()
        es_p5.close()

        if DEBUG:
            def ddump(sync):
                sync.wait_ge(sems["vector"], cnt["vector"])
                for nm_s, nm_d in [("sm", "dbg_sm"), ("xq", "dbg_xq"),
                                   ("ao", "dbg_attnout"),
                                   ("rec", "dbg_sums"), ("oin", "dbg_oin"),
                                   ("x1", "dbg_x1"), ("h", "dbg_h"), ("y", "dbg_y")]:
                    inc("out", sync.dma_start(out=dbg[nm_d], in_=snap[nm_s][:]), 16)
            on("sync", ddump)

        def fin(sync):
            sync.wait_ge(sems["out"], cnt["out"])
        on("sync", fin)

    return nc


_cache = {}


def _host_consts():
    t = np.arange(128)[:, None]
    j = np.arange(896)[None, :]
    cmask = np.where((j - 384) >= t, 0.0, MASKV).astype(np.float32)
    ident = np.eye(128, dtype=ml_dtypes.bfloat16)
    return cmask, ident


def kernel(L0_attn_x, L0_attn_w_q, L0_attn_w_k, L0_attn_w_v, L0_attn_w_o,
           L0_ffn_W_up, L0_ffn_W_gate, L0_ffn_W_down):
    if "nc" not in _cache:
        _cache["nc"] = build_program()
    nc = _cache["nc"]

    x2d = np.ascontiguousarray(np.asarray(L0_attn_x, dtype=np.float32).reshape(TOK, E))
    wq = np.ascontiguousarray(np.asarray(L0_attn_w_q, dtype=np.float32))
    wk = np.ascontiguousarray(np.asarray(L0_attn_w_k, dtype=np.float32))
    wv = np.ascontiguousarray(np.asarray(L0_attn_w_v, dtype=np.float32))
    wo = np.ascontiguousarray(np.asarray(L0_attn_w_o, dtype=np.float32))
    wup = np.ascontiguousarray(np.asarray(L0_ffn_W_up, dtype=np.float32))
    wgate = np.ascontiguousarray(np.asarray(L0_ffn_W_gate, dtype=np.float32))
    wdown = np.ascontiguousarray(np.asarray(L0_ffn_W_down, dtype=np.float32))
    cmask, ident = _host_consts()

    in_maps = []
    for c in range(NC):
        in_maps.append({
            "x_full": x2d,
            "x_mine": np.ascontiguousarray(x2d[TPC * c:TPC * (c + 1)]),
            "wq_s": np.ascontiguousarray(wq[256 * c:256 * (c + 1)]),
            "wk_s": np.ascontiguousarray(wk[256 * c:256 * (c + 1)]),
            "wv_s": np.ascontiguousarray(wv[256 * c:256 * (c + 1)]),
            "wo_s": np.ascontiguousarray(wo[256 * c:256 * (c + 1)]),
            "up_s": np.ascontiguousarray(wup[1024 * c:1024 * (c + 1)]),
            "gate_s": np.ascontiguousarray(wgate[1024 * c:1024 * (c + 1)]),
            "down_s": np.ascontiguousarray(wdown[256 * c:256 * (c + 1)]),
            "cmask_in": cmask,
            "ident_in": ident,
        })
    res = run_bass_kernel_spmd(nc, in_maps, core_ids=list(range(NC)))
    if DEBUG:
        _cache["results"] = res
    out = np.concatenate([np.asarray(res.results[c]["out"]) for c in range(NC)], axis=0)
    return out.reshape(B, S, E).astype(np.float32)


# version 14
# speedup vs baseline: 1.1185x; 1.0392x over previous
import math
from contextlib import ExitStack

import numpy as np
import ml_dtypes

import concourse.bass as bass
import concourse.mybir as mybir
from concourse.alu_op_type import AluOpType
from concourse.bass_utils import run_bass_kernel_spmd
from concourse.aot_env import allow_ici_all_to_all

F32 = mybir.dt.float32
BF16 = mybir.dt.bfloat16
I8 = mybir.dt.int8
AF = mybir.ActivationFunctionType
AX = mybir.AxisListType
Alu = AluOpType

B, S, E, H, D, DE = 2, 4096, 2048, 16, 128, 8192
TOK = B * S
NC = 8
TPC = TOK // NC
NT = TOK // 128
NTB = S // 128
ET = E // 128
XT = DE // 128
MASKV = -1.0e5

RC_Q, RC_K, RC_V, RC_O, RC_UP, RC_GATE, RC_DOWN = range(7)
SC_Q, SC_K, SC_V, SC_O, SC_UP, SC_GATE, SC_DOWN = range(7, 14)

DEBUG = False


class DmaGroup:
    __slots__ = ("sem", "target")

    def __init__(self, sem, target):
        self.sem = sem
        self.target = target


class DmaPool:

    def __init__(self, nc, es, n=24):
        self.sems = [es.enter_context(nc.semaphore(f"dpool{i}")) for i in range(n)]
        self.tot = [0] * n
        self.idx = 0

    def issue(self, instrs):
        i = self.idx % len(self.sems)
        self.idx += 1
        for ins in instrs:
            ins.then_inc(self.sems[i], 16)
            self.tot[i] += 16
        return DmaGroup(self.sems[i], self.tot[i])

    def wait(self, eng, grp):
        eng.wait_ge(grp.sem, grp.target)


def build_program():
    nc = bass.Bass("TRN2", target_bir_lowering=False)

    x_full = nc.dram_tensor("x_full", [TOK, E], F32, kind="ExternalInput").ap()
    x_mine = nc.dram_tensor("x_mine", [TPC, E], F32, kind="ExternalInput").ap()
    wq_s = nc.dram_tensor("wq_s", [256, E], F32, kind="ExternalInput").ap()
    wk_s = nc.dram_tensor("wk_s", [256, E], F32, kind="ExternalInput").ap()
    wv_s = nc.dram_tensor("wv_s", [256, E], F32, kind="ExternalInput").ap()
    wo_s = nc.dram_tensor("wo_s", [256, E], F32, kind="ExternalInput").ap()
    up_s = nc.dram_tensor("up_s", [1024, E], F32, kind="ExternalInput").ap()
    gate_s = nc.dram_tensor("gate_s", [1024, E], F32, kind="ExternalInput").ap()
    down_s = nc.dram_tensor("down_s", [256, DE], F32, kind="ExternalInput").ap()
    cmask_in = nc.dram_tensor("cmask_in", [128, 896], F32, kind="ExternalInput").ap()
    ident_in = nc.dram_tensor("ident_in", [128, 128], BF16, kind="ExternalInput").ap()
    out_ext = nc.dram_tensor("out", [TPC, E], F32, kind="ExternalOutput").ap()

    dbg = {}
    if DEBUG:
        for nm, sh, dt in [("dbg_scales", [1, 16], F32), ("dbg_xq", [128, E], I8),
                           ("dbg_sm", [128, 44], F32), ("dbg_partials", [128, 8], F32),
                           ("dbg_qk", [128, 512], BF16), ("dbg_v", [128, 256], BF16),
                           ("dbg_attnout", [128, 256], BF16), ("dbg_oin", [128, 512], BF16),
                           ("dbg_x1", [128, 1024], BF16), ("dbg_h", [128, 512], BF16),
                           ("dbg_y", [128, 1024], BF16), ("dbg_sums", [128, 32], F32)]:
            dbg[nm] = nc.dram_tensor(nm, sh, dt, kind="ExternalOutput").ap()

    pt_hbm = nc.dram_tensor("pt_hbm", [128, 8], F32).ap()
    ar_in = nc.dram_tensor("ar_in", [1, 8], F32)
    ar_out = nc.dram_tensor("ar_out", [1, 8], F32)
    sc_hbm = nc.dram_tensor("sc_hbm", [1, 16], F32).ap()
    ag_wo_in = nc.dram_tensor("ag_wo_in", [E, 256], BF16)
    ag_wo_out = nc.dram_tensor("ag_wo_out", [NC * E, 256], BF16, addr_space="Shared")
    ag_up_in = nc.dram_tensor("ag_up_in", [E, 1024], BF16)
    ag_up_out = nc.dram_tensor("ag_up_out", [NC * E, 1024], BF16, addr_space="Shared")
    ag_gate_in = nc.dram_tensor("ag_gate_in", [E, 1024], BF16)
    ag_gate_out = nc.dram_tensor("ag_gate_out", [NC * E, 1024], BF16, addr_space="Shared")
    ag_dn_in = nc.dram_tensor("ag_dn_in", [DE, 256], BF16)
    ag_dn_out = nc.dram_tensor("ag_dn_out", [NC * DE, 256], BF16, addr_space="Shared")
    a2a_in = nc.dram_tensor("a2a_in", [TOK, 256], BF16)
    a2a_out = nc.dram_tensor("a2a_out", [TOK, 256], BF16)
    sums_bounce = nc.dram_tensor("sums_bounce", [1, S], BF16).ap()
    x1_hbm = nc.dram_tensor("x1_hbm", [TPC, E], F32).ap()

    rg = [list(range(NC))]

    es = ExitStack()
    with es:
        block = es.enter_context(nc.Block())
        sems = {n: es.enter_context(nc.semaphore("sem_" + n))
                for n in ["vector", "scalar", "tensor", "cc",
                          "ag", "out", "x1w", "a2aw"]}
        cnt = {n: 0 for n in sems}
        pool = DmaPool(nc, es)

        def inc(n, ins, k=1):
            ins.then_inc(sems[n], k)
            cnt[n] += k
            return cnt[n]

        def on(name, f):
            getattr(block, name)(f)

        squeue = []

        def sinc(ins, k=1):
            ins.then_inc(sems["scalar"], k)
            return ins

        def sq(fn, incs=1):
            squeue.append(fn)
            cnt["scalar"] += incs
            return cnt["scalar"]

        def sflush():
            if not squeue:
                return
            fns = list(squeue)
            squeue.clear()

            def run(s):
                for f in fns:
                    f(s)
            on("scalar", run)

        SB = lambda name, shape, dt: es.enter_context(nc.sbuf_tensor(name, shape, dt))

        scales_sb = SB("scales_sb", [128, 16], F32)
        eps_t = SB("eps_t", [128, 1], F32)
        neghalf_t = SB("neghalf_t", [128, 1], F32)
        ones_bf = SB("ones_bf", [128, 1], BF16)
        ident = SB("ident", [128, 128], BF16)
        cmask = SB("cmask", [128, 896], F32)
        sm = SB("smalls", [128, 44], F32)

        if DEBUG:
            snap = {
                "sm": SB("sn_sm", [128, 44], F32), "xq": SB("sn_xq", [128, E], I8),
                "ao": SB("sn_ao", [128, 256], BF16),
                "rec": SB("sn_rec", [128, NTB], F32), "oin": SB("sn_oin", [128, 512], BF16),
                "x1": SB("sn_x1", [128, 1024], BF16), "h": SB("sn_h", [128, 512], BF16),
                "y": SB("sn_y", [128, 1024], BF16),
            }

        es_a = ExitStack()
        wqkvT = es_a.enter_context(nc.sbuf_tensor("wqkvT", [128, ET * 768], BF16))

        g_const_h = [None]

        def p0s(sync):
            i1 = sync.dma_start(out=ident[:], in_=ident_in)
            i2 = sync.dma_start(out=cmask[:], in_=cmask_in)
            g_const_h[0] = pool.issue([i1, i2])
        on("sync", p0s)
        g_const = g_const_h[0]

        def p0v(v):
            v.memset(eps_t[:], 1.0e-5)
            v.memset(neghalf_t[:], -0.5)
            inc("vector", v.memset(ones_bf[:], 1.0))
        on("vector", p0v)

        es_p1 = ExitStack()
        wt0 = es_p1.enter_context(nc.sbuf_tensor("wt0", [128, DE], F32))
        wt1 = es_p1.enter_context(nc.sbuf_tensor("wt1", [128, DE], F32))
        wt2 = es_p1.enter_context(nc.sbuf_tensor("wt2", [128, DE], F32))
        partials = es_p1.enter_context(nc.sbuf_tensor("partials", [128, 8], F32))
        ptmp = es_p1.enter_context(nc.sbuf_tensor("ptmp", [128, 1], F32))
        pt7 = es_p1.enter_context(nc.sbuf_tensor("pt7", [8, 128], F32))
        pt7s = es_p1.enter_context(nc.sbuf_tensor("pt7s", [8, 1], F32))
        srow = es_p1.enter_context(nc.sbuf_tensor("srow", [1, 16], F32))
        q8t = es_p1.enter_context(nc.sbuf_tensor("q8t", [128, DE], I8))
        qbft = es_p1.enter_context(nc.sbuf_tensor("qbft", [128, DE], BF16))
        stage0 = es_p1.enter_context(nc.sbuf_tensor("stage0", [128, DE], BF16))
        stage1 = es_p1.enter_context(nc.sbuf_tensor("stage1", [128, DE], BF16))
        stageb = [stage0, stage1]
        ptpw0 = es_p1.enter_context(nc.psum_tensor("ptpw0", [128, 1024], BF16))
        ptpw1 = es_p1.enter_context(nc.psum_tensor("ptpw1", [128, 1024], BF16))
        wt = [wt0, wt1, wt2]
        ptpw = [ptpw0, ptpw1]

        wslices = [
            (wq_s, 256, E, RC_Q), (wk_s, 256, E, RC_K), (wv_s, 256, E, RC_V),
            (wo_s, 256, E, RC_O), (up_s, 1024, E, RC_UP),
            (gate_s, 1024, E, RC_GATE), (down_s, 256, DE, RC_DOWN),
        ]

        vdone = []
        gload = []
        i = 0
        for (ap, rows, cols, rc) in wslices:
            for r0 in range(0, rows, 128):
                buf = i % 3
                ii = i

                def ldw(q, ap=ap, r0=r0, buf=buf, ii=ii, cols=cols):
                    if ii >= 3:
                        q.wait_ge(sems["vector"], vdone[ii - 3])
                    ins = q.dma_start(out=wt[buf][:, 0:cols], in_=ap[r0:r0 + 128, :])
                    gload.append(pool.issue([ins]))
                on("sync" if ii % 2 == 0 else "scalar", ldw)

                def absr(v, rc=rc, buf=buf, cols=cols, ii=ii, first=(r0 == 0)):
                    pool.wait(v, gload[ii])
                    v.tensor_reduce(out=ptmp[:], in_=wt[buf][:, 0:cols],
                                    axis=AX.X, op=Alu.add, apply_absolute_value=True)
                    v.drain()
                    if first:
                        c = inc("vector", v.tensor_copy(out=partials[:, rc:rc + 1], in_=ptmp[:]))
                    else:
                        c = inc("vector", v.tensor_tensor(out=partials[:, rc:rc + 1],
                                                          in0=partials[:, rc:rc + 1],
                                                          in1=ptmp[:], op=Alu.add))
                    vdone.append(c)
                on("vector", absr)
                i += 1
        c_passA = cnt["vector"]

        g_pt = [None, None]

        def pr1(sync):
            sync.wait_ge(sems["vector"], c_passA)
            g_pt[0] = pool.issue([sync.dma_start(out=pt_hbm, in_=partials[:])])
            pool.wait(sync, g_pt[0])
            g_pt[1] = pool.issue([sync.dma_start(
                out=pt7[:], in_=pt_hbm.rearrange("p t -> t p"),
                allow_slow_non_contiguous=True)])
        on("sync", pr1)

        def pr2(v):
            pool.wait(v, g_pt[1])
            inc("vector", v.tensor_reduce(out=pt7s[:], in_=pt7[:], axis=AX.X, op=Alu.add))
        on("vector", pr2)
        c_pr2 = cnt["vector"]

        g_arin_h = [None]

        def pr3(sync):
            sync.wait_ge(sems["vector"], c_pr2)
            dst = ar_in.ap().rearrange("x (t o) -> (x t) o", o=1)[0:8, :]
            g_arin_h[0] = pool.issue([sync.dma_start(out=dst, in_=pt7s[:],
                                                     allow_slow_non_contiguous=True)])
        on("sync", pr3)
        g_arin = g_arin_h[0]

        def cc1(g):
            pool.wait(g, g_arin)
            inc("cc", g.collective_compute("AllReduce", Alu.add, replica_groups=rg,
                                           ins=[ar_in.ap().opt()], outs=[ar_out.ap().opt()]))
        on("gpsimd", cc1)
        c_ar = cnt["cc"]

        numel = {RC_Q: E * E, RC_K: E * E, RC_V: E * E, RC_O: E * E,
                 RC_UP: DE * E, RC_GATE: DE * E, RC_DOWN: E * DE}
        g_srow_h = [None]

        def sc1(sync):
            sync.wait_ge(sems["cc"], c_ar)
            g_srow_h[0] = pool.issue([sync.dma_start(out=srow[0:1, 0:8],
                                                     in_=ar_out.ap()[0:1, 0:8])])
        on("sync", sc1)
        g_srow = g_srow_h[0]

        if DEBUG:
            def dpart(sync):
                sync.wait_ge(sems["vector"], c_passA)
                inc("out", sync.dma_start(out=dbg["dbg_partials"], in_=partials[:]), 16)
            on("sync", dpart)

        def sc2(v):
            pool.wait(v, g_srow)
            for k in range(7):
                v.tensor_scalar(out=srow[0:1, k:k + 1], in0=srow[0:1, k:k + 1],
                                scalar1=1.0 / numel[k], scalar2=None, op0=Alu.mult)
                v.drain()
            for k in (RC_Q, RC_K, RC_V, RC_O):
                v.tensor_scalar(out=srow[0:1, k:k + 1], in0=srow[0:1, k:k + 1],
                                scalar1=1.0e-5, scalar2=None, op0=Alu.max)
                v.drain()
            for k in (RC_UP, RC_GATE, RC_DOWN):
                v.tensor_scalar(out=srow[0:1, k:k + 1], in0=srow[0:1, k:k + 1],
                                scalar1=1.0e-8, scalar2=None, op0=Alu.add)
                v.drain()
            v.tensor_scalar(out=srow[0:1, SC_Q:SC_Q + 1], in0=srow[0:1, RC_Q:RC_Q + 1],
                            scalar1=1.0 / math.sqrt(D), scalar2=None, op0=Alu.mult)
            v.drain()
            for s_c, d_c in ((RC_K, SC_K), (RC_V, SC_V), (RC_O, SC_O),
                             (RC_UP, SC_UP), (RC_GATE, SC_GATE), (RC_DOWN, SC_DOWN)):
                v.tensor_copy(out=srow[0:1, d_c:d_c + 1], in_=srow[0:1, s_c:s_c + 1])
            v.drain()
            inc("vector", v.reciprocal(out=srow[0:1, 0:7], in_=srow[0:1, 0:7]))
        on("vector", sc2)
        c_sc2 = cnt["vector"]

        g_scales_h = [None]

        def sc3(sync):
            sync.wait_ge(sems["vector"], c_sc2)
            g1 = pool.issue([sync.dma_start(out=sc_hbm, in_=srow[:])])
            pool.wait(sync, g1)
            g_scales_h[0] = pool.issue([sync.dma_start(
                out=scales_sb[:], in_=sc_hbm[0:1, 0:16].partition_broadcast(128))])
            if DEBUG:
                inc("out", sync.dma_start(out=dbg["dbg_scales"], in_=srow[:]), 16)
        on("sync", sc3)
        g_scales = g_scales_h[0]

        qkv_off = {id(wq_s): 0, id(wk_s): 256, id(wv_s): 512}
        wjobs = [
            (wq_s, 256, E, RC_Q, None), (wk_s, 256, E, RC_K, None),
            (wv_s, 256, E, RC_V, None), (wo_s, 256, E, RC_O, ag_wo_in),
            (up_s, 1024, E, RC_UP, ag_up_in), (gate_s, 1024, E, RC_GATE, ag_gate_in),
            (down_s, 256, DE, RC_DOWN, ag_dn_in),
        ]
        ji = 0
        qdone = []
        gloadb = []
        tpfree = [0, 0]
        stg_ag = [0, 0]
        stg_vec = [0, 0]
        for (ap, rows, cols, rc, dest) in wjobs:
            nets = cols // 128
            for j in range(rows // 128):
                buf = ji % 3
                jj = ji

                def ldq(q, ap=ap, j=j, buf=buf, jj=jj, cols=cols):
                    pool.wait(q, g_scales)
                    if jj >= 3:
                        q.wait_ge(sems["vector"], qdone[jj - 3])
                    ins = q.dma_start(out=wt[buf][:, 0:cols],
                                      in_=ap[j * 128:(j + 1) * 128, :])
                    gloadb.append(pool.issue([ins]))
                on("sync" if jj % 2 == 0 else "scalar", ldq)

                def qn(v, rc=rc, buf=buf, cols=cols, jj=jj):
                    pool.wait(v, gloadb[jj])
                    pool.wait(v, g_scales)
                    v.tensor_scalar(out=wt[buf][:, 0:cols], in0=wt[buf][:, 0:cols],
                                    scalar1=scales_sb[:, rc:rc + 1], scalar2=1.0,
                                    op0=Alu.mult, op1=Alu.min)
                    v.drain()
                    v.tensor_scalar(out=q8t[:, 0:cols], in0=wt[buf][:, 0:cols],
                                    scalar1=-1.0, scalar2=None, op0=Alu.max)
                    v.drain()
                    c = inc("vector", v.tensor_copy(out=qbft[:, 0:cols], in_=q8t[:, 0:cols]))
                    qdone.append(c)
                on("vector", qn)
                c_q = cnt["vector"]

                for g0 in range(0, nets, 8):
                    gN = min(8, nets - g0)
                    pb = (g0 // 8) % 2

                    def tpw(t, g0=g0, gN=gN, pb=pb, c_q=c_q, sb_=ji % 2):
                        t.wait_ge(sems["vector"], c_q)
                        t.wait_ge(sems["vector"], tpfree[pb])
                        t.wait_ge(sems["ag"], stg_ag[sb_])
                        t.wait_ge(sems["vector"], stg_vec[sb_])
                        pool.wait(t, g_const)
                        for k in range(gN):
                            ins = t.transpose(out=ptpw[pb][:, 128 * k:128 * (k + 1)],
                                              in_=qbft[:, 128 * (g0 + k):128 * (g0 + k + 1)],
                                              identity=ident[:])
                            if k == gN - 1:
                                inc("tensor", ins)
                    on("tensor", tpw)
                    c_tp = cnt["tensor"]

                    def evw(v, g0=g0, gN=gN, pb=pb, c_tp=c_tp, sb_=ji % 2):
                        v.wait_ge(sems["tensor"], c_tp)
                        c = inc("vector", v.tensor_copy(out=stageb[sb_][:, 128 * g0:128 * (g0 + gN)],
                                                        in_=ptpw[pb][:, 0:128 * gN]))
                        tpfree[pb] = c
                    on("vector", evw)
                c_stage = cnt["vector"]

                if dest is None:
                    off = qkv_off[id(ap)] + 128 * j

                    def wqkv(v, off=off, nets=nets, c_stage=c_stage, sb_=ji % 2):
                        v.wait_ge(sems["vector"], c_stage)
                        v.drain()
                        dst = wqkvT.ap().rearrange("p (et c) -> p et c", c=768)[:, :, off:off + 128]
                        src = stageb[sb_].ap().rearrange("p (et c) -> p et c", c=128)[:, 0:nets, :]
                        c = inc("vector", v.tensor_copy(out=dst, in_=src))
                        stg_vec[sb_] = c
                    on("vector", wqkv)
                else:
                    def wag(sync, dest=dest, j=j, nets=nets, c_stage=c_stage, sb_=ji % 2):
                        sync.wait_ge(sems["vector"], c_stage)
                        dst = dest.ap().rearrange("(et p) o -> p et o", p=128)[:, :, 128 * j:128 * (j + 1)]
                        src = stageb[sb_].ap().rearrange("p (et c) -> p et c", c=128)[:, 0:nets, :]
                        inc("ag", sync.dma_start(out=dst, in_=src), 16)
                        stg_ag[sb_] = cnt["ag"]
                    on("sync", wag)
                ji += 1
        c_wqkv_rdy = cnt["vector"]
        c_ag_writes = cnt["ag"]

        def cc2(g):
            g.wait_ge(sems["ag"], c_ag_writes)
            for (i_, o_) in ((ag_wo_in, ag_wo_out), (ag_up_in, ag_up_out),
                             (ag_gate_in, ag_gate_out), (ag_dn_in, ag_dn_out)):
                inc("cc", g.collective_compute("AllGather", Alu.bypass, replica_groups=rg,
                                               ins=[i_.ap().opt()], outs=[o_.ap().opt()]))
        on("gpsimd", cc2)
        cc_w = cnt["cc"]
        c_p1_tensor = cnt["tensor"]

        es_p1.close()

        qT_sb = es_a.enter_context(nc.sbuf_tensor("qT_sb", [128, NT * 256], BF16))
        kT_sb = es_a.enter_context(nc.sbuf_tensor("kT_sb", [128, NT * 256], BF16))
        v_sb = es_a.enter_context(nc.sbuf_tensor("v_sb", [128, NT * 256], BF16))
        attnout_sb = es_a.enter_context(nc.sbuf_tensor("attnout_sb", [128, NT * 256], BF16))

        (S1, MEAN, EX2, M2, VAR, SQV, RSTD, MX, MN, T1, T2, AMAX, MAXLN, CC_, SXV,
         INVSX, AQ, BQ) = range(18)

        es_p2 = ExitStack()
        xt0 = es_p2.enter_context(nc.sbuf_tensor("xt0", [128, E], F32))
        xt1 = es_p2.enter_context(nc.sbuf_tensor("xt1", [128, E], F32))
        sqdump = es_p2.enter_context(nc.sbuf_tensor("sqdump", [128, E], F32))
        xq8a = es_p2.enter_context(nc.sbuf_tensor("xq8a", [128, E], I8))
        xq8b = es_p2.enter_context(nc.sbuf_tensor("xq8b", [128, E], I8))
        xqbfa = es_p2.enter_context(nc.sbuf_tensor("xqbfa", [128, E], BF16))
        xqbfb = es_p2.enter_context(nc.sbuf_tensor("xqbfb", [128, E], BF16))
        xq8 = [xq8a, xq8b]
        xqbf = [xqbfa, xqbfb]
        xqT = es_p2.enter_context(nc.sbuf_tensor("xqT", [128, E], BF16))
        qkstage0 = es_p2.enter_context(nc.sbuf_tensor("qkstage0", [128, 512], BF16))
        qkstage1 = es_p2.enter_context(nc.sbuf_tensor("qkstage1", [128, 512], BF16))
        qkstage = [qkstage0, qkstage1]
        ptpx0 = es_p2.enter_context(nc.psum_tensor("ptpx0", [128, 1024], BF16))
        ptpx1 = es_p2.enter_context(nc.psum_tensor("ptpx1", [128, 1024], BF16))
        pq1 = es_p2.enter_context(nc.psum_tensor("pq1", [128, 512], F32))
        pq2 = es_p2.enter_context(nc.psum_tensor("pq2", [128, 256], F32))
        ptpqk = es_p2.enter_context(nc.psum_tensor("ptpqk", [128, 512], BF16))
        xts = [xt0, xt1]
        ptpx = [ptpx0, ptpx1]
        c_tile_done = [0] * NT
        gx = [None] * NT

        c_quant_h = [0] * NT
        c_mmq_h = [0] * NT
        c_evq_h = [0] * NT
        c_tqk_h = [0] * NT
        c_evqk_h = [0] * NT
        c_xqT_h = [0] * NT

        def p2_pre(tt):
            buf = tt % 2
            po = 20 * (tt % 2)

            def ldx(sync, tt=tt, buf=buf):
                if tt >= 2:
                    sync.wait_ge(sems["scalar"], c_tile_done[tt - 2])
                    sync.wait_ge(sems["vector"], c_quant_h[tt - 2] if tt >= 2 else 0)
                else:
                    sync.wait_ge(sems["vector"], c_wqkv_rdy)
                    sync.wait_ge(sems["tensor"], c_p1_tensor)
                gx[tt] = pool.issue([sync.dma_start(out=xts[buf][:],
                                                    in_=x_full[tt * 128:(tt + 1) * 128, :])])
            on("sync", ldx)

            def st_v(v, tt=tt, buf=buf, po=po):
                pool.wait(v, gx[tt])
                v.tensor_reduce(out=sm[:, po + S1:po + S1 + 1], in_=xts[buf][:],
                                axis=AX.X, op=Alu.add)
                v.tensor_reduce(out=sm[:, po + MX:po + MX + 1], in_=xts[buf][:],
                                axis=AX.X, op=Alu.max)
                inc("vector", v.tensor_reduce(out=sm[:, po + MN:po + MN + 1], in_=xts[buf][:],
                                              axis=AX.X, op=Alu.min))
            on("vector", st_v)
            c_sum = cnt["vector"]

            def st_s(s, buf=buf, po=po, c_sum=c_sum):
                s.wait_ge(sems["vector"], c_sum)
                sinc(s.activation(out=sqdump[:], in_=xts[buf][:], func=AF.Square,
                                  accum_out=sm[:, po + EX2:po + EX2 + 1]))
            c_sq = sq(st_s)

            def sml_v(v, po=po, c_sq=c_sq):
                v.wait_ge(sems["scalar"], c_sq)
                v.tensor_scalar(out=sm[:, po + MEAN:po + MEAN + 1], in0=sm[:, po + S1:po + S1 + 1],
                                scalar1=1.0 / E, scalar2=None, op0=Alu.mult)
                v.tensor_scalar(out=sm[:, po + EX2:po + EX2 + 1], in0=sm[:, po + EX2:po + EX2 + 1],
                                scalar1=1.0 / E, scalar2=None, op0=Alu.mult)
                v.drain()
                v.tensor_tensor(out=sm[:, po + M2:po + M2 + 1], in0=sm[:, po + MEAN:po + MEAN + 1],
                                in1=sm[:, po + MEAN:po + MEAN + 1], op=Alu.mult)
                v.drain()
                inc("vector", v.tensor_tensor(out=sm[:, po + VAR:po + VAR + 1],
                                              in0=sm[:, po + EX2:po + EX2 + 1],
                                              in1=sm[:, po + M2:po + M2 + 1], op=Alu.subtract))
            on("vector", sml_v)
            c_var = cnt["vector"]

            def sml_s(s, po=po, c_var=c_var):
                s.wait_ge(sems["vector"], c_var)
                s.activation(out=sm[:, po + SQV:po + SQV + 1],
                             in_=sm[:, po + VAR:po + VAR + 1],
                             func=AF.Ln, bias=eps_t[:, 0:1], scale=1.0)
                s.drain()
                sinc(s.activation(out=sm[:, po + RSTD:po + RSTD + 1],
                                  in_=sm[:, po + SQV:po + SQV + 1],
                                  func=AF.Exp, scale=neghalf_t[:, 0:1]))
            c_sqv = sq(sml_s)

            def qv(v, buf=buf, po=po, tt=tt, c_sqv=c_sqv):
                v.wait_ge(sems["scalar"], c_sqv)
                P = lambda k: sm[:, po + k:po + k + 1]
                v.tensor_tensor(out=P(T1), in0=P(MX), in1=P(MEAN), op=Alu.subtract)
                v.tensor_tensor(out=P(T2), in0=P(MEAN), in1=P(MN), op=Alu.subtract)
                v.drain()
                v.tensor_tensor(out=P(AMAX), in0=P(T1), in1=P(T2), op=Alu.max)
                v.drain()
                v.tensor_tensor(out=P(MAXLN), in0=P(AMAX), in1=P(RSTD), op=Alu.mult)
                v.drain()
                v.tensor_scalar(out=P(CC_), in0=P(MAXLN), scalar1=1.0e-5, scalar2=None, op0=Alu.max)
                v.drain()
                v.reciprocal(out=P(SXV), in_=P(CC_))
                v.drain()
                v.tensor_scalar(out=P(SXV), in0=P(SXV), scalar1=127.0, scalar2=None, op0=Alu.mult)
                v.tensor_scalar(out=P(INVSX), in0=P(CC_), scalar1=1.0 / 127.0, scalar2=None, op0=Alu.mult)
                v.drain()
                v.tensor_tensor(out=P(AQ), in0=P(RSTD), in1=P(SXV), op=Alu.mult)
                v.drain()
                v.tensor_scalar(out=P(BQ), in0=P(MEAN), scalar1=P(AQ), scalar2=-1.0,
                                op0=Alu.mult, op1=Alu.mult)
                v.drain()
                v.tensor_scalar(out=xq8[buf][:], in0=xts[buf][:], scalar1=P(AQ), scalar2=P(BQ),
                                op0=Alu.mult, op1=Alu.add)
                v.drain()
                inc("vector", v.tensor_copy(out=xqbf[buf][:], in_=xq8[buf][:]))
            on("vector", qv)
            c_quant_h[tt] = cnt["vector"]

            if DEBUG and tt == 0:
                def dxq(v):
                    v.tensor_copy(out=snap["xq"][:], in_=xq8[0][:])
                    inc("vector", v.tensor_copy(out=snap["sm"][:], in_=sm[:]))
                on("vector", dxq)
                c_quant_h[tt] = cnt["vector"]

        def p2_pe(tt):
            buf = tt % 2
            po = 20 * (tt % 2)
            c_quant = c_quant_h[tt]
            c_xqT = 0
            for g0 in (0, 8):
                pb = g0 // 8

                def tpx(t, g0=g0, pb=pb, tt=tt, buf=buf, c_quant=c_quant):
                    t.wait_ge(sems["vector"], c_quant)
                    if tt > 0:
                        t.wait_ge(sems["scalar"], c_xqT_h[tt - 1])
                    pool.wait(t, g_const)
                    for k in range(8):
                        ins = t.transpose(out=ptpx[pb][:, 128 * k:128 * (k + 1)],
                                          in_=xqbf[buf][:, 128 * (g0 + k):128 * (g0 + k + 1)],
                                          identity=ident[:])
                        if k == 7:
                            inc("tensor", ins)
                on("tensor", tpx)
                c_tpx = cnt["tensor"]

                def evx(s, g0=g0, pb=pb, c_tpx=c_tpx, tt=tt):
                    s.wait_ge(sems["tensor"], c_tpx)
                    if tt > 0:
                        s.wait_ge(sems["tensor"], c_mmq_h[tt - 1])
                    sinc(s.copy(out=xqT[:, 128 * g0:128 * (g0 + 8)], in_=ptpx[pb][:]))
                c_xqT = sq(evx)
                c_xqT_h[tt] = c_xqT

            def mmq(t, c_xqT=c_xqT, tt=tt):
                t.wait_ge(sems["scalar"], c_xqT)
                t.wait_ge(sems["vector"], c_wqkv_rdy)
                if tt > 0:
                    t.wait_ge(sems["vector"], c_evq_h[tt - 1])
                for et in range(ET):
                    t.matmul(out=pq1[:], lhsT=xqT[:, 128 * et:128 * (et + 1)],
                             rhs=wqkvT[:, 768 * et:768 * et + 512],
                             start=(et == 0), stop=(et == ET - 1))
                for et in range(ET):
                    ins = t.matmul(out=pq2[:], lhsT=xqT[:, 128 * et:128 * (et + 1)],
                                   rhs=wqkvT[:, 768 * et + 512:768 * (et + 1)],
                                   start=(et == 0), stop=(et == ET - 1))
                    if et == ET - 1:
                        inc("tensor", ins)
            on("tensor", mmq)
            c_mmq_h[tt] = cnt["tensor"]
            c_mmq = cnt["tensor"]

            def evq(v, tt=tt, po=po, c_mmq=c_mmq, buf=buf):
                v.wait_ge(sems["tensor"], c_mmq)
                if tt > 1:
                    v.wait_ge(sems["tensor"], c_tqk_h[tt - 2])
                P = lambda k: sm[:, po + k:po + k + 1]
                v.tensor_scalar(out=qkstage[buf][:, 0:256], in0=pq1[:, 0:256],
                                scalar1=P(INVSX), scalar2=scales_sb[:, SC_Q:SC_Q + 1],
                                op0=Alu.mult, op1=Alu.mult)
                v.tensor_scalar(out=qkstage[buf][:, 256:512], in0=pq1[:, 256:512],
                                scalar1=P(INVSX), scalar2=scales_sb[:, SC_K:SC_K + 1],
                                op0=Alu.mult, op1=Alu.mult)
                inc("vector", v.tensor_scalar(out=v_sb[:, 256 * tt:256 * (tt + 1)],
                                              in0=pq2[:], scalar1=P(INVSX),
                                              scalar2=scales_sb[:, SC_V:SC_V + 1],
                                              op0=Alu.mult, op1=Alu.mult))
            on("vector", evq)
            c_evq_h[tt] = cnt["vector"]
            c_evq = cnt["vector"]


        def p2_pe_b(tt):
            buf = tt % 2

            def tqk(t, tt=tt, buf=buf):
                t.wait_ge(sems["vector"], c_evq_h[tt])
                if tt > 0:
                    t.wait_ge(sems["scalar"], c_evqk_h[tt - 1])
                for k in range(4):
                    ins = t.transpose(out=ptpqk[:, 128 * k:128 * (k + 1)],
                                      in_=qkstage[buf][:, 128 * k:128 * (k + 1)],
                                      identity=ident[:])
                    if k == 3:
                        inc("tensor", ins)
            on("tensor", tqk)
            c_tqk_h[tt] = cnt["tensor"]
            c_tqk = cnt["tensor"]

            def evqk(s, tt=tt, c_tqk=c_tqk):
                s.wait_ge(sems["tensor"], c_tqk)
                s.copy(out=qT_sb[:, 256 * tt:256 * (tt + 1)], in_=ptpqk[:, 0:256])
                sinc(s.copy(out=kT_sb[:, 256 * tt:256 * (tt + 1)], in_=ptpqk[:, 256:512]))
            c = sq(evqk)
            c_evqk_h[tt] = c
            c_tile_done[tt] = c

        for step in range(NT + 2):
            if step < NT:
                p2_pre(step)
            if step >= 1 and step - 1 < NT:
                p2_pe(step - 1)
            if step >= 2:
                p2_pe_b(step - 2)

        sflush()
        es_p2.close()

        es_p3 = ExitStack()
        attnT0 = es_p3.enter_context(nc.sbuf_tensor("attnT0", [128, 512], BF16))
        attnT1 = es_p3.enter_context(nc.sbuf_tensor("attnT1", [128, 512], BF16))
        attnT2 = es_p3.enter_context(nc.sbuf_tensor("attnT2", [128, 512], BF16))
        scm0 = es_p3.enter_context(nc.sbuf_tensor("scm0", [128, 512], F32))
        scm1 = es_p3.enter_context(nc.sbuf_tensor("scm1", [128, 512], F32))
        scm2 = es_p3.enter_context(nc.sbuf_tensor("scm2", [128, 512], F32))
        outT = es_p3.enter_context(nc.sbuf_tensor("outT", [128, 4096], BF16))
        sumsrow = es_p3.enter_context(nc.sbuf_tensor("sumsrow", [1, S], F32))
        recips = es_p3.enter_context(nc.sbuf_tensor("recips", [128, NTB], F32))
        sumsbf = es_p3.enter_context(nc.sbuf_tensor("sumsbf", [1, S], BF16))
        sums32 = es_p3.enter_context(nc.sbuf_tensor("sums32", [32, 128], BF16))
        psc0 = es_p3.enter_context(nc.psum_tensor("psc0", [128, 512], F32))
        psc1 = es_p3.enter_context(nc.psum_tensor("psc1", [128, 512], F32))
        psc2 = es_p3.enter_context(nc.psum_tensor("psc2", [128, 512], F32))
        pout = es_p3.enter_context(nc.psum_tensor("pout", [128, 512], F32))
        psum1 = es_p3.enter_context(nc.psum_tensor("psum1", [1, 512], F32))
        ptpo = es_p3.enter_context(nc.psum_tensor("ptpo", [128, 512], BF16))
        ptps = es_p3.enter_context(nc.psum_tensor("ptps", [128, 32], BF16))
        psc = [psc0, psc1, psc2]
        c_qkv_all = cnt["vector"]
        g_sums_prev = [None]

        attnTb = [attnT0, attnT1, attnT2]
        scmb = [scm0, scm1, scm2]
        for bh in range(4):
            b, ds = bh // 2, bh % 2
            tiles = [(sc, tl) for sc in range(8) for tl in range(4 * sc + 4)]
            NTI = len(tiles)
            c_exp_h = [0] * NTI
            c_ov_h = [0] * NTI
            psc_cons = [None, None, None]
            c_ev_sc_map = {}
            c_ev_last = 0

            def emit_mm_sc(idx):
                sc, tl = tiles[idx]
                tg = NTB * b + tl
                pb = idx % 3

                def mm_sc(t, tg=tg, pb=pb, sc=sc, cons=psc_cons[pb]):
                    t.wait_ge(sems["vector"], c_qkv_all)
                    if cons is not None:
                        t.wait_ge(sems[cons[0]], cons[1])
                    qv_ = qT_sb.ap().rearrange("p (t c) -> p t c", c=256)
                    rhs = qv_[:, NTB * b + 4 * sc:NTB * b + 4 * sc + 4,
                              128 * ds:128 * (ds + 1)]
                    inc("tensor", t.matmul(
                        out=psc[pb][:],
                        lhsT=kT_sb[:, 256 * tg + 128 * ds:256 * tg + 128 * (ds + 1)],
                        rhs=rhs, start=True, stop=True))
                on("tensor", mm_sc)
                return cnt["tensor"]

            def emit_exp(idx, c_mmsc):
                sc, tl = tiles[idx]
                pb = idx % 3
                diag = tl >= 4 * sc
                c_at_free = c_ov_h[idx - 3] if idx >= 3 else 0
                if diag:
                    kk = tl - 4 * sc

                    def mskv(v, kk=kk, pb=pb, c_mmsc=c_mmsc):
                        v.wait_ge(sems["tensor"], c_mmsc)
                        if c_exp_h[idx - 3] if idx >= 3 else 0:
                            v.wait_ge(sems["scalar"], c_exp_h[idx - 3])
                        inc("vector", v.tensor_tensor(
                            out=scmb[pb][:], in0=psc[pb][:],
                            in1=cmask[:, 384 - 128 * kk:896 - 128 * kk], op=Alu.add))
                    on("vector", mskv)
                    c_mv = cnt["vector"]
                    psc_cons[pb] = ("vector", c_mv)

                    def expd(s, pb=pb, c_mv=c_mv, c_at_free=c_at_free):
                        s.wait_ge(sems["vector"], c_mv)
                        if c_at_free:
                            s.wait_ge(sems["tensor"], c_at_free)
                        sinc(s.activation(out=attnTb[pb][:], in_=scmb[pb][:], func=AF.Exp))
                    sq(expd)
                else:
                    def expn(s, pb=pb, c_mmsc=c_mmsc, c_at_free=c_at_free):
                        s.wait_ge(sems["tensor"], c_mmsc)
                        if c_at_free:
                            s.wait_ge(sems["tensor"], c_at_free)
                        sinc(s.activation(out=attnTb[pb][:], in_=psc[pb][:], func=AF.Exp))
                    sq(expn)
                    psc_cons[pb] = ("scalar", cnt["scalar"])
                c_exp_h[idx] = cnt["scalar"]

            def emit_mm_ov(idx):
                sc, tl = tiles[idx]
                tg = NTB * b + tl
                pb = idx % 3
                first = tl == 0
                last = tl == 4 * sc + 3
                c_exp = c_exp_h[idx]
                c_prev_ev = c_ev_sc_map.get(sc - 1, 0)

                def mm_ov(t, tg=tg, pb=pb, first=first, last=last,
                          c_exp=c_exp, c_prev_ev=c_prev_ev):
                    t.wait_ge(sems["scalar"], c_exp)
                    if first and c_prev_ev:
                        t.wait_ge(sems["vector"], c_prev_ev)
                    t.matmul(out=pout[:],
                             lhsT=v_sb[:, 256 * tg + 128 * ds:256 * tg + 128 * (ds + 1)],
                             rhs=attnTb[pb][:], start=first, stop=last)
                    inc("tensor", t.matmul(out=psum1[:], lhsT=ones_bf[:], rhs=attnTb[pb][:],
                                           start=first, stop=last))
                on("tensor", mm_ov)
                c_ov_h[idx] = cnt["tensor"]
                if last:
                    def ev_sc(v, sc=sc, c_last_mm=cnt["tensor"]):
                        v.wait_ge(sems["tensor"], c_last_mm)
                        if sc == 0 and g_sums_prev[0] is not None:
                            pool.wait(v, g_sums_prev[0])
                        v.tensor_copy(out=outT[:, 512 * sc:512 * (sc + 1)], in_=pout[:])
                        inc("vector", v.tensor_copy(out=sumsrow[0:1, 512 * sc:512 * (sc + 1)],
                                                    in_=psum1[0:1, :]))
                    on("vector", ev_sc)
                    c_ev_sc_map[sc] = cnt["vector"]

            for idx in range(NTI):
                c_mmsc = emit_mm_sc(idx)
                emit_exp(idx, c_mmsc)
                if idx >= 1:
                    emit_mm_ov(idx - 1)
            emit_mm_ov(NTI - 1)
            sflush()
            c_ev_sc = c_ev_sc_map[7]

            def sums_bf(v, bh=bh):
                v.drain()
                inc("vector", v.tensor_copy(out=sumsbf[0:1, :], in_=sumsrow[0:1, :]))
            on("vector", sums_bf)
            c_sbf = cnt["vector"]

            g_sums_h = [None]

            def sums_mv(sync, c_sbf=c_sbf):
                sync.wait_ge(sems["vector"], c_sbf)
                g1 = pool.issue([sync.dma_start(out=sums_bounce, in_=sumsbf[0:1, :])])
                pool.wait(sync, g1)
                src = sums_bounce.rearrange("x (t p) -> (x t) p", t=32)
                g_sums_h[0] = pool.issue([sync.dma_start(out=sums32[:], in_=src)])
            on("sync", sums_mv)
            g_sums = g_sums_h[0]
            g_sums_prev[0] = g_sums

            def sums_tp(t, g_sums=g_sums):
                pool.wait(t, g_sums)
                inc("tensor", t.transpose(out=ptps[:], in_=sums32[:], identity=ident[0:32, 0:32]))
            on("tensor", sums_tp)
            c_stp = cnt["tensor"]

            def recv(v, c_stp=c_stp, bh=bh):
                v.wait_ge(sems["tensor"], c_stp)
                v.tensor_copy(out=recips[:], in_=ptps[:])
                v.drain()
                inc("vector", v.reciprocal(out=recips[:], in_=recips[:]))
                v.drain()
                if DEBUG and bh == 0:
                    inc("vector", v.tensor_copy(out=snap["rec"][:], in_=recips[:]))
            on("vector", recv)
            c_rec = cnt["vector"]

            c_ev_tp = 0
            for sc in range(8):
                def tpo(t, sc=sc, c_rec=c_rec, c_ev_tp=c_ev_tp):
                    t.wait_ge(sems["vector"], c_rec)
                    if c_ev_tp:
                        t.wait_ge(sems["vector"], c_ev_tp)
                    for j in range(4):
                        ins = t.transpose(out=ptpo[:, 128 * j:128 * (j + 1)],
                                          in_=outT[:, 512 * sc + 128 * j:512 * sc + 128 * (j + 1)],
                                          identity=ident[:])
                        if j == 3:
                            inc("tensor", ins)
                on("tensor", tpo)
                c_tpo = cnt["tensor"]

                def evtp(v, sc=sc, b=b, ds=ds, c_tpo=c_tpo):
                    v.wait_ge(sems["tensor"], c_tpo)
                    for j in range(4):
                        st = 4 * sc + j
                        ins = v.tensor_scalar(
                            out=attnout_sb[:, 256 * (NTB * b + st) + 128 * ds:
                                           256 * (NTB * b + st) + 128 * (ds + 1)],
                            in0=ptpo[:, 128 * j:128 * (j + 1)],
                            scalar1=recips[:, st:st + 1], scalar2=None, op0=Alu.mult)
                        if j == 3:
                            inc("vector", ins)
                on("vector", evtp)
                c_ev_tp = cnt["vector"]

        if DEBUG:
            def dato(v):
                v.wait_ge(sems["vector"], cnt["vector"])
                inc("vector", v.tensor_copy(out=snap["ao"][:], in_=attnout_sb[:, 0:256]))
            on("vector", dato)

        def at_out(sync):
            sync.wait_ge(sems["vector"], cnt["vector"])
            dst = a2a_in.ap().rearrange("(t p) f -> p t f", p=128)
            src_ = attnout_sb.ap().rearrange("p (t f) -> p t f", f=256)
            inc("a2aw", sync.dma_start(out=dst, in_=src_), 16)
        on("sync", at_out)
        c_a2a_in = cnt["a2aw"]

        def cc3(g):
            g.wait_ge(sems["a2aw"], c_a2a_in)
            with allow_ici_all_to_all():
                inc("cc", g.collective_compute("AllToAll", Alu.bypass, replica_groups=rg,
                                               ins=[a2a_in.ap().opt()], outs=[a2a_out.ap().opt()]))
        on("gpsimd", cc3)
        cc_at = cnt["cc"]
        c_p3_v = cnt["vector"]
        c_p3_t = cnt["tensor"]
        c_p3_s = cnt["scalar"]

        es_p3.close()
        es_a.close()

        OS1, OMEAN, OEX2, OM2, OVAR, OSQV, ORSTD, OCC, OSX, OINVSX, OBT = range(11)

        es_p4 = ExitStack()
        woT_sb = es_p4.enter_context(nc.sbuf_tensor("woT_sb", [128, ET * 2048], BF16))
        oin0 = es_p4.enter_context(nc.sbuf_tensor("oin0", [128, E], BF16))
        oin1 = es_p4.enter_context(nc.sbuf_tensor("oin1", [128, E], BF16))
        oinb = [oin0, oin1]
        xres0 = es_p4.enter_context(nc.sbuf_tensor("xres0", [128, E], F32))
        xres1 = es_p4.enter_context(nc.sbuf_tensor("xres1", [128, E], F32))
        xresb = [xres0, xres1]
        oq8a = es_p4.enter_context(nc.sbuf_tensor("oq8a", [128, E], I8))
        oq8b = es_p4.enter_context(nc.sbuf_tensor("oq8b", [128, E], I8))
        oq8b_ = [oq8a, oq8b]
        oqbfa = es_p4.enter_context(nc.sbuf_tensor("oqbfa", [128, E], BF16))
        oqbfb = es_p4.enter_context(nc.sbuf_tensor("oqbfb", [128, E], BF16))
        oqbfb_ = [oqbfa, oqbfb]
        oqT = es_p4.enter_context(nc.sbuf_tensor("oqT", [128, E], BF16))
        oproj0 = es_p4.enter_context(nc.sbuf_tensor("oproj0", [128, E], F32))
        oproj1 = es_p4.enter_context(nc.sbuf_tensor("oproj1", [128, E], F32))
        oprojb = [oproj0, oproj1]
        osq = es_p4.enter_context(nc.sbuf_tensor("osq", [128, E], F32))
        lnres0 = es_p4.enter_context(nc.sbuf_tensor("lnres0", [128, E], F32))
        lnres1 = es_p4.enter_context(nc.sbuf_tensor("lnres1", [128, E], F32))
        lnresb = [lnres0, lnres1]
        ptpo2a = es_p4.enter_context(nc.psum_tensor("ptpo2a", [128, 1024], BF16))
        ptpo2b = es_p4.enter_context(nc.psum_tensor("ptpo2b", [128, 1024], BF16))
        po0 = es_p4.enter_context(nc.psum_tensor("po0", [128, 512], F32))
        po1 = es_p4.enter_context(nc.psum_tensor("po1", [128, 512], F32))
        po2 = es_p4.enter_context(nc.psum_tensor("po2", [128, 512], F32))
        po3 = es_p4.enter_context(nc.psum_tensor("po3", [128, 512], F32))
        pob = [po0, po1, po2, po3]
        ptpo2 = [ptpo2a, ptpo2b]

        g_wo_h = [None]

        def ldwo(sync):
            sync.wait_ge(sems["cc"], cc_w)
            sync.wait_ge(sems["vector"], c_p3_v)
            sync.wait_ge(sems["tensor"], c_p3_t)
            sync.wait_ge(sems["scalar"], c_p3_s)
            ins = []
            for r in range(NC):
                src = ag_wo_out.ap()[E * r:E * (r + 1), :].rearrange("(et p) o -> p et o", p=128)
                dst = woT_sb.ap().rearrange("p (et c) -> p et c", c=2048)[:, :, 256 * r:256 * (r + 1)]
                ins.append(sync.dma_start(out=dst, in_=src))
            g_wo_h[0] = pool.issue(ins)
        on("sync", ldwo)
        g_wo = g_wo_h[0]

        NJ = TPC // 128
        c_oq_h = [0] * NJ
        c_evoq_h = [0] * NJ
        c_evo_h = [[0] * 4 for _ in range(NJ)]
        c_x1v_h = [0] * NJ

        def p4_pre(j):
            ob = j % 2
            OP = 11 * (j % 2)
            g_oin_h = [None]

            def ldoin(sync, j=j, ob=ob):
                sync.wait_ge(sems["cc"], cc_at)
                if j >= 2:
                    sync.wait_ge(sems["vector"], c_oq_h[j - 2])
                ins = []
                for r in range(NC):
                    ins.append(sync.dma_start(out=oinb[ob][:, 256 * r:256 * (r + 1)],
                                              in_=a2a_out.ap()[TPC * r + 128 * j:
                                                               TPC * r + 128 * (j + 1), :]))
                g_oin_h[0] = pool.issue(ins)
            on("sync", ldoin)
            g_oin = g_oin_h[0]

            def oquant(v, g_oin=g_oin, j=j, ob=ob, OP=OP):
                pool.wait(v, g_oin)
                oin = oinb[ob]
                if DEBUG and j == 0:
                    v.tensor_copy(out=snap["oin"][:], in_=oin[:, 0:512])
                P = lambda k: sm[:, OP + k:OP + k + 1]
                v.tensor_reduce(out=P(OCC), in_=oin[:], axis=AX.X, op=Alu.max,
                                apply_absolute_value=True)
                v.drain()
                v.tensor_scalar(out=P(OCC), in0=P(OCC), scalar1=1.0e-5, scalar2=None, op0=Alu.max)
                v.drain()
                v.reciprocal(out=P(OSX), in_=P(OCC))
                v.drain()
                v.tensor_scalar(out=P(OSX), in0=P(OSX), scalar1=127.0, scalar2=None, op0=Alu.mult)
                v.tensor_scalar(out=P(OINVSX), in0=P(OCC), scalar1=1.0 / 127.0,
                                scalar2=None, op0=Alu.mult)
                v.drain()
                v.tensor_scalar(out=oq8b_[ob][:], in0=oin[:], scalar1=P(OSX),
                                scalar2=None, op0=Alu.mult)
                v.drain()
                inc("vector", v.tensor_copy(out=oqbfb_[ob][:], in_=oq8b_[ob][:]))
            on("vector", oquant)
            c_oq_h[j] = cnt["vector"]

        def p4_mid(j):
            ob = j % 2
            c_oq = c_oq_h[j]
            for g0 in (0, 8):
                pb = g0 // 8

                def tpoq(t, g0=g0, pb=pb, c_oq=c_oq, j=j, ob=ob):
                    t.wait_ge(sems["vector"], c_oq)
                    if j > 0:
                        t.wait_ge(sems["scalar"], c_evoq_h[j - 1])
                    pool.wait(t, g_const)
                    for k in range(8):
                        ins = t.transpose(out=ptpo2[pb][:, 128 * k:128 * (k + 1)],
                                          in_=oqbfb_[ob][:, 128 * (g0 + k):128 * (g0 + k + 1)],
                                          identity=ident[:])
                        if k == 7:
                            inc("tensor", ins)
                on("tensor", tpoq)
                c_t = cnt["tensor"]

                def evoq(s, g0=g0, pb=pb, c_t=c_t):
                    s.wait_ge(sems["tensor"], c_t)
                    sinc(s.copy(out=oqT[:, 128 * g0:128 * (g0 + 8)], in_=ptpo2[pb][:]))
                c_evoq_h[j] = sq(evoq)
            c_evoq = c_evoq_h[j]

            for ch in range(4):
                def mm_o(t, ch=ch, j=j, c_evoq=c_evoq):
                    t.wait_ge(sems["scalar"], c_evoq)
                    pool.wait(t, g_wo)
                    if j > 0:
                        t.wait_ge(sems["vector"], c_evo_h[j - 1][ch])
                    for et in range(ET):
                        ins = t.matmul(out=pob[ch][:], lhsT=oqT[:, 128 * et:128 * (et + 1)],
                                       rhs=woT_sb[:, 2048 * et + 512 * ch:2048 * et + 512 * (ch + 1)],
                                       start=(et == 0), stop=(et == ET - 1))
                        if et == ET - 1:
                            inc("tensor", ins)
                on("tensor", mm_o)
                c_mo = cnt["tensor"]

                def ev_o(v, ch=ch, j=j, c_mo=c_mo, OP=11 * (j % 2)):
                    v.wait_ge(sems["tensor"], c_mo)
                    inc("vector", v.tensor_scalar(out=oprojb[j % 2][:, 512 * ch:512 * (ch + 1)],
                                                  in0=pob[ch][:],
                                                  scalar1=sm[:, OP + OINVSX:OP + OINVSX + 1],
                                                  scalar2=scales_sb[:, SC_O:SC_O + 1],
                                                  op0=Alu.mult, op1=Alu.mult))
                on("vector", ev_o)
                c_evo_h[j][ch] = cnt["vector"]

        def p4_post(j):
            ob = j % 2
            OP = 11 * (j % 2)
            oproj = oprojb[ob]
            c_op = c_evo_h[j][3]
            g_xr_h = [None]

            def ldxr(sync, j=j, ob=ob):
                if j >= 2:
                    sync.wait_ge(sems["vector"], c_x1v_h[j - 2])
                g_xr_h[0] = pool.issue([sync.dma_start(out=xresb[ob][:],
                                                       in_=x_mine[128 * j:128 * (j + 1), :])])
            on("sync", ldxr)
            g_xr = g_xr_h[0]

            def oln_v(v, c_op=c_op, OP=OP, oproj=oproj):
                v.wait_ge(sems["vector"], c_op)
                inc("vector", v.tensor_reduce(out=sm[:, OP + OS1:OP + OS1 + 1], in_=oproj[:],
                                              axis=AX.X, op=Alu.add))
            on("vector", oln_v)
            c_os1 = cnt["vector"]

            def oln_s(s, c_os1=c_os1, OP=OP, oproj=oproj):
                s.wait_ge(sems["vector"], c_os1)
                sinc(s.activation(out=osq[:], in_=oproj[:], func=AF.Square,
                                  accum_out=sm[:, OP + OEX2:OP + OEX2 + 1]))
            c_osq = sq(oln_s)

            def oln_v2(v, c_osq=c_osq, OP=OP):
                v.wait_ge(sems["scalar"], c_osq)
                P = lambda k: sm[:, OP + k:OP + k + 1]
                v.tensor_scalar(out=P(OMEAN), in0=P(OS1), scalar1=1.0 / E, scalar2=None, op0=Alu.mult)
                v.tensor_scalar(out=P(OEX2), in0=P(OEX2), scalar1=1.0 / E, scalar2=None, op0=Alu.mult)
                v.drain()
                v.tensor_tensor(out=P(OM2), in0=P(OMEAN), in1=P(OMEAN), op=Alu.mult)
                v.drain()
                inc("vector", v.tensor_tensor(out=P(OVAR), in0=P(OEX2), in1=P(OM2), op=Alu.subtract))
            on("vector", oln_v2)
            c_ov = cnt["vector"]

            def oln_s2(s, c_ov=c_ov, OP=OP):
                s.wait_ge(sems["vector"], c_ov)
                s.activation(out=sm[:, OP + OSQV:OP + OSQV + 1], in_=sm[:, OP + OVAR:OP + OVAR + 1],
                             func=AF.Ln, bias=eps_t[:, 0:1], scale=1.0)
                s.drain()
                sinc(s.activation(out=sm[:, OP + ORSTD:OP + ORSTD + 1],
                                  in_=sm[:, OP + OSQV:OP + OSQV + 1],
                                  func=AF.Exp, scale=neghalf_t[:, 0:1]))
            c_osqv = sq(oln_s2)

            def oln_v3(v, j=j, c_osqv=c_osqv, ob=ob, OP=OP, oproj=oproj, g_xr=g_xr,
                       c_x1w=cnt["x1w"]):
                v.wait_ge(sems["scalar"], c_osqv)
                v.wait_ge(sems["x1w"], c_x1w)
                pool.wait(v, g_xr)
                P = lambda k: sm[:, OP + k:OP + k + 1]
                v.tensor_scalar(out=P(OBT), in0=P(OMEAN), scalar1=P(ORSTD),
                                scalar2=-1.0, op0=Alu.mult, op1=Alu.mult)
                v.drain()
                v.tensor_scalar(out=lnresb[ob][:], in0=oproj[:],
                                scalar1=P(ORSTD), scalar2=P(OBT),
                                op0=Alu.mult, op1=Alu.add)
                v.drain()
                inc("vector", v.tensor_tensor(out=lnresb[ob][:], in0=lnresb[ob][:],
                                              in1=xresb[ob][:], op=Alu.add))
                if DEBUG and j == 0:
                    inc("vector", v.tensor_copy(out=snap["x1"][:], in_=lnresb[0][:, 0:1024]))
            on("vector", oln_v3)
            c_x1 = cnt["vector"]
            c_x1v_h[j] = c_x1

            def stx1(sync, j=j, c_x1=c_x1, ob=ob):
                sync.wait_ge(sems["vector"], c_x1)
                inc("x1w", sync.dma_start(out=x1_hbm[128 * j:128 * (j + 1), :], in_=lnresb[ob][:]), 16)
            on("sync", stx1)

        for step in range(NJ + 2):
            if step < NJ:
                p4_pre(step)
            if 1 <= step <= NJ:
                p4_mid(step - 1)
            if step >= 2:
                p4_post(step - 2)
        sflush()
        c_x1_all = cnt["x1w"]
        c_p4_v = cnt["vector"]
        c_p4_t = cnt["tensor"]
        c_p4_s = cnt["scalar"]
        es_p4.close()

        FS1, FMEAN, FEX2, FM2, FVAR, FSQV, FRSTD, FBT = range(31, 39)

        es_p5 = ExitStack()
        x1t = es_p5.enter_context(nc.sbuf_tensor("x1t", [128, E], F32))
        fsq = es_p5.enter_context(nc.sbuf_tensor("fsq", [128, E], F32))
        hin = es_p5.enter_context(nc.sbuf_tensor("hin", [128, E], BF16))
        hinT = es_p5.enter_context(nc.sbuf_tensor("hinT", [128, ET * 512], BF16))
        wug0 = es_p5.enter_context(nc.sbuf_tensor("wug0", [128, 2 * 2048], BF16))
        wug1 = es_p5.enter_context(nc.sbuf_tensor("wug1", [128, 2 * 2048], BF16))
        silu0 = es_p5.enter_context(nc.sbuf_tensor("silu0", [128, 512], F32))
        silu1 = es_p5.enter_context(nc.sbuf_tensor("silu1", [128, 512], F32))
        silub = [silu0, silu1]
        hT = es_p5.enter_context(nc.sbuf_tensor("hT", [128, 32 * 512], BF16))
        wdn0 = es_p5.enter_context(nc.sbuf_tensor("wdn0", [128, 32 * 512], BF16))
        wdn1 = es_p5.enter_context(nc.sbuf_tensor("wdn1", [128, 32 * 512], BF16))
        y0 = es_p5.enter_context(nc.sbuf_tensor("y0", [128, E], F32))
        y1 = es_p5.enter_context(nc.sbuf_tensor("y1", [128, E], F32))
        y2 = es_p5.enter_context(nc.sbuf_tensor("y2", [128, E], F32))
        y3 = es_p5.enter_context(nc.sbuf_tensor("y3", [128, E], F32))
        ptpf = es_p5.enter_context(nc.psum_tensor("ptpf", [128, 1024], BF16))
        pupa = es_p5.enter_context(nc.psum_tensor("pupa", [128, 512], F32))
        pupb = es_p5.enter_context(nc.psum_tensor("pupb", [128, 512], F32))
        pgta = es_p5.enter_context(nc.psum_tensor("pgta", [128, 512], F32))
        pgtb = es_p5.enter_context(nc.psum_tensor("pgtb", [128, 512], F32))
        pup = [pupa, pupb]
        pgt = [pgta, pgtb]
        pd0 = es_p5.enter_context(nc.psum_tensor("pd0", [128, 512], F32))
        pd1 = es_p5.enter_context(nc.psum_tensor("pd1", [128, 512], F32))
        wug = [wug0, wug1]
        wdn = [wdn0, wdn1]
        ysb = [y0, y1, y2, y3]
        pd = [pd0, pd1]

        for grp in range(2):
            c_hinT = 0
            for j2 in range(4):
                j = 4 * grp + j2
                g_x1_h = [None]

                def ldx1(sync, j=j, j2=j2, grp=grp):
                    sync.wait_ge(sems["x1w"], c_x1_all)
                    if grp == 0 and j2 == 0:
                        sync.wait_ge(sems["vector"], c_p4_v)
                        sync.wait_ge(sems["tensor"], c_p4_t)
                        sync.wait_ge(sems["scalar"], c_p4_s)
                    else:
                        sync.wait_ge(sems["vector"], cnt["vector"])
                    g_x1_h[0] = pool.issue([sync.dma_start(out=x1t[:],
                                                           in_=x1_hbm[128 * j:128 * (j + 1), :])])
                on("sync", ldx1)
                g_x1 = g_x1_h[0]

                def fst_v(v, g_x1=g_x1):
                    pool.wait(v, g_x1)
                    inc("vector", v.tensor_reduce(out=sm[:, FS1:FS1 + 1], in_=x1t[:],
                                                  axis=AX.X, op=Alu.add))
                on("vector", fst_v)
                c_fs1 = cnt["vector"]

                def fst_s(s, c_fs1=c_fs1):
                    s.wait_ge(sems["vector"], c_fs1)
                    sinc(s.activation(out=fsq[:], in_=x1t[:], func=AF.Square,
                                      accum_out=sm[:, FEX2:FEX2 + 1]))
                c_fsq = sq(fst_s)

                def fsm(v, c_fsq=c_fsq):
                    v.wait_ge(sems["scalar"], c_fsq)
                    P = lambda k: sm[:, k:k + 1]
                    v.tensor_scalar(out=P(FMEAN), in0=P(FS1), scalar1=1.0 / E, scalar2=None, op0=Alu.mult)
                    v.tensor_scalar(out=P(FEX2), in0=P(FEX2), scalar1=1.0 / E, scalar2=None, op0=Alu.mult)
                    v.drain()
                    v.tensor_tensor(out=P(FM2), in0=P(FMEAN), in1=P(FMEAN), op=Alu.mult)
                    v.drain()
                    inc("vector", v.tensor_tensor(out=P(FVAR), in0=P(FEX2), in1=P(FM2), op=Alu.subtract))
                on("vector", fsm)
                c_fv = cnt["vector"]

                def fsm_s(s, c_fv=c_fv):
                    s.wait_ge(sems["vector"], c_fv)
                    s.activation(out=sm[:, FSQV:FSQV + 1], in_=sm[:, FVAR:FVAR + 1],
                                 func=AF.Ln, bias=eps_t[:, 0:1], scale=1.0)
                    s.drain()
                    sinc(s.activation(out=sm[:, FRSTD:FRSTD + 1],
                                      in_=sm[:, FSQV:FSQV + 1],
                                      func=AF.Exp, scale=neghalf_t[:, 0:1]))
                c_fsqv = sq(fsm_s)

                def fnorm(v, c_fsqv=c_fsqv):
                    v.wait_ge(sems["scalar"], c_fsqv)
                    P = lambda k: sm[:, k:k + 1]
                    v.tensor_scalar(out=P(FBT), in0=P(FMEAN), scalar1=P(FRSTD),
                                    scalar2=-1.0, op0=Alu.mult, op1=Alu.mult)
                    v.drain()
                    inc("vector", v.tensor_scalar(out=hin[:], in0=x1t[:],
                                                  scalar1=P(FRSTD), scalar2=P(FBT),
                                                  op0=Alu.mult, op1=Alu.add))
                on("vector", fnorm)
                c_hin = cnt["vector"]

                for g0 in (0, 8):
                    def tph(t, g0=g0, c_hin=c_hin):
                        t.wait_ge(sems["vector"], c_hin)
                        t.wait_ge(sems["vector"], cnt["vector"])
                        pool.wait(t, g_const)
                        for k in range(8):
                            ins = t.transpose(out=ptpf[:, 128 * k:128 * (k + 1)],
                                              in_=hin[:, 128 * (g0 + k):128 * (g0 + k + 1)],
                                              identity=ident[:])
                            if k == 7:
                                inc("tensor", ins)
                    on("tensor", tph)
                    c_tph = cnt["tensor"]

                    def evh(v, g0=g0, j2=j2, c_tph=c_tph):
                        v.wait_ge(sems["tensor"], c_tph)
                        dst = hinT.ap().rearrange("p (et c) -> p et c", c=512)[:, g0:g0 + 8,
                                                                               128 * j2:128 * (j2 + 1)]
                        src = ptpf.ap().rearrange("p (et c) -> p et c", c=128)
                        inc("vector", v.tensor_copy(out=dst, in_=src))
                    on("vector", evh)
                c_hinT = cnt["vector"]

            for xh in range(2):
                c_silu_done = [cnt["scalar"], cnt["scalar"]]
                c_doh_done = [cnt["vector"], cnt["vector"]]
                c_mug_h = [cnt["tensor"], cnt["tensor"]]
                for ebl in range(32):
                    eb = 32 * xh + ebl
                    wb = eb % 2
                    r, ebr = eb // 8, eb % 8
                    g_ug_h = [None]

                    def ldug(sync, r=r, ebr=ebr, wb=wb, eb=eb, ebl=ebl):
                        sync.wait_ge(sems["cc"], cc_w)
                        if c_mug_h[ebl % 2]:
                            sync.wait_ge(sems["tensor"], c_mug_h[ebl % 2])
                        ins = []
                        for idx, agt in enumerate((ag_up_out, ag_gate_out)):
                            src = agt.ap()[E * r:E * (r + 1), 128 * ebr:128 * (ebr + 1)] \
                                .rearrange("(et p) o -> p et o", p=128)
                            dst = wug[wb].ap().rearrange("p (et c) -> p et c", c=128) \
                                [:, 16 * idx:16 * idx + 16, :]
                            ins.append(sync.dma_start(out=dst, in_=src))
                        g_ug_h[0] = pool.issue(ins)
                    on("sync", ldug)
                    g_ug = g_ug_h[0]

                    ps = ebl % 2
                    def mm_ug(t, wb=wb, g_ug=g_ug, c_hinT=c_hinT, eb=eb, ps=ps,
                              c_sfree=c_silu_done[ebl % 2], c_vfree=c_doh_done[ebl % 2]):
                        pool.wait(t, g_ug)
                        t.wait_ge(sems["vector"], c_hinT)
                        if c_sfree:
                            t.wait_ge(sems["scalar"], c_sfree)
                        if c_vfree:
                            t.wait_ge(sems["vector"], c_vfree)
                        for et in range(ET):
                            t.matmul(out=pup[ps][:], lhsT=wug[wb][:, 128 * et:128 * (et + 1)],
                                     rhs=hinT[:, 512 * et:512 * (et + 1)],
                                     start=(et == 0), stop=(et == ET - 1))
                        for et in range(ET):
                            ins = t.matmul(out=pgt[ps][:], lhsT=wug[wb][:, 2048 + 128 * et:2048 + 128 * (et + 1)],
                                           rhs=hinT[:, 512 * et:512 * (et + 1)],
                                           start=(et == 0), stop=(et == ET - 1))
                            if et == ET - 1:
                                inc("tensor", ins)
                    on("tensor", mm_ug)
                    c_mug = cnt["tensor"]
                    c_mug_h[ebl % 2] = c_mug

                    def dosilu(s, c_mug=c_mug, ps=ps, sb_=ebl % 2):
                        s.wait_ge(sems["tensor"], c_mug)
                        sinc(s.activation(out=silub[sb_][:], in_=pup[ps][:], func=AF.Silu,
                                          scale=scales_sb[:, SC_UP:SC_UP + 1]))
                    c_sil = sq(dosilu)
                    c_silu_done[ebl % 2] = c_sil

                    def doh(v, ebl=ebl, c_sil=c_sil, ps=ps, sb_=ebl % 2, grp=grp, eb=eb):
                        v.wait_ge(sems["scalar"], c_sil)
                        inc("vector", v.scalar_tensor_tensor(
                            out=hT[:, 512 * ebl:512 * (ebl + 1)], in0=pgt[ps][:],
                            scalar=scales_sb[:, SC_GATE:SC_GATE + 1], in1=silub[sb_][:],
                            op0=Alu.mult, op1=Alu.mult))
                        if DEBUG and grp == 0 and eb == 0:
                            v.drain()
                            inc("vector", v.tensor_copy(out=snap["h"][:], in_=hT[:, 0:512]))
                    on("vector", doh)
                    c_doh_done[ebl % 2] = cnt["vector"]
                c_h_ready = cnt["vector"]

                c_evd_done = [cnt["vector"], cnt["vector"]]
                c_md_h = [cnt["tensor"], cnt["tensor"]]
                for ch in range(4):
                    wb = ch % 2
                    g_dn_h = [None]

                    def lddn(sync, ch=ch, xh=xh, wb=wb):
                        sync.wait_ge(sems["cc"], cc_w)
                        if c_md_h[ch % 2]:
                            sync.wait_ge(sems["tensor"], c_md_h[ch % 2])
                        ins = []
                        for half in range(2):
                            rr = 2 * ch + half
                            src = ag_dn_out.ap()[DE * rr + 4096 * xh:DE * rr + 4096 * (xh + 1), :] \
                                .rearrange("(xt p) d -> p xt d", p=128)
                            dst = wdn[wb].ap().rearrange("p (xt c) -> p xt c", c=512) \
                                [:, :, 256 * half:256 * (half + 1)]
                            ins.append(sync.dma_start(out=dst, in_=src))
                        g_dn_h[0] = pool.issue(ins)
                    on("sync", lddn)
                    g_dn = g_dn_h[0]

                    for j2 in range(4):
                        pb = j2 % 2

                        def mm_d(t, j2=j2, pb=pb, wb=wb, g_dn=g_dn, c_h_ready=c_h_ready,
                                 c_free=c_evd_done[pb]):
                            pool.wait(t, g_dn)
                            t.wait_ge(sems["vector"], c_h_ready)
                            if c_free:
                                t.wait_ge(sems["vector"], c_free)
                            for xt in range(32):
                                ins = t.matmul(out=pd[pb][:],
                                               lhsT=hT[:, 512 * xt + 128 * j2:512 * xt + 128 * (j2 + 1)],
                                               rhs=wdn[wb][:, 512 * xt:512 * (xt + 1)],
                                               start=(xt == 0), stop=(xt == 31))
                                if xt == 31:
                                    inc("tensor", ins)
                        on("tensor", mm_d)
                        c_md = cnt["tensor"]
                        c_md_h[ch % 2] = c_md

                        def ev_d(v, j2=j2, pb=pb, ch=ch, xh=xh, c_md=c_md):
                            v.wait_ge(sems["tensor"], c_md)
                            if xh == 0:
                                ins = v.tensor_scalar(out=ysb[j2][:, 512 * ch:512 * (ch + 1)],
                                                      in0=pd[pb][:],
                                                      scalar1=scales_sb[:, SC_DOWN:SC_DOWN + 1],
                                                      scalar2=None, op0=Alu.mult)
                            else:
                                ins = v.scalar_tensor_tensor(out=ysb[j2][:, 512 * ch:512 * (ch + 1)],
                                                             in0=pd[pb][:],
                                                             scalar=scales_sb[:, SC_DOWN:SC_DOWN + 1],
                                                             in1=ysb[j2][:, 512 * ch:512 * (ch + 1)],
                                                             op0=Alu.mult, op1=Alu.add)
                            inc("vector", ins)
                        on("vector", ev_d)
                        c_evd_done[pb] = cnt["vector"]
            c_y = cnt["vector"]

            for j2 in range(4):
                j = 4 * grp + j2
                yt = ysb[j2]

                def yln_v(v, yt=yt, c_y=c_y):
                    v.wait_ge(sems["vector"], c_y)
                    inc("vector", v.tensor_reduce(out=sm[:, FS1:FS1 + 1], in_=yt[:],
                                                  axis=AX.X, op=Alu.add))
                on("vector", yln_v)
                c_ys1 = cnt["vector"]

                def yln_s(s, yt=yt, c_ys1=c_ys1):
                    s.wait_ge(sems["vector"], c_ys1)
                    sinc(s.activation(out=fsq[:], in_=yt[:], func=AF.Square,
                                      accum_out=sm[:, FEX2:FEX2 + 1]))
                c_ysq = sq(yln_s)

                def yln_v2(v, c_ysq=c_ysq):
                    v.wait_ge(sems["scalar"], c_ysq)
                    P = lambda k: sm[:, k:k + 1]
                    v.tensor_scalar(out=P(FMEAN), in0=P(FS1), scalar1=1.0 / E, scalar2=None, op0=Alu.mult)
                    v.tensor_scalar(out=P(FEX2), in0=P(FEX2), scalar1=1.0 / E, scalar2=None, op0=Alu.mult)
                    v.drain()
                    v.tensor_tensor(out=P(FM2), in0=P(FMEAN), in1=P(FMEAN), op=Alu.mult)
                    v.drain()
                    inc("vector", v.tensor_tensor(out=P(FVAR), in0=P(FEX2), in1=P(FM2), op=Alu.subtract))
                on("vector", yln_v2)
                c_yv = cnt["vector"]

                def yln_s2(s, c_yv=c_yv):
                    s.wait_ge(sems["vector"], c_yv)
                    s.activation(out=sm[:, FSQV:FSQV + 1], in_=sm[:, FVAR:FVAR + 1],
                                 func=AF.Ln, bias=eps_t[:, 0:1], scale=1.0)
                    s.drain()
                    sinc(s.activation(out=sm[:, FRSTD:FRSTD + 1],
                                      in_=sm[:, FSQV:FSQV + 1],
                                      func=AF.Exp, scale=neghalf_t[:, 0:1]))
                c_ysqv = sq(yln_s2)

                def ynorm(v, yt=yt, c_ysqv=c_ysqv):
                    v.wait_ge(sems["scalar"], c_ysqv)
                    P = lambda k: sm[:, k:k + 1]
                    v.tensor_scalar(out=P(FBT), in0=P(FMEAN), scalar1=P(FRSTD),
                                    scalar2=-1.0, op0=Alu.mult, op1=Alu.mult)
                    v.drain()
                    inc("vector", v.tensor_scalar(out=fsq[:], in0=yt[:],
                                                  scalar1=P(FRSTD), scalar2=P(FBT),
                                                  op0=Alu.mult, op1=Alu.add))
                on("vector", ynorm)
                c_yn = cnt["vector"]

                g_x1b_h = [None]

                def ldx1b(sync, j=j, c_yn=c_yn):
                    sync.wait_ge(sems["vector"], c_yn)
                    g_x1b_h[0] = pool.issue([sync.dma_start(out=x1t[:],
                                                            in_=x1_hbm[128 * j:128 * (j + 1), :])])
                on("sync", ldx1b)
                g_x1b = g_x1b_h[0]

                def yres(v, yt=yt, j=j, g_x1b=g_x1b):
                    pool.wait(v, g_x1b)
                    inc("vector", v.tensor_tensor(out=yt[:], in0=fsq[:], in1=x1t[:], op=Alu.add))
                    if DEBUG and j == 0:
                        inc("vector", v.tensor_copy(out=snap["y"][:], in_=yt[:, 0:1024]))
                on("vector", yres)
                c_out = cnt["vector"]

                def stout(sync, j=j, yt=yt, c_out=c_out):
                    sync.wait_ge(sems["vector"], c_out)
                    inc("out", sync.dma_start(out=out_ext[128 * j:128 * (j + 1), :], in_=yt[:]), 16)
                on("sync", stout)
        sflush()
        es_p5.close()

        if DEBUG:
            def ddump(sync):
                sync.wait_ge(sems["vector"], cnt["vector"])
                for nm_s, nm_d in [("sm", "dbg_sm"), ("xq", "dbg_xq"),
                                   ("ao", "dbg_attnout"),
                                   ("rec", "dbg_sums"), ("oin", "dbg_oin"),
                                   ("x1", "dbg_x1"), ("h", "dbg_h"), ("y", "dbg_y")]:
                    inc("out", sync.dma_start(out=dbg[nm_d], in_=snap[nm_s][:]), 16)
            on("sync", ddump)

        def fin(sync):
            sync.wait_ge(sems["out"], cnt["out"])
        on("sync", fin)

    return nc


_cache = {}


def _host_consts():
    t = np.arange(128)[:, None]
    j = np.arange(896)[None, :]
    cmask = np.where((j - 384) >= t, 0.0, MASKV).astype(np.float32)
    ident = np.eye(128, dtype=ml_dtypes.bfloat16)
    return cmask, ident


def kernel(L0_attn_x, L0_attn_w_q, L0_attn_w_k, L0_attn_w_v, L0_attn_w_o,
           L0_ffn_W_up, L0_ffn_W_gate, L0_ffn_W_down):
    if "nc" not in _cache:
        _cache["nc"] = build_program()
    nc = _cache["nc"]

    x2d = np.ascontiguousarray(np.asarray(L0_attn_x, dtype=np.float32).reshape(TOK, E))
    wq = np.ascontiguousarray(np.asarray(L0_attn_w_q, dtype=np.float32))
    wk = np.ascontiguousarray(np.asarray(L0_attn_w_k, dtype=np.float32))
    wv = np.ascontiguousarray(np.asarray(L0_attn_w_v, dtype=np.float32))
    wo = np.ascontiguousarray(np.asarray(L0_attn_w_o, dtype=np.float32))
    wup = np.ascontiguousarray(np.asarray(L0_ffn_W_up, dtype=np.float32))
    wgate = np.ascontiguousarray(np.asarray(L0_ffn_W_gate, dtype=np.float32))
    wdown = np.ascontiguousarray(np.asarray(L0_ffn_W_down, dtype=np.float32))
    cmask, ident = _host_consts()

    in_maps = []
    for c in range(NC):
        in_maps.append({
            "x_full": x2d,
            "x_mine": np.ascontiguousarray(x2d[TPC * c:TPC * (c + 1)]),
            "wq_s": np.ascontiguousarray(wq[256 * c:256 * (c + 1)]),
            "wk_s": np.ascontiguousarray(wk[256 * c:256 * (c + 1)]),
            "wv_s": np.ascontiguousarray(wv[256 * c:256 * (c + 1)]),
            "wo_s": np.ascontiguousarray(wo[256 * c:256 * (c + 1)]),
            "up_s": np.ascontiguousarray(wup[1024 * c:1024 * (c + 1)]),
            "gate_s": np.ascontiguousarray(wgate[1024 * c:1024 * (c + 1)]),
            "down_s": np.ascontiguousarray(wdown[256 * c:256 * (c + 1)]),
            "cmask_in": cmask,
            "ident_in": ident,
        })
    res = run_bass_kernel_spmd(nc, in_maps, core_ids=list(range(NC)))
    if DEBUG:
        _cache["results"] = res
    out = np.concatenate([np.asarray(res.results[c]["out"]) for c in range(NC)], axis=0)
    return out.reshape(B, S, E).astype(np.float32)


# version 15
# speedup vs baseline: 1.1421x; 1.0211x over previous
import math
from contextlib import ExitStack

import numpy as np
import ml_dtypes

import concourse.bass as bass
import concourse.mybir as mybir
from concourse.alu_op_type import AluOpType
from concourse.bass_utils import run_bass_kernel_spmd
from concourse.aot_env import allow_ici_all_to_all

F32 = mybir.dt.float32
BF16 = mybir.dt.bfloat16
I8 = mybir.dt.int8
AF = mybir.ActivationFunctionType
AX = mybir.AxisListType
Alu = AluOpType

B, S, E, H, D, DE = 2, 4096, 2048, 16, 128, 8192
TOK = B * S
NC = 8
TPC = TOK // NC
NT = TOK // 128
NTB = S // 128
ET = E // 128
XT = DE // 128
MASKV = -1.0e5

RC_Q, RC_K, RC_V, RC_O, RC_UP, RC_GATE, RC_DOWN = range(7)
SC_Q, SC_K, SC_V, SC_O, SC_UP, SC_GATE, SC_DOWN = range(7, 14)

DEBUG = False


class DmaGroup:
    __slots__ = ("sem", "target")

    def __init__(self, sem, target):
        self.sem = sem
        self.target = target


class DmaPool:

    def __init__(self, nc, es, n=24):
        self.sems = [es.enter_context(nc.semaphore(f"dpool{i}")) for i in range(n)]
        self.tot = [0] * n
        self.idx = 0

    def issue(self, instrs):
        i = self.idx % len(self.sems)
        self.idx += 1
        for ins in instrs:
            ins.then_inc(self.sems[i], 16)
            self.tot[i] += 16
        return DmaGroup(self.sems[i], self.tot[i])

    def wait(self, eng, grp):
        eng.wait_ge(grp.sem, grp.target)


def build_program():
    nc = bass.Bass("TRN2", target_bir_lowering=False)

    x_full = nc.dram_tensor("x_full", [TOK, E], F32, kind="ExternalInput").ap()
    x_mine = nc.dram_tensor("x_mine", [TPC, E], F32, kind="ExternalInput").ap()
    wq_s = nc.dram_tensor("wq_s", [256, E], F32, kind="ExternalInput").ap()
    wk_s = nc.dram_tensor("wk_s", [256, E], F32, kind="ExternalInput").ap()
    wv_s = nc.dram_tensor("wv_s", [256, E], F32, kind="ExternalInput").ap()
    wo_s = nc.dram_tensor("wo_s", [256, E], F32, kind="ExternalInput").ap()
    up_s = nc.dram_tensor("up_s", [1024, E], F32, kind="ExternalInput").ap()
    gate_s = nc.dram_tensor("gate_s", [1024, E], F32, kind="ExternalInput").ap()
    down_s = nc.dram_tensor("down_s", [256, DE], F32, kind="ExternalInput").ap()
    cmask_in = nc.dram_tensor("cmask_in", [128, 896], F32, kind="ExternalInput").ap()
    ident_in = nc.dram_tensor("ident_in", [128, 128], BF16, kind="ExternalInput").ap()
    out_ext = nc.dram_tensor("out", [TPC, E], F32, kind="ExternalOutput").ap()

    dbg = {}
    if DEBUG:
        for nm, sh, dt in [("dbg_scales", [1, 16], F32), ("dbg_xq", [128, E], I8),
                           ("dbg_sm", [128, 44], F32), ("dbg_partials", [128, 8], F32),
                           ("dbg_qk", [128, 512], BF16), ("dbg_v", [128, 256], BF16),
                           ("dbg_attnout", [128, 256], BF16), ("dbg_oin", [128, 512], BF16),
                           ("dbg_x1", [128, 1024], BF16), ("dbg_h", [128, 512], BF16),
                           ("dbg_y", [128, 1024], BF16), ("dbg_sums", [128, 32], F32)]:
            dbg[nm] = nc.dram_tensor(nm, sh, dt, kind="ExternalOutput").ap()

    pt_hbm = nc.dram_tensor("pt_hbm", [128, 8], F32).ap()
    ar_in = nc.dram_tensor("ar_in", [1, 8], F32)
    ar_out = nc.dram_tensor("ar_out", [1, 8], F32)
    sc_hbm = nc.dram_tensor("sc_hbm", [1, 16], F32).ap()
    ag_wo_in = nc.dram_tensor("ag_wo_in", [E, 256], BF16)
    ag_wo_out = nc.dram_tensor("ag_wo_out", [NC * E, 256], BF16, addr_space="Shared")
    ag_up_in = nc.dram_tensor("ag_up_in", [E, 1024], BF16)
    ag_up_out = nc.dram_tensor("ag_up_out", [NC * E, 1024], BF16, addr_space="Shared")
    ag_gate_in = nc.dram_tensor("ag_gate_in", [E, 1024], BF16)
    ag_gate_out = nc.dram_tensor("ag_gate_out", [NC * E, 1024], BF16, addr_space="Shared")
    ag_dn_in = nc.dram_tensor("ag_dn_in", [DE, 256], BF16)
    ag_dn_out = nc.dram_tensor("ag_dn_out", [NC * DE, 256], BF16, addr_space="Shared")
    a2a_in = nc.dram_tensor("a2a_in", [TOK, 256], BF16)
    a2a_out = nc.dram_tensor("a2a_out", [TOK, 256], BF16)
    sums_bounce = nc.dram_tensor("sums_bounce", [1, S], BF16).ap()
    x1_hbm = nc.dram_tensor("x1_hbm", [TPC, E], F32).ap()

    rg = [list(range(NC))]

    es = ExitStack()
    with es:
        block = es.enter_context(nc.Block())
        sems = {n: es.enter_context(nc.semaphore("sem_" + n))
                for n in ["vector", "scalar", "tensor", "cc",
                          "ag", "out", "x1w", "a2aw"]}
        cnt = {n: 0 for n in sems}
        pool = DmaPool(nc, es)

        def inc(n, ins, k=1):
            ins.then_inc(sems[n], k)
            cnt[n] += k
            return cnt[n]

        def on(name, f):
            getattr(block, name)(f)

        squeue = []

        def sinc(ins, k=1):
            ins.then_inc(sems["scalar"], k)
            return ins

        def sq(fn, incs=1):
            squeue.append(fn)
            cnt["scalar"] += incs
            return cnt["scalar"]

        def sflush():
            if not squeue:
                return
            fns = list(squeue)
            squeue.clear()

            def run(s):
                for f in fns:
                    f(s)
            on("scalar", run)

        SB = lambda name, shape, dt: es.enter_context(nc.sbuf_tensor(name, shape, dt))

        scales_sb = SB("scales_sb", [128, 16], F32)
        eps_t = SB("eps_t", [128, 1], F32)
        neghalf_t = SB("neghalf_t", [128, 1], F32)
        ones_bf = SB("ones_bf", [128, 1], BF16)
        ident = SB("ident", [128, 128], BF16)
        cmask = SB("cmask", [128, 896], F32)
        sm = SB("smalls", [128, 44], F32)

        if DEBUG:
            snap = {
                "sm": SB("sn_sm", [128, 44], F32), "xq": SB("sn_xq", [128, E], I8),
                "ao": SB("sn_ao", [128, 256], BF16),
                "rec": SB("sn_rec", [128, NTB], F32), "oin": SB("sn_oin", [128, 512], BF16),
                "x1": SB("sn_x1", [128, 1024], BF16), "h": SB("sn_h", [128, 512], BF16),
                "y": SB("sn_y", [128, 1024], BF16),
            }

        es_a = ExitStack()
        wqkvT = es_a.enter_context(nc.sbuf_tensor("wqkvT", [128, ET * 768], BF16))

        g_const_h = [None]

        def p0s(sync):
            i1 = sync.dma_start(out=ident[:], in_=ident_in)
            i2 = sync.dma_start(out=cmask[:], in_=cmask_in)
            g_const_h[0] = pool.issue([i1, i2])
        on("sync", p0s)
        g_const = g_const_h[0]

        def p0v(v):
            v.memset(eps_t[:], 1.0e-5)
            v.memset(neghalf_t[:], -0.5)
            inc("vector", v.memset(ones_bf[:], 1.0))
        on("vector", p0v)

        es_p1 = ExitStack()
        wt0 = es_p1.enter_context(nc.sbuf_tensor("wt0", [128, DE], F32))
        wt1 = es_p1.enter_context(nc.sbuf_tensor("wt1", [128, DE], F32))
        wt2 = es_p1.enter_context(nc.sbuf_tensor("wt2", [128, DE], F32))
        partials = es_p1.enter_context(nc.sbuf_tensor("partials", [128, 8], F32))
        ptmp = es_p1.enter_context(nc.sbuf_tensor("ptmp", [128, 1], F32))
        pt7 = es_p1.enter_context(nc.sbuf_tensor("pt7", [8, 128], F32))
        pt7s = es_p1.enter_context(nc.sbuf_tensor("pt7s", [8, 1], F32))
        srow = es_p1.enter_context(nc.sbuf_tensor("srow", [1, 16], F32))
        q8t = es_p1.enter_context(nc.sbuf_tensor("q8t", [128, DE], I8))
        qbft = es_p1.enter_context(nc.sbuf_tensor("qbft", [128, DE], BF16))
        stage0 = es_p1.enter_context(nc.sbuf_tensor("stage0", [128, DE], BF16))
        stage1 = es_p1.enter_context(nc.sbuf_tensor("stage1", [128, DE], BF16))
        stageb = [stage0, stage1]
        ptpw0 = es_p1.enter_context(nc.psum_tensor("ptpw0", [128, 1024], BF16))
        ptpw1 = es_p1.enter_context(nc.psum_tensor("ptpw1", [128, 1024], BF16))
        wt = [wt0, wt1, wt2]
        ptpw = [ptpw0, ptpw1]

        wslices = [
            (wq_s, 256, E, RC_Q), (wk_s, 256, E, RC_K), (wv_s, 256, E, RC_V),
            (wo_s, 256, E, RC_O), (up_s, 1024, E, RC_UP),
            (gate_s, 1024, E, RC_GATE), (down_s, 256, DE, RC_DOWN),
        ]

        vdone = []
        gload = []
        i = 0
        for (ap, rows, cols, rc) in wslices:
            for r0 in range(0, rows, 128):
                buf = i % 3
                ii = i

                def ldw(q, ap=ap, r0=r0, buf=buf, ii=ii, cols=cols):
                    if ii >= 3:
                        q.wait_ge(sems["vector"], vdone[ii - 3])
                    ins = q.dma_start(out=wt[buf][:, 0:cols], in_=ap[r0:r0 + 128, :])
                    gload.append(pool.issue([ins]))
                on("sync" if ii % 2 == 0 else "scalar", ldw)

                def absr(v, rc=rc, buf=buf, cols=cols, ii=ii, first=(r0 == 0)):
                    pool.wait(v, gload[ii])
                    v.tensor_reduce(out=ptmp[:], in_=wt[buf][:, 0:cols],
                                    axis=AX.X, op=Alu.add, apply_absolute_value=True)
                    v.drain()
                    if first:
                        c = inc("vector", v.tensor_copy(out=partials[:, rc:rc + 1], in_=ptmp[:]))
                    else:
                        c = inc("vector", v.tensor_tensor(out=partials[:, rc:rc + 1],
                                                          in0=partials[:, rc:rc + 1],
                                                          in1=ptmp[:], op=Alu.add))
                    vdone.append(c)
                on("vector", absr)
                i += 1
        c_passA = cnt["vector"]

        g_pt = [None, None]

        def pr1(sync):
            sync.wait_ge(sems["vector"], c_passA)
            g_pt[0] = pool.issue([sync.dma_start(out=pt_hbm, in_=partials[:])])
            pool.wait(sync, g_pt[0])
            g_pt[1] = pool.issue([sync.dma_start(
                out=pt7[:], in_=pt_hbm.rearrange("p t -> t p"),
                allow_slow_non_contiguous=True)])
        on("sync", pr1)

        def pr2(v):
            pool.wait(v, g_pt[1])
            inc("vector", v.tensor_reduce(out=pt7s[:], in_=pt7[:], axis=AX.X, op=Alu.add))
        on("vector", pr2)
        c_pr2 = cnt["vector"]

        g_arin_h = [None]

        def pr3(sync):
            sync.wait_ge(sems["vector"], c_pr2)
            dst = ar_in.ap().rearrange("x (t o) -> (x t) o", o=1)[0:8, :]
            g_arin_h[0] = pool.issue([sync.dma_start(out=dst, in_=pt7s[:],
                                                     allow_slow_non_contiguous=True)])
        on("sync", pr3)
        g_arin = g_arin_h[0]

        def cc1(g):
            pool.wait(g, g_arin)
            inc("cc", g.collective_compute("AllReduce", Alu.add, replica_groups=rg,
                                           ins=[ar_in.ap().opt()], outs=[ar_out.ap().opt()]))
        on("gpsimd", cc1)
        c_ar = cnt["cc"]

        numel = {RC_Q: E * E, RC_K: E * E, RC_V: E * E, RC_O: E * E,
                 RC_UP: DE * E, RC_GATE: DE * E, RC_DOWN: E * DE}
        g_srow_h = [None]

        def sc1(sync):
            sync.wait_ge(sems["cc"], c_ar)
            g_srow_h[0] = pool.issue([sync.dma_start(out=srow[0:1, 0:8],
                                                     in_=ar_out.ap()[0:1, 0:8])])
        on("sync", sc1)
        g_srow = g_srow_h[0]

        if DEBUG:
            def dpart(sync):
                sync.wait_ge(sems["vector"], c_passA)
                inc("out", sync.dma_start(out=dbg["dbg_partials"], in_=partials[:]), 16)
            on("sync", dpart)

        def sc2(v):
            pool.wait(v, g_srow)
            for k in range(7):
                v.tensor_scalar(out=srow[0:1, k:k + 1], in0=srow[0:1, k:k + 1],
                                scalar1=1.0 / numel[k], scalar2=None, op0=Alu.mult)
                v.drain()
            for k in (RC_Q, RC_K, RC_V, RC_O):
                v.tensor_scalar(out=srow[0:1, k:k + 1], in0=srow[0:1, k:k + 1],
                                scalar1=1.0e-5, scalar2=None, op0=Alu.max)
                v.drain()
            for k in (RC_UP, RC_GATE, RC_DOWN):
                v.tensor_scalar(out=srow[0:1, k:k + 1], in0=srow[0:1, k:k + 1],
                                scalar1=1.0e-8, scalar2=None, op0=Alu.add)
                v.drain()
            v.tensor_scalar(out=srow[0:1, SC_Q:SC_Q + 1], in0=srow[0:1, RC_Q:RC_Q + 1],
                            scalar1=1.0 / math.sqrt(D), scalar2=None, op0=Alu.mult)
            v.drain()
            for s_c, d_c in ((RC_K, SC_K), (RC_V, SC_V), (RC_O, SC_O),
                             (RC_UP, SC_UP), (RC_GATE, SC_GATE), (RC_DOWN, SC_DOWN)):
                v.tensor_copy(out=srow[0:1, d_c:d_c + 1], in_=srow[0:1, s_c:s_c + 1])
            v.drain()
            inc("vector", v.reciprocal(out=srow[0:1, 0:7], in_=srow[0:1, 0:7]))
        on("vector", sc2)
        c_sc2 = cnt["vector"]

        g_scales_h = [None]

        def sc3(sync):
            sync.wait_ge(sems["vector"], c_sc2)
            g1 = pool.issue([sync.dma_start(out=sc_hbm, in_=srow[:])])
            pool.wait(sync, g1)
            g_scales_h[0] = pool.issue([sync.dma_start(
                out=scales_sb[:], in_=sc_hbm[0:1, 0:16].partition_broadcast(128))])
            if DEBUG:
                inc("out", sync.dma_start(out=dbg["dbg_scales"], in_=srow[:]), 16)
        on("sync", sc3)
        g_scales = g_scales_h[0]

        qkv_off = {id(wq_s): 0, id(wk_s): 256, id(wv_s): 512}
        wjobs = [
            (wq_s, 256, E, RC_Q, None), (wk_s, 256, E, RC_K, None),
            (wv_s, 256, E, RC_V, None), (wo_s, 256, E, RC_O, ag_wo_in),
            (up_s, 1024, E, RC_UP, ag_up_in), (gate_s, 1024, E, RC_GATE, ag_gate_in),
            (down_s, 256, DE, RC_DOWN, ag_dn_in),
        ]
        ji = 0
        qdone = []
        gloadb = []
        tpfree = [0, 0]
        c_tpw_h = [0]
        qcast_prev = [0]
        stg_ag = [0, 0]
        stg_vec = [0, 0]
        for (ap, rows, cols, rc, dest) in wjobs:
            nets = cols // 128
            for j in range(rows // 128):
                buf = ji % 3
                jj = ji

                def ldq(q, ap=ap, j=j, buf=buf, jj=jj, cols=cols):
                    pool.wait(q, g_scales)
                    if jj >= 3:
                        q.wait_ge(sems["vector"], qdone[jj - 3])
                    ins = q.dma_start(out=wt[buf][:, 0:cols],
                                      in_=ap[j * 128:(j + 1) * 128, :])
                    gloadb.append(pool.issue([ins]))
                on("sync" if jj % 2 == 0 else "scalar", ldq)

                def qn(v, rc=rc, buf=buf, cols=cols, jj=jj):
                    pool.wait(v, gloadb[jj])
                    pool.wait(v, g_scales)
                    if jj > 0:
                        v.wait_ge(sems["scalar"], qcast_prev[0])
                    v.tensor_scalar(out=wt[buf][:, 0:cols], in0=wt[buf][:, 0:cols],
                                    scalar1=scales_sb[:, rc:rc + 1], scalar2=1.0,
                                    op0=Alu.mult, op1=Alu.min)
                    v.drain()
                    c = inc("vector", v.tensor_scalar(out=q8t[:, 0:cols], in0=wt[buf][:, 0:cols],
                                                      scalar1=-1.0, scalar2=None, op0=Alu.max))
                    qdone.append(c)
                on("vector", qn)
                c_q8w = cnt["vector"]

                def qcastw(s, cols=cols, c_q8w=c_q8w, jj=jj):
                    s.wait_ge(sems["vector"], c_q8w)
                    if jj > 0:
                        s.wait_ge(sems["tensor"], c_tpw_h[0])
                    inc("scalar", s.copy(out=qbft[:, 0:cols], in_=q8t[:, 0:cols]))
                on("scalar", qcastw)
                c_q = cnt["scalar"]
                qcast_prev[0] = c_q

                for g0 in range(0, nets, 8):
                    gN = min(8, nets - g0)
                    pb = (g0 // 8) % 2

                    def tpw(t, g0=g0, gN=gN, pb=pb, c_q=c_q, sb_=ji % 2):
                        t.wait_ge(sems["scalar"], c_q)
                        t.wait_ge(sems["scalar"], tpfree[pb])
                        t.wait_ge(sems["ag"], stg_ag[sb_])
                        t.wait_ge(sems["vector"], stg_vec[sb_])
                        pool.wait(t, g_const)
                        for k in range(gN):
                            ins = t.transpose(out=ptpw[pb][:, 128 * k:128 * (k + 1)],
                                              in_=qbft[:, 128 * (g0 + k):128 * (g0 + k + 1)],
                                              identity=ident[:])
                            if k == gN - 1:
                                inc("tensor", ins)
                    on("tensor", tpw)
                    c_tp = cnt["tensor"]
                    c_tpw_h[0] = c_tp

                    def evw(s, g0=g0, gN=gN, pb=pb, c_tp=c_tp, sb_=ji % 2):
                        s.wait_ge(sems["tensor"], c_tp)
                        inc("scalar", s.copy(out=stageb[sb_][:, 128 * g0:128 * (g0 + gN)],
                                             in_=ptpw[pb][:, 0:128 * gN]))
                    on("scalar", evw)
                    tpfree[pb] = cnt["scalar"]
                c_stage = cnt["scalar"]

                if dest is None:
                    off = qkv_off[id(ap)] + 128 * j

                    def wqkv(v, off=off, nets=nets, c_stage=c_stage, sb_=ji % 2):
                        v.wait_ge(sems["scalar"], c_stage)
                        dst = wqkvT.ap().rearrange("p (et c) -> p et c", c=768)[:, :, off:off + 128]
                        src = stageb[sb_].ap().rearrange("p (et c) -> p et c", c=128)[:, 0:nets, :]
                        c = inc("vector", v.tensor_copy(out=dst, in_=src))
                        stg_vec[sb_] = c
                    on("vector", wqkv)
                else:
                    def wag(sync, dest=dest, j=j, nets=nets, c_stage=c_stage, sb_=ji % 2):
                        sync.wait_ge(sems["scalar"], c_stage)
                        dst = dest.ap().rearrange("(et p) o -> p et o", p=128)[:, :, 128 * j:128 * (j + 1)]
                        src = stageb[sb_].ap().rearrange("p (et c) -> p et c", c=128)[:, 0:nets, :]
                        inc("ag", sync.dma_start(out=dst, in_=src), 16)
                        stg_ag[sb_] = cnt["ag"]
                    on("sync", wag)
                ji += 1
        c_wqkv_rdy = cnt["vector"]
        c_ag_writes = cnt["ag"]

        def cc2(g):
            g.wait_ge(sems["ag"], c_ag_writes)
            for (i_, o_) in ((ag_wo_in, ag_wo_out), (ag_up_in, ag_up_out),
                             (ag_gate_in, ag_gate_out), (ag_dn_in, ag_dn_out)):
                inc("cc", g.collective_compute("AllGather", Alu.bypass, replica_groups=rg,
                                               ins=[i_.ap().opt()], outs=[o_.ap().opt()]))
        on("gpsimd", cc2)
        cc_w = cnt["cc"]
        c_p1_tensor = cnt["tensor"]

        es_p1.close()

        qT_sb = es_a.enter_context(nc.sbuf_tensor("qT_sb", [128, NT * 256], BF16))
        kT_sb = es_a.enter_context(nc.sbuf_tensor("kT_sb", [128, NT * 256], BF16))
        v_sb = es_a.enter_context(nc.sbuf_tensor("v_sb", [128, NT * 256], BF16))
        attnout_sb = es_a.enter_context(nc.sbuf_tensor("attnout_sb", [128, NT * 256], BF16))

        (S1, MEAN, EX2, M2, VAR, SQV, RSTD, MX, MN, T1, T2, AMAX, MAXLN, CC_, SXV,
         INVSX, AQ, BQ) = range(18)

        es_p2 = ExitStack()
        xt0 = es_p2.enter_context(nc.sbuf_tensor("xt0", [128, E], F32))
        xt1 = es_p2.enter_context(nc.sbuf_tensor("xt1", [128, E], F32))
        sqdump = es_p2.enter_context(nc.sbuf_tensor("sqdump", [128, E], F32))
        xq8a = es_p2.enter_context(nc.sbuf_tensor("xq8a", [128, E], I8))
        xq8b = es_p2.enter_context(nc.sbuf_tensor("xq8b", [128, E], I8))
        xqbfa = es_p2.enter_context(nc.sbuf_tensor("xqbfa", [128, E], BF16))
        xqbfb = es_p2.enter_context(nc.sbuf_tensor("xqbfb", [128, E], BF16))
        xq8 = [xq8a, xq8b]
        xqbf = [xqbfa, xqbfb]
        xqT = es_p2.enter_context(nc.sbuf_tensor("xqT", [128, E], BF16))
        qkstage0 = es_p2.enter_context(nc.sbuf_tensor("qkstage0", [128, 512], BF16))
        qkstage1 = es_p2.enter_context(nc.sbuf_tensor("qkstage1", [128, 512], BF16))
        qkstage = [qkstage0, qkstage1]
        ptpx0 = es_p2.enter_context(nc.psum_tensor("ptpx0", [128, 1024], BF16))
        ptpx1 = es_p2.enter_context(nc.psum_tensor("ptpx1", [128, 1024], BF16))
        pq1 = es_p2.enter_context(nc.psum_tensor("pq1", [128, 512], F32))
        pq2 = es_p2.enter_context(nc.psum_tensor("pq2", [128, 256], F32))
        ptpqk = es_p2.enter_context(nc.psum_tensor("ptpqk", [128, 512], BF16))
        xts = [xt0, xt1]
        ptpx = [ptpx0, ptpx1]
        c_tile_done = [0] * NT
        gx = [None] * NT

        c_quant_h = [0] * NT
        c_mmq_h = [0] * NT
        c_evq_h = [0] * NT
        c_tqk_h = [0] * NT
        c_evqk_h = [0] * NT
        c_xqT_h = [0] * NT

        def p2_pre(tt):
            buf = tt % 2
            po = 20 * (tt % 2)

            def ldx(sync, tt=tt, buf=buf):
                if tt >= 2:
                    sync.wait_ge(sems["scalar"], c_tile_done[tt - 2])
                    sync.wait_ge(sems["vector"], c_quant_h[tt - 2] if tt >= 2 else 0)
                else:
                    sync.wait_ge(sems["vector"], c_wqkv_rdy)
                    sync.wait_ge(sems["tensor"], c_p1_tensor)
                gx[tt] = pool.issue([sync.dma_start(out=xts[buf][:],
                                                    in_=x_full[tt * 128:(tt + 1) * 128, :])])
            on("sync", ldx)

            def st_v(v, tt=tt, buf=buf, po=po):
                pool.wait(v, gx[tt])
                v.tensor_reduce(out=sm[:, po + MX:po + MX + 1], in_=xts[buf][:],
                                axis=AX.X, op=Alu.max)
                inc("vector", v.tensor_reduce(out=sm[:, po + MN:po + MN + 1], in_=xts[buf][:],
                                              axis=AX.X, op=Alu.min))
            on("vector", st_v)
            c_sum = cnt["vector"]

            def st_s(s, buf=buf, po=po, c_sum=c_sum, tt=tt):
                pool.wait(s, gx[tt])
                s.activation(out=sqdump[:], in_=xts[buf][:], func=AF.Identity,
                             accum_out=sm[:, po + S1:po + S1 + 1])
                sinc(s.activation(out=sqdump[:], in_=xts[buf][:], func=AF.Square,
                                  accum_out=sm[:, po + EX2:po + EX2 + 1]))
            c_sq = sq(st_s)

            def sml_v(v, po=po, c_sq=c_sq):
                v.wait_ge(sems["scalar"], c_sq)
                v.tensor_scalar(out=sm[:, po + MEAN:po + MEAN + 1], in0=sm[:, po + S1:po + S1 + 1],
                                scalar1=1.0 / E, scalar2=None, op0=Alu.mult)
                v.tensor_scalar(out=sm[:, po + EX2:po + EX2 + 1], in0=sm[:, po + EX2:po + EX2 + 1],
                                scalar1=1.0 / E, scalar2=None, op0=Alu.mult)
                v.drain()
                v.tensor_tensor(out=sm[:, po + M2:po + M2 + 1], in0=sm[:, po + MEAN:po + MEAN + 1],
                                in1=sm[:, po + MEAN:po + MEAN + 1], op=Alu.mult)
                v.drain()
                inc("vector", v.tensor_tensor(out=sm[:, po + VAR:po + VAR + 1],
                                              in0=sm[:, po + EX2:po + EX2 + 1],
                                              in1=sm[:, po + M2:po + M2 + 1], op=Alu.subtract))
            on("vector", sml_v)
            c_var = cnt["vector"]

            def sml_s(s, po=po, c_var=c_var):
                s.wait_ge(sems["vector"], c_var)
                s.activation(out=sm[:, po + SQV:po + SQV + 1],
                             in_=sm[:, po + VAR:po + VAR + 1],
                             func=AF.Ln, bias=eps_t[:, 0:1], scale=1.0)
                s.drain()
                sinc(s.activation(out=sm[:, po + RSTD:po + RSTD + 1],
                                  in_=sm[:, po + SQV:po + SQV + 1],
                                  func=AF.Exp, scale=neghalf_t[:, 0:1]))
            c_sqv = sq(sml_s)

            def qv(v, buf=buf, po=po, tt=tt, c_sqv=c_sqv):
                v.wait_ge(sems["scalar"], c_sqv)
                P = lambda k: sm[:, po + k:po + k + 1]
                v.tensor_tensor(out=P(T1), in0=P(MX), in1=P(MEAN), op=Alu.subtract)
                v.tensor_tensor(out=P(T2), in0=P(MEAN), in1=P(MN), op=Alu.subtract)
                v.drain()
                v.tensor_tensor(out=P(AMAX), in0=P(T1), in1=P(T2), op=Alu.max)
                v.drain()
                v.tensor_tensor(out=P(MAXLN), in0=P(AMAX), in1=P(RSTD), op=Alu.mult)
                v.drain()
                v.tensor_scalar(out=P(CC_), in0=P(MAXLN), scalar1=1.0e-5, scalar2=None, op0=Alu.max)
                v.drain()
                v.reciprocal(out=P(SXV), in_=P(CC_))
                v.drain()
                v.tensor_scalar(out=P(SXV), in0=P(SXV), scalar1=127.0, scalar2=None, op0=Alu.mult)
                v.tensor_scalar(out=P(INVSX), in0=P(CC_), scalar1=1.0 / 127.0, scalar2=None, op0=Alu.mult)
                v.drain()
                v.tensor_tensor(out=P(AQ), in0=P(RSTD), in1=P(SXV), op=Alu.mult)
                v.drain()
                v.tensor_scalar(out=P(BQ), in0=P(MEAN), scalar1=P(AQ), scalar2=-1.0,
                                op0=Alu.mult, op1=Alu.mult)
                v.drain()
                v.tensor_scalar(out=xq8[buf][:], in0=xts[buf][:], scalar1=P(AQ), scalar2=P(BQ),
                                op0=Alu.mult, op1=Alu.add)
                v.drain()
                inc("vector", v.tensor_copy(out=xqbf[buf][:], in_=xq8[buf][:]))
            on("vector", qv)
            c_quant_h[tt] = cnt["vector"]

            if DEBUG and tt == 0:
                def dxq(v):
                    v.tensor_copy(out=snap["xq"][:], in_=xq8[0][:])
                    inc("vector", v.tensor_copy(out=snap["sm"][:], in_=sm[:]))
                on("vector", dxq)
                c_quant_h[tt] = cnt["vector"]

        def p2_pe(tt):
            buf = tt % 2
            po = 20 * (tt % 2)
            c_quant = c_quant_h[tt]
            c_xqT = 0
            for g0 in (0, 8):
                pb = g0 // 8

                def tpx(t, g0=g0, pb=pb, tt=tt, buf=buf, c_quant=c_quant):
                    t.wait_ge(sems["vector"], c_quant)
                    if tt > 0:
                        t.wait_ge(sems["scalar"], c_xqT_h[tt - 1])
                    pool.wait(t, g_const)
                    for k in range(8):
                        ins = t.transpose(out=ptpx[pb][:, 128 * k:128 * (k + 1)],
                                          in_=xqbf[buf][:, 128 * (g0 + k):128 * (g0 + k + 1)],
                                          identity=ident[:])
                        if k == 7:
                            inc("tensor", ins)
                on("tensor", tpx)
                c_tpx = cnt["tensor"]

                def evx(s, g0=g0, pb=pb, c_tpx=c_tpx, tt=tt):
                    s.wait_ge(sems["tensor"], c_tpx)
                    if tt > 0:
                        s.wait_ge(sems["tensor"], c_mmq_h[tt - 1])
                    sinc(s.copy(out=xqT[:, 128 * g0:128 * (g0 + 8)], in_=ptpx[pb][:]))
                c_xqT = sq(evx)
                c_xqT_h[tt] = c_xqT

            def mmq(t, c_xqT=c_xqT, tt=tt):
                t.wait_ge(sems["scalar"], c_xqT)
                t.wait_ge(sems["vector"], c_wqkv_rdy)
                if tt > 0:
                    t.wait_ge(sems["vector"], c_evq_h[tt - 1])
                for et in range(ET):
                    t.matmul(out=pq1[:], lhsT=xqT[:, 128 * et:128 * (et + 1)],
                             rhs=wqkvT[:, 768 * et:768 * et + 512],
                             start=(et == 0), stop=(et == ET - 1))
                for et in range(ET):
                    ins = t.matmul(out=pq2[:], lhsT=xqT[:, 128 * et:128 * (et + 1)],
                                   rhs=wqkvT[:, 768 * et + 512:768 * (et + 1)],
                                   start=(et == 0), stop=(et == ET - 1))
                    if et == ET - 1:
                        inc("tensor", ins)
            on("tensor", mmq)
            c_mmq_h[tt] = cnt["tensor"]
            c_mmq = cnt["tensor"]

            def evq(v, tt=tt, po=po, c_mmq=c_mmq, buf=buf):
                v.wait_ge(sems["tensor"], c_mmq)
                if tt > 1:
                    v.wait_ge(sems["tensor"], c_tqk_h[tt - 2])
                P = lambda k: sm[:, po + k:po + k + 1]
                v.tensor_scalar(out=qkstage[buf][:, 0:256], in0=pq1[:, 0:256],
                                scalar1=P(INVSX), scalar2=scales_sb[:, SC_Q:SC_Q + 1],
                                op0=Alu.mult, op1=Alu.mult)
                v.tensor_scalar(out=qkstage[buf][:, 256:512], in0=pq1[:, 256:512],
                                scalar1=P(INVSX), scalar2=scales_sb[:, SC_K:SC_K + 1],
                                op0=Alu.mult, op1=Alu.mult)
                inc("vector", v.tensor_scalar(out=v_sb[:, 256 * tt:256 * (tt + 1)],
                                              in0=pq2[:], scalar1=P(INVSX),
                                              scalar2=scales_sb[:, SC_V:SC_V + 1],
                                              op0=Alu.mult, op1=Alu.mult))
            on("vector", evq)
            c_evq_h[tt] = cnt["vector"]
            c_evq = cnt["vector"]


        def p2_pe_b(tt):
            buf = tt % 2

            def tqk(t, tt=tt, buf=buf):
                t.wait_ge(sems["vector"], c_evq_h[tt])
                if tt > 0:
                    t.wait_ge(sems["scalar"], c_evqk_h[tt - 1])
                for k in range(4):
                    ins = t.transpose(out=ptpqk[:, 128 * k:128 * (k + 1)],
                                      in_=qkstage[buf][:, 128 * k:128 * (k + 1)],
                                      identity=ident[:])
                    if k == 3:
                        inc("tensor", ins)
            on("tensor", tqk)
            c_tqk_h[tt] = cnt["tensor"]
            c_tqk = cnt["tensor"]

            def evqk(s, tt=tt, c_tqk=c_tqk):
                s.wait_ge(sems["tensor"], c_tqk)
                s.copy(out=qT_sb[:, 256 * tt:256 * (tt + 1)], in_=ptpqk[:, 0:256])
                sinc(s.copy(out=kT_sb[:, 256 * tt:256 * (tt + 1)], in_=ptpqk[:, 256:512]))
            c = sq(evqk)
            c_evqk_h[tt] = c
            c_tile_done[tt] = c

        for step in range(NT + 2):
            if step < NT:
                p2_pre(step)
            if step >= 1 and step - 1 < NT:
                p2_pe(step - 1)
            if step >= 2:
                p2_pe_b(step - 2)

        sflush()
        es_p2.close()

        es_p3 = ExitStack()
        attnT0 = es_p3.enter_context(nc.sbuf_tensor("attnT0", [128, 512], BF16))
        attnT1 = es_p3.enter_context(nc.sbuf_tensor("attnT1", [128, 512], BF16))
        attnT2 = es_p3.enter_context(nc.sbuf_tensor("attnT2", [128, 512], BF16))
        scm0 = es_p3.enter_context(nc.sbuf_tensor("scm0", [128, 512], F32))
        scm1 = es_p3.enter_context(nc.sbuf_tensor("scm1", [128, 512], F32))
        scm2 = es_p3.enter_context(nc.sbuf_tensor("scm2", [128, 512], F32))
        outT = es_p3.enter_context(nc.sbuf_tensor("outT", [128, 4096], BF16))
        sumsrow = es_p3.enter_context(nc.sbuf_tensor("sumsrow", [1, S], F32))
        recips = es_p3.enter_context(nc.sbuf_tensor("recips", [128, NTB], F32))
        sumsbf = es_p3.enter_context(nc.sbuf_tensor("sumsbf", [1, S], BF16))
        sums32 = es_p3.enter_context(nc.sbuf_tensor("sums32", [32, 128], BF16))
        psc0 = es_p3.enter_context(nc.psum_tensor("psc0", [128, 512], F32))
        psc1 = es_p3.enter_context(nc.psum_tensor("psc1", [128, 512], F32))
        psc2 = es_p3.enter_context(nc.psum_tensor("psc2", [128, 512], F32))
        pout = es_p3.enter_context(nc.psum_tensor("pout", [128, 512], F32))
        psum1 = es_p3.enter_context(nc.psum_tensor("psum1", [1, 512], F32))
        ptpo = es_p3.enter_context(nc.psum_tensor("ptpo", [128, 512], BF16))
        ptps = es_p3.enter_context(nc.psum_tensor("ptps", [128, 32], BF16))
        psc = [psc0, psc1, psc2]
        c_qkv_all = cnt["vector"]
        g_sums_prev = [None]

        attnTb = [attnT0, attnT1, attnT2]
        scmb = [scm0, scm1, scm2]
        for bh in range(4):
            b, ds = bh // 2, bh % 2
            tiles = [(sc, tl) for sc in range(8) for tl in range(4 * sc + 4)]
            NTI = len(tiles)
            c_exp_h = [0] * NTI
            c_ov_h = [0] * NTI
            psc_cons = [None, None, None]
            c_ev_sc_map = {}
            c_ev_last = 0

            def emit_mm_sc(idx):
                sc, tl = tiles[idx]
                tg = NTB * b + tl
                pb = idx % 3

                def mm_sc(t, tg=tg, pb=pb, sc=sc, cons=psc_cons[pb]):
                    t.wait_ge(sems["vector"], c_qkv_all)
                    if cons is not None:
                        t.wait_ge(sems[cons[0]], cons[1])
                    qv_ = qT_sb.ap().rearrange("p (t c) -> p t c", c=256)
                    rhs = qv_[:, NTB * b + 4 * sc:NTB * b + 4 * sc + 4,
                              128 * ds:128 * (ds + 1)]
                    inc("tensor", t.matmul(
                        out=psc[pb][:],
                        lhsT=kT_sb[:, 256 * tg + 128 * ds:256 * tg + 128 * (ds + 1)],
                        rhs=rhs, start=True, stop=True))
                on("tensor", mm_sc)
                return cnt["tensor"]

            def emit_exp(idx, c_mmsc):
                sc, tl = tiles[idx]
                pb = idx % 3
                diag = tl >= 4 * sc
                c_at_free = c_ov_h[idx - 3] if idx >= 3 else 0
                if diag:
                    kk = tl - 4 * sc

                    def mskv(v, kk=kk, pb=pb, c_mmsc=c_mmsc):
                        v.wait_ge(sems["tensor"], c_mmsc)
                        if c_exp_h[idx - 3] if idx >= 3 else 0:
                            v.wait_ge(sems["scalar"], c_exp_h[idx - 3])
                        inc("vector", v.tensor_tensor(
                            out=scmb[pb][:], in0=psc[pb][:],
                            in1=cmask[:, 384 - 128 * kk:896 - 128 * kk], op=Alu.add))
                    on("vector", mskv)
                    c_mv = cnt["vector"]
                    psc_cons[pb] = ("vector", c_mv)

                    def expd(s, pb=pb, c_mv=c_mv, c_at_free=c_at_free):
                        s.wait_ge(sems["vector"], c_mv)
                        if c_at_free:
                            s.wait_ge(sems["tensor"], c_at_free)
                        sinc(s.activation(out=attnTb[pb][:], in_=scmb[pb][:], func=AF.Exp))
                    sq(expd)
                else:
                    def expn(s, pb=pb, c_mmsc=c_mmsc, c_at_free=c_at_free):
                        s.wait_ge(sems["tensor"], c_mmsc)
                        if c_at_free:
                            s.wait_ge(sems["tensor"], c_at_free)
                        sinc(s.activation(out=attnTb[pb][:], in_=psc[pb][:], func=AF.Exp))
                    sq(expn)
                    psc_cons[pb] = ("scalar", cnt["scalar"])
                c_exp_h[idx] = cnt["scalar"]

            def emit_mm_ov(idx):
                sc, tl = tiles[idx]
                tg = NTB * b + tl
                pb = idx % 3
                first = tl == 0
                last = tl == 4 * sc + 3
                c_exp = c_exp_h[idx]
                c_prev_ev = c_ev_sc_map.get(sc - 1, 0)

                def mm_ov(t, tg=tg, pb=pb, first=first, last=last,
                          c_exp=c_exp, c_prev_ev=c_prev_ev):
                    t.wait_ge(sems["scalar"], c_exp)
                    if first and c_prev_ev:
                        t.wait_ge(sems["vector"], c_prev_ev)
                    t.matmul(out=pout[:],
                             lhsT=v_sb[:, 256 * tg + 128 * ds:256 * tg + 128 * (ds + 1)],
                             rhs=attnTb[pb][:], start=first, stop=last)
                    inc("tensor", t.matmul(out=psum1[:], lhsT=ones_bf[:], rhs=attnTb[pb][:],
                                           start=first, stop=last))
                on("tensor", mm_ov)
                c_ov_h[idx] = cnt["tensor"]
                if last:
                    def ev_sc(v, sc=sc, c_last_mm=cnt["tensor"]):
                        v.wait_ge(sems["tensor"], c_last_mm)
                        if sc == 0 and g_sums_prev[0] is not None:
                            pool.wait(v, g_sums_prev[0])
                        v.tensor_copy(out=outT[:, 512 * sc:512 * (sc + 1)], in_=pout[:])
                        inc("vector", v.tensor_copy(out=sumsrow[0:1, 512 * sc:512 * (sc + 1)],
                                                    in_=psum1[0:1, :]))
                    on("vector", ev_sc)
                    c_ev_sc_map[sc] = cnt["vector"]

            for idx in range(NTI):
                c_mmsc = emit_mm_sc(idx)
                emit_exp(idx, c_mmsc)
                if idx >= 1:
                    emit_mm_ov(idx - 1)
            emit_mm_ov(NTI - 1)
            sflush()
            c_ev_sc = c_ev_sc_map[7]

            def sums_bf(v, bh=bh):
                v.drain()
                inc("vector", v.tensor_copy(out=sumsbf[0:1, :], in_=sumsrow[0:1, :]))
            on("vector", sums_bf)
            c_sbf = cnt["vector"]

            g_sums_h = [None]

            def sums_mv(sync, c_sbf=c_sbf):
                sync.wait_ge(sems["vector"], c_sbf)
                g1 = pool.issue([sync.dma_start(out=sums_bounce, in_=sumsbf[0:1, :])])
                pool.wait(sync, g1)
                src = sums_bounce.rearrange("x (t p) -> (x t) p", t=32)
                g_sums_h[0] = pool.issue([sync.dma_start(out=sums32[:], in_=src)])
            on("sync", sums_mv)
            g_sums = g_sums_h[0]
            g_sums_prev[0] = g_sums

            def sums_tp(t, g_sums=g_sums):
                pool.wait(t, g_sums)
                inc("tensor", t.transpose(out=ptps[:], in_=sums32[:], identity=ident[0:32, 0:32]))
            on("tensor", sums_tp)
            c_stp = cnt["tensor"]

            def recv(v, c_stp=c_stp, bh=bh):
                v.wait_ge(sems["tensor"], c_stp)
                v.tensor_copy(out=recips[:], in_=ptps[:])
                v.drain()
                inc("vector", v.reciprocal(out=recips[:], in_=recips[:]))
                v.drain()
                if DEBUG and bh == 0:
                    inc("vector", v.tensor_copy(out=snap["rec"][:], in_=recips[:]))
            on("vector", recv)
            c_rec = cnt["vector"]

            c_ev_tp = 0
            for sc in range(8):
                def tpo(t, sc=sc, c_rec=c_rec, c_ev_tp=c_ev_tp):
                    t.wait_ge(sems["vector"], c_rec)
                    if c_ev_tp:
                        t.wait_ge(sems["vector"], c_ev_tp)
                    for j in range(4):
                        ins = t.transpose(out=ptpo[:, 128 * j:128 * (j + 1)],
                                          in_=outT[:, 512 * sc + 128 * j:512 * sc + 128 * (j + 1)],
                                          identity=ident[:])
                        if j == 3:
                            inc("tensor", ins)
                on("tensor", tpo)
                c_tpo = cnt["tensor"]

                def evtp(v, sc=sc, b=b, ds=ds, c_tpo=c_tpo):
                    v.wait_ge(sems["tensor"], c_tpo)
                    for j in range(4):
                        st = 4 * sc + j
                        ins = v.tensor_scalar(
                            out=attnout_sb[:, 256 * (NTB * b + st) + 128 * ds:
                                           256 * (NTB * b + st) + 128 * (ds + 1)],
                            in0=ptpo[:, 128 * j:128 * (j + 1)],
                            scalar1=recips[:, st:st + 1], scalar2=None, op0=Alu.mult)
                        if j == 3:
                            inc("vector", ins)
                on("vector", evtp)
                c_ev_tp = cnt["vector"]

        if DEBUG:
            def dato(v):
                v.wait_ge(sems["vector"], cnt["vector"])
                inc("vector", v.tensor_copy(out=snap["ao"][:], in_=attnout_sb[:, 0:256]))
            on("vector", dato)

        def at_out(sync):
            sync.wait_ge(sems["vector"], cnt["vector"])
            dst = a2a_in.ap().rearrange("(t p) f -> p t f", p=128)
            src_ = attnout_sb.ap().rearrange("p (t f) -> p t f", f=256)
            inc("a2aw", sync.dma_start(out=dst, in_=src_), 16)
        on("sync", at_out)
        c_a2a_in = cnt["a2aw"]

        def cc3(g):
            g.wait_ge(sems["a2aw"], c_a2a_in)
            with allow_ici_all_to_all():
                inc("cc", g.collective_compute("AllToAll", Alu.bypass, replica_groups=rg,
                                               ins=[a2a_in.ap().opt()], outs=[a2a_out.ap().opt()]))
        on("gpsimd", cc3)
        cc_at = cnt["cc"]
        c_p3_v = cnt["vector"]
        c_p3_t = cnt["tensor"]
        c_p3_s = cnt["scalar"]

        es_p3.close()
        es_a.close()

        OS1, OMEAN, OEX2, OM2, OVAR, OSQV, ORSTD, OCC, OSX, OINVSX, OBT = range(11)

        es_p4 = ExitStack()
        woT_sb = es_p4.enter_context(nc.sbuf_tensor("woT_sb", [128, ET * 2048], BF16))
        oin0 = es_p4.enter_context(nc.sbuf_tensor("oin0", [128, E], BF16))
        oin1 = es_p4.enter_context(nc.sbuf_tensor("oin1", [128, E], BF16))
        oinb = [oin0, oin1]
        xres0 = es_p4.enter_context(nc.sbuf_tensor("xres0", [128, E], F32))
        xres1 = es_p4.enter_context(nc.sbuf_tensor("xres1", [128, E], F32))
        xresb = [xres0, xres1]
        oq8a = es_p4.enter_context(nc.sbuf_tensor("oq8a", [128, E], I8))
        oq8b = es_p4.enter_context(nc.sbuf_tensor("oq8b", [128, E], I8))
        oq8b_ = [oq8a, oq8b]
        oqbfa = es_p4.enter_context(nc.sbuf_tensor("oqbfa", [128, E], BF16))
        oqbfb = es_p4.enter_context(nc.sbuf_tensor("oqbfb", [128, E], BF16))
        oqbfb_ = [oqbfa, oqbfb]
        oqT = es_p4.enter_context(nc.sbuf_tensor("oqT", [128, E], BF16))
        oproj0 = es_p4.enter_context(nc.sbuf_tensor("oproj0", [128, E], F32))
        oproj1 = es_p4.enter_context(nc.sbuf_tensor("oproj1", [128, E], F32))
        oprojb = [oproj0, oproj1]
        osq = es_p4.enter_context(nc.sbuf_tensor("osq", [128, E], F32))
        lnres0 = es_p4.enter_context(nc.sbuf_tensor("lnres0", [128, E], F32))
        lnres1 = es_p4.enter_context(nc.sbuf_tensor("lnres1", [128, E], F32))
        lnresb = [lnres0, lnres1]
        ptpo2a = es_p4.enter_context(nc.psum_tensor("ptpo2a", [128, 1024], BF16))
        ptpo2b = es_p4.enter_context(nc.psum_tensor("ptpo2b", [128, 1024], BF16))
        po0 = es_p4.enter_context(nc.psum_tensor("po0", [128, 512], F32))
        po1 = es_p4.enter_context(nc.psum_tensor("po1", [128, 512], F32))
        po2 = es_p4.enter_context(nc.psum_tensor("po2", [128, 512], F32))
        po3 = es_p4.enter_context(nc.psum_tensor("po3", [128, 512], F32))
        pob = [po0, po1, po2, po3]
        ptpo2 = [ptpo2a, ptpo2b]

        g_wo_h = [None]

        def ldwo(sync):
            sync.wait_ge(sems["cc"], cc_w)
            sync.wait_ge(sems["vector"], c_p3_v)
            sync.wait_ge(sems["tensor"], c_p3_t)
            sync.wait_ge(sems["scalar"], c_p3_s)
            ins = []
            for r in range(NC):
                src = ag_wo_out.ap()[E * r:E * (r + 1), :].rearrange("(et p) o -> p et o", p=128)
                dst = woT_sb.ap().rearrange("p (et c) -> p et c", c=2048)[:, :, 256 * r:256 * (r + 1)]
                ins.append(sync.dma_start(out=dst, in_=src))
            g_wo_h[0] = pool.issue(ins)
        on("sync", ldwo)
        g_wo = g_wo_h[0]

        NJ = TPC // 128
        g_x1w_h = [None] * NJ
        c_oq_h = [0] * NJ
        c_evoq_h = [0] * NJ
        c_evo_h = [[0] * 4 for _ in range(NJ)]
        c_x1v_h = [0] * NJ

        def p4_pre(j):
            ob = j % 2
            OP = 11 * (j % 2)
            g_oin_h = [None]

            def ldoin(sync, j=j, ob=ob):
                sync.wait_ge(sems["cc"], cc_at)
                if j >= 2:
                    sync.wait_ge(sems["vector"], c_oq_h[j - 2])
                ins = []
                for r in range(NC):
                    ins.append(sync.dma_start(out=oinb[ob][:, 256 * r:256 * (r + 1)],
                                              in_=a2a_out.ap()[TPC * r + 128 * j:
                                                               TPC * r + 128 * (j + 1), :]))
                g_oin_h[0] = pool.issue(ins)
            on("sync", ldoin)
            g_oin = g_oin_h[0]

            def oquant(v, g_oin=g_oin, j=j, ob=ob, OP=OP):
                pool.wait(v, g_oin)
                oin = oinb[ob]
                if DEBUG and j == 0:
                    v.tensor_copy(out=snap["oin"][:], in_=oin[:, 0:512])
                P = lambda k: sm[:, OP + k:OP + k + 1]
                v.tensor_reduce(out=P(OCC), in_=oin[:], axis=AX.X, op=Alu.max,
                                apply_absolute_value=True)
                v.drain()
                v.tensor_scalar(out=P(OCC), in0=P(OCC), scalar1=1.0e-5, scalar2=None, op0=Alu.max)
                v.drain()
                v.reciprocal(out=P(OSX), in_=P(OCC))
                v.drain()
                v.tensor_scalar(out=P(OSX), in0=P(OSX), scalar1=127.0, scalar2=None, op0=Alu.mult)
                v.tensor_scalar(out=P(OINVSX), in0=P(OCC), scalar1=1.0 / 127.0,
                                scalar2=None, op0=Alu.mult)
                v.drain()
                v.tensor_scalar(out=oq8b_[ob][:], in0=oin[:], scalar1=P(OSX),
                                scalar2=None, op0=Alu.mult)
                v.drain()
                inc("vector", v.tensor_copy(out=oqbfb_[ob][:], in_=oq8b_[ob][:]))
            on("vector", oquant)
            c_oq_h[j] = cnt["vector"]

        def p4_mid(j):
            ob = j % 2
            c_oq = c_oq_h[j]
            for g0 in (0, 8):
                pb = g0 // 8

                def tpoq(t, g0=g0, pb=pb, c_oq=c_oq, j=j, ob=ob):
                    t.wait_ge(sems["vector"], c_oq)
                    if j > 0:
                        t.wait_ge(sems["scalar"], c_evoq_h[j - 1])
                    pool.wait(t, g_const)
                    for k in range(8):
                        ins = t.transpose(out=ptpo2[pb][:, 128 * k:128 * (k + 1)],
                                          in_=oqbfb_[ob][:, 128 * (g0 + k):128 * (g0 + k + 1)],
                                          identity=ident[:])
                        if k == 7:
                            inc("tensor", ins)
                on("tensor", tpoq)
                c_t = cnt["tensor"]

                def evoq(s, g0=g0, pb=pb, c_t=c_t):
                    s.wait_ge(sems["tensor"], c_t)
                    sinc(s.copy(out=oqT[:, 128 * g0:128 * (g0 + 8)], in_=ptpo2[pb][:]))
                c_evoq_h[j] = sq(evoq)
            c_evoq = c_evoq_h[j]

            for ch in range(4):
                def mm_o(t, ch=ch, j=j, c_evoq=c_evoq):
                    t.wait_ge(sems["scalar"], c_evoq)
                    pool.wait(t, g_wo)
                    if j > 0:
                        t.wait_ge(sems["vector"], c_evo_h[j - 1][ch])
                    for et in range(ET):
                        ins = t.matmul(out=pob[ch][:], lhsT=oqT[:, 128 * et:128 * (et + 1)],
                                       rhs=woT_sb[:, 2048 * et + 512 * ch:2048 * et + 512 * (ch + 1)],
                                       start=(et == 0), stop=(et == ET - 1))
                        if et == ET - 1:
                            inc("tensor", ins)
                on("tensor", mm_o)
                c_mo = cnt["tensor"]

                def ev_o(v, ch=ch, j=j, c_mo=c_mo, OP=11 * (j % 2)):
                    v.wait_ge(sems["tensor"], c_mo)
                    inc("vector", v.tensor_scalar(out=oprojb[j % 2][:, 512 * ch:512 * (ch + 1)],
                                                  in0=pob[ch][:],
                                                  scalar1=sm[:, OP + OINVSX:OP + OINVSX + 1],
                                                  scalar2=scales_sb[:, SC_O:SC_O + 1],
                                                  op0=Alu.mult, op1=Alu.mult))
                on("vector", ev_o)
                c_evo_h[j][ch] = cnt["vector"]

        def p4_post(j):
            ob = j % 2
            OP = 11 * (j % 2)
            oproj = oprojb[ob]
            c_op = c_evo_h[j][3]
            g_xr_h = [None]

            def ldxr(sync, j=j, ob=ob):
                if j >= 2:
                    sync.wait_ge(sems["vector"], c_x1v_h[j - 2])
                g_xr_h[0] = pool.issue([sync.dma_start(out=xresb[ob][:],
                                                       in_=x_mine[128 * j:128 * (j + 1), :])])
            on("sync", ldxr)
            g_xr = g_xr_h[0]

            def oln_v(v, c_op=c_op, OP=OP, oproj=oproj):
                v.wait_ge(sems["vector"], c_op)
                inc("vector", v.tensor_reduce(out=sm[:, OP + OS1:OP + OS1 + 1], in_=oproj[:],
                                              axis=AX.X, op=Alu.add))
            on("vector", oln_v)
            c_os1 = cnt["vector"]

            def oln_s(s, c_os1=c_os1, OP=OP, oproj=oproj):
                s.wait_ge(sems["vector"], c_os1)
                sinc(s.activation(out=osq[:], in_=oproj[:], func=AF.Square,
                                  accum_out=sm[:, OP + OEX2:OP + OEX2 + 1]))
            c_osq = sq(oln_s)

            def oln_v2(v, c_osq=c_osq, OP=OP):
                v.wait_ge(sems["scalar"], c_osq)
                P = lambda k: sm[:, OP + k:OP + k + 1]
                v.tensor_scalar(out=P(OMEAN), in0=P(OS1), scalar1=1.0 / E, scalar2=None, op0=Alu.mult)
                v.tensor_scalar(out=P(OEX2), in0=P(OEX2), scalar1=1.0 / E, scalar2=None, op0=Alu.mult)
                v.drain()
                v.tensor_tensor(out=P(OM2), in0=P(OMEAN), in1=P(OMEAN), op=Alu.mult)
                v.drain()
                inc("vector", v.tensor_tensor(out=P(OVAR), in0=P(OEX2), in1=P(OM2), op=Alu.subtract))
            on("vector", oln_v2)
            c_ov = cnt["vector"]

            def oln_s2(s, c_ov=c_ov, OP=OP):
                s.wait_ge(sems["vector"], c_ov)
                s.activation(out=sm[:, OP + OSQV:OP + OSQV + 1], in_=sm[:, OP + OVAR:OP + OVAR + 1],
                             func=AF.Ln, bias=eps_t[:, 0:1], scale=1.0)
                s.drain()
                sinc(s.activation(out=sm[:, OP + ORSTD:OP + ORSTD + 1],
                                  in_=sm[:, OP + OSQV:OP + OSQV + 1],
                                  func=AF.Exp, scale=neghalf_t[:, 0:1]))
            c_osqv = sq(oln_s2)

            def oln_v3(v, j=j, c_osqv=c_osqv, ob=ob, OP=OP, oproj=oproj, g_xr=g_xr):
                v.wait_ge(sems["scalar"], c_osqv)
                if j >= 2:
                    pool.wait(v, g_x1w_h[j - 2])
                pool.wait(v, g_xr)
                P = lambda k: sm[:, OP + k:OP + k + 1]
                v.tensor_scalar(out=P(OBT), in0=P(OMEAN), scalar1=P(ORSTD),
                                scalar2=-1.0, op0=Alu.mult, op1=Alu.mult)
                v.drain()
                v.tensor_scalar(out=lnresb[ob][:], in0=oproj[:],
                                scalar1=P(ORSTD), scalar2=P(OBT),
                                op0=Alu.mult, op1=Alu.add)
                v.drain()
                inc("vector", v.tensor_tensor(out=lnresb[ob][:], in0=lnresb[ob][:],
                                              in1=xresb[ob][:], op=Alu.add))
                if DEBUG and j == 0:
                    inc("vector", v.tensor_copy(out=snap["x1"][:], in_=lnresb[0][:, 0:1024]))
            on("vector", oln_v3)
            c_x1 = cnt["vector"]
            c_x1v_h[j] = c_x1

            def stx1(sync, j=j, c_x1=c_x1, ob=ob):
                sync.wait_ge(sems["vector"], c_x1)
                ins = sync.dma_start(out=x1_hbm[128 * j:128 * (j + 1), :], in_=lnresb[ob][:])
                g_x1w_h[j] = pool.issue([ins])
            on("sync", stx1)

        for step in range(NJ + 2):
            if step < NJ:
                p4_pre(step)
            if 1 <= step <= NJ:
                p4_mid(step - 1)
            if step >= 2:
                p4_post(step - 2)
        sflush()
        c_x1_all = cnt["x1w"]
        c_p4_v = cnt["vector"]
        c_p4_t = cnt["tensor"]
        c_p4_s = cnt["scalar"]
        es_p4.close()

        FS1, FMEAN, FEX2, FM2, FVAR, FSQV, FRSTD, FBT = range(31, 39)

        es_p5 = ExitStack()
        x1t = es_p5.enter_context(nc.sbuf_tensor("x1t", [128, E], F32))
        fsq = es_p5.enter_context(nc.sbuf_tensor("fsq", [128, E], F32))
        hin = es_p5.enter_context(nc.sbuf_tensor("hin", [128, E], BF16))
        hinT = es_p5.enter_context(nc.sbuf_tensor("hinT", [128, ET * 512], BF16))
        wug0 = es_p5.enter_context(nc.sbuf_tensor("wug0", [128, 2 * 2048], BF16))
        wug1 = es_p5.enter_context(nc.sbuf_tensor("wug1", [128, 2 * 2048], BF16))
        silu0 = es_p5.enter_context(nc.sbuf_tensor("silu0", [128, 512], F32))
        silu1 = es_p5.enter_context(nc.sbuf_tensor("silu1", [128, 512], F32))
        silub = [silu0, silu1]
        hT = es_p5.enter_context(nc.sbuf_tensor("hT", [128, 32 * 512], BF16))
        wdn0 = es_p5.enter_context(nc.sbuf_tensor("wdn0", [128, 32 * 512], BF16))
        wdn1 = es_p5.enter_context(nc.sbuf_tensor("wdn1", [128, 32 * 512], BF16))
        y0 = es_p5.enter_context(nc.sbuf_tensor("y0", [128, E], F32))
        y1 = es_p5.enter_context(nc.sbuf_tensor("y1", [128, E], F32))
        y2 = es_p5.enter_context(nc.sbuf_tensor("y2", [128, E], F32))
        y3 = es_p5.enter_context(nc.sbuf_tensor("y3", [128, E], F32))
        ptpf = es_p5.enter_context(nc.psum_tensor("ptpf", [128, 1024], BF16))
        pupa = es_p5.enter_context(nc.psum_tensor("pupa", [128, 512], F32))
        pupb = es_p5.enter_context(nc.psum_tensor("pupb", [128, 512], F32))
        pgta = es_p5.enter_context(nc.psum_tensor("pgta", [128, 512], F32))
        pgtb = es_p5.enter_context(nc.psum_tensor("pgtb", [128, 512], F32))
        pup = [pupa, pupb]
        pgt = [pgta, pgtb]
        pd0 = es_p5.enter_context(nc.psum_tensor("pd0", [128, 512], F32))
        pd1 = es_p5.enter_context(nc.psum_tensor("pd1", [128, 512], F32))
        wug = [wug0, wug1]
        wdn = [wdn0, wdn1]
        ysb = [y0, y1, y2, y3]
        pd = [pd0, pd1]

        for grp in range(2):
            c_hinT = 0
            for j2 in range(4):
                j = 4 * grp + j2
                g_x1_h = [None]

                def ldx1(sync, j=j, j2=j2, grp=grp):
                    pool.wait(sync, g_x1w_h[j])
                    if grp == 0 and j2 == 0:
                        sync.wait_ge(sems["vector"], c_p4_v)
                        sync.wait_ge(sems["tensor"], c_p4_t)
                        sync.wait_ge(sems["scalar"], c_p4_s)
                    else:
                        sync.wait_ge(sems["vector"], cnt["vector"])
                    g_x1_h[0] = pool.issue([sync.dma_start(out=x1t[:],
                                                           in_=x1_hbm[128 * j:128 * (j + 1), :])])
                on("sync", ldx1)
                g_x1 = g_x1_h[0]

                def fst_v(v, g_x1=g_x1):
                    pool.wait(v, g_x1)
                    inc("vector", v.tensor_reduce(out=sm[:, FS1:FS1 + 1], in_=x1t[:],
                                                  axis=AX.X, op=Alu.add))
                on("vector", fst_v)
                c_fs1 = cnt["vector"]

                def fst_s(s, c_fs1=c_fs1):
                    s.wait_ge(sems["vector"], c_fs1)
                    sinc(s.activation(out=fsq[:], in_=x1t[:], func=AF.Square,
                                      accum_out=sm[:, FEX2:FEX2 + 1]))
                c_fsq = sq(fst_s)

                def fsm(v, c_fsq=c_fsq):
                    v.wait_ge(sems["scalar"], c_fsq)
                    P = lambda k: sm[:, k:k + 1]
                    v.tensor_scalar(out=P(FMEAN), in0=P(FS1), scalar1=1.0 / E, scalar2=None, op0=Alu.mult)
                    v.tensor_scalar(out=P(FEX2), in0=P(FEX2), scalar1=1.0 / E, scalar2=None, op0=Alu.mult)
                    v.drain()
                    v.tensor_tensor(out=P(FM2), in0=P(FMEAN), in1=P(FMEAN), op=Alu.mult)
                    v.drain()
                    inc("vector", v.tensor_tensor(out=P(FVAR), in0=P(FEX2), in1=P(FM2), op=Alu.subtract))
                on("vector", fsm)
                c_fv = cnt["vector"]

                def fsm_s(s, c_fv=c_fv):
                    s.wait_ge(sems["vector"], c_fv)
                    s.activation(out=sm[:, FSQV:FSQV + 1], in_=sm[:, FVAR:FVAR + 1],
                                 func=AF.Ln, bias=eps_t[:, 0:1], scale=1.0)
                    s.drain()
                    sinc(s.activation(out=sm[:, FRSTD:FRSTD + 1],
                                      in_=sm[:, FSQV:FSQV + 1],
                                      func=AF.Exp, scale=neghalf_t[:, 0:1]))
                c_fsqv = sq(fsm_s)

                def fnorm(v, c_fsqv=c_fsqv):
                    v.wait_ge(sems["scalar"], c_fsqv)
                    P = lambda k: sm[:, k:k + 1]
                    v.tensor_scalar(out=P(FBT), in0=P(FMEAN), scalar1=P(FRSTD),
                                    scalar2=-1.0, op0=Alu.mult, op1=Alu.mult)
                    v.drain()
                    inc("vector", v.tensor_scalar(out=hin[:], in0=x1t[:],
                                                  scalar1=P(FRSTD), scalar2=P(FBT),
                                                  op0=Alu.mult, op1=Alu.add))
                on("vector", fnorm)
                c_hin = cnt["vector"]

                for g0 in (0, 8):
                    def tph(t, g0=g0, c_hin=c_hin):
                        t.wait_ge(sems["vector"], c_hin)
                        t.wait_ge(sems["vector"], cnt["vector"])
                        pool.wait(t, g_const)
                        for k in range(8):
                            ins = t.transpose(out=ptpf[:, 128 * k:128 * (k + 1)],
                                              in_=hin[:, 128 * (g0 + k):128 * (g0 + k + 1)],
                                              identity=ident[:])
                            if k == 7:
                                inc("tensor", ins)
                    on("tensor", tph)
                    c_tph = cnt["tensor"]

                    def evh(v, g0=g0, j2=j2, c_tph=c_tph):
                        v.wait_ge(sems["tensor"], c_tph)
                        dst = hinT.ap().rearrange("p (et c) -> p et c", c=512)[:, g0:g0 + 8,
                                                                               128 * j2:128 * (j2 + 1)]
                        src = ptpf.ap().rearrange("p (et c) -> p et c", c=128)
                        inc("vector", v.tensor_copy(out=dst, in_=src))
                    on("vector", evh)
                c_hinT = cnt["vector"]

            for xh in range(2):
                c_silu_done = [cnt["scalar"], cnt["scalar"]]
                c_doh_done = [cnt["vector"], cnt["vector"]]
                c_mug_h = [cnt["tensor"], cnt["tensor"]]
                for ebl in range(32):
                    eb = 32 * xh + ebl
                    wb = eb % 2
                    r, ebr = eb // 8, eb % 8
                    g_ug_h = [None]

                    def ldug(sync, r=r, ebr=ebr, wb=wb, eb=eb, ebl=ebl):
                        sync.wait_ge(sems["cc"], cc_w)
                        if c_mug_h[ebl % 2]:
                            sync.wait_ge(sems["tensor"], c_mug_h[ebl % 2])
                        ins = []
                        for idx, agt in enumerate((ag_up_out, ag_gate_out)):
                            src = agt.ap()[E * r:E * (r + 1), 128 * ebr:128 * (ebr + 1)] \
                                .rearrange("(et p) o -> p et o", p=128)
                            dst = wug[wb].ap().rearrange("p (et c) -> p et c", c=128) \
                                [:, 16 * idx:16 * idx + 16, :]
                            ins.append(sync.dma_start(out=dst, in_=src))
                        g_ug_h[0] = pool.issue(ins)
                    on("sync", ldug)
                    g_ug = g_ug_h[0]

                    ps = ebl % 2
                    def mm_ug(t, wb=wb, g_ug=g_ug, c_hinT=c_hinT, eb=eb, ps=ps,
                              c_sfree=c_silu_done[ebl % 2], c_vfree=c_doh_done[ebl % 2]):
                        pool.wait(t, g_ug)
                        t.wait_ge(sems["vector"], c_hinT)
                        if c_sfree:
                            t.wait_ge(sems["scalar"], c_sfree)
                        if c_vfree:
                            t.wait_ge(sems["vector"], c_vfree)
                        for et in range(ET):
                            t.matmul(out=pup[ps][:], lhsT=wug[wb][:, 128 * et:128 * (et + 1)],
                                     rhs=hinT[:, 512 * et:512 * (et + 1)],
                                     start=(et == 0), stop=(et == ET - 1))
                        for et in range(ET):
                            ins = t.matmul(out=pgt[ps][:], lhsT=wug[wb][:, 2048 + 128 * et:2048 + 128 * (et + 1)],
                                           rhs=hinT[:, 512 * et:512 * (et + 1)],
                                           start=(et == 0), stop=(et == ET - 1))
                            if et == ET - 1:
                                inc("tensor", ins)
                    on("tensor", mm_ug)
                    c_mug = cnt["tensor"]
                    c_mug_h[ebl % 2] = c_mug

                    def dosilu(s, c_mug=c_mug, ps=ps, sb_=ebl % 2):
                        s.wait_ge(sems["tensor"], c_mug)
                        sinc(s.activation(out=silub[sb_][:], in_=pup[ps][:], func=AF.Silu,
                                          scale=scales_sb[:, SC_UP:SC_UP + 1]))
                    c_sil = sq(dosilu)
                    c_silu_done[ebl % 2] = c_sil

                    def doh(v, ebl=ebl, c_sil=c_sil, ps=ps, sb_=ebl % 2, grp=grp, eb=eb):
                        v.wait_ge(sems["scalar"], c_sil)
                        inc("vector", v.scalar_tensor_tensor(
                            out=hT[:, 512 * ebl:512 * (ebl + 1)], in0=pgt[ps][:],
                            scalar=scales_sb[:, SC_GATE:SC_GATE + 1], in1=silub[sb_][:],
                            op0=Alu.mult, op1=Alu.mult))
                        if DEBUG and grp == 0 and eb == 0:
                            v.drain()
                            inc("vector", v.tensor_copy(out=snap["h"][:], in_=hT[:, 0:512]))
                    on("vector", doh)
                    c_doh_done[ebl % 2] = cnt["vector"]
                c_h_ready = cnt["vector"]

                c_evd_done = [cnt["vector"], cnt["vector"]]
                c_md_h = [cnt["tensor"], cnt["tensor"]]
                for ch in range(4):
                    wb = ch % 2
                    g_dn_h = [None]

                    def lddn(sync, ch=ch, xh=xh, wb=wb):
                        sync.wait_ge(sems["cc"], cc_w)
                        if c_md_h[ch % 2]:
                            sync.wait_ge(sems["tensor"], c_md_h[ch % 2])
                        ins = []
                        for half in range(2):
                            rr = 2 * ch + half
                            src = ag_dn_out.ap()[DE * rr + 4096 * xh:DE * rr + 4096 * (xh + 1), :] \
                                .rearrange("(xt p) d -> p xt d", p=128)
                            dst = wdn[wb].ap().rearrange("p (xt c) -> p xt c", c=512) \
                                [:, :, 256 * half:256 * (half + 1)]
                            ins.append(sync.dma_start(out=dst, in_=src))
                        g_dn_h[0] = pool.issue(ins)
                    on("sync", lddn)
                    g_dn = g_dn_h[0]

                    for j2 in range(4):
                        pb = j2 % 2

                        def mm_d(t, j2=j2, pb=pb, wb=wb, g_dn=g_dn, c_h_ready=c_h_ready,
                                 c_free=c_evd_done[pb]):
                            pool.wait(t, g_dn)
                            t.wait_ge(sems["vector"], c_h_ready)
                            if c_free:
                                t.wait_ge(sems["vector"], c_free)
                            for xt in range(32):
                                ins = t.matmul(out=pd[pb][:],
                                               lhsT=hT[:, 512 * xt + 128 * j2:512 * xt + 128 * (j2 + 1)],
                                               rhs=wdn[wb][:, 512 * xt:512 * (xt + 1)],
                                               start=(xt == 0), stop=(xt == 31))
                                if xt == 31:
                                    inc("tensor", ins)
                        on("tensor", mm_d)
                        c_md = cnt["tensor"]
                        c_md_h[ch % 2] = c_md

                        def ev_d(v, j2=j2, pb=pb, ch=ch, xh=xh, c_md=c_md):
                            v.wait_ge(sems["tensor"], c_md)
                            if xh == 0:
                                ins = v.tensor_scalar(out=ysb[j2][:, 512 * ch:512 * (ch + 1)],
                                                      in0=pd[pb][:],
                                                      scalar1=scales_sb[:, SC_DOWN:SC_DOWN + 1],
                                                      scalar2=None, op0=Alu.mult)
                            else:
                                ins = v.scalar_tensor_tensor(out=ysb[j2][:, 512 * ch:512 * (ch + 1)],
                                                             in0=pd[pb][:],
                                                             scalar=scales_sb[:, SC_DOWN:SC_DOWN + 1],
                                                             in1=ysb[j2][:, 512 * ch:512 * (ch + 1)],
                                                             op0=Alu.mult, op1=Alu.add)
                            inc("vector", ins)
                        on("vector", ev_d)
                        c_evd_done[pb] = cnt["vector"]
            c_y = cnt["vector"]

            for j2 in range(4):
                j = 4 * grp + j2
                yt = ysb[j2]

                def yln_v(v, yt=yt, c_y=c_y):
                    v.wait_ge(sems["vector"], c_y)
                    inc("vector", v.tensor_reduce(out=sm[:, FS1:FS1 + 1], in_=yt[:],
                                                  axis=AX.X, op=Alu.add))
                on("vector", yln_v)
                c_ys1 = cnt["vector"]

                def yln_s(s, yt=yt, c_ys1=c_ys1):
                    s.wait_ge(sems["vector"], c_ys1)
                    sinc(s.activation(out=fsq[:], in_=yt[:], func=AF.Square,
                                      accum_out=sm[:, FEX2:FEX2 + 1]))
                c_ysq = sq(yln_s)

                def yln_v2(v, c_ysq=c_ysq):
                    v.wait_ge(sems["scalar"], c_ysq)
                    P = lambda k: sm[:, k:k + 1]
                    v.tensor_scalar(out=P(FMEAN), in0=P(FS1), scalar1=1.0 / E, scalar2=None, op0=Alu.mult)
                    v.tensor_scalar(out=P(FEX2), in0=P(FEX2), scalar1=1.0 / E, scalar2=None, op0=Alu.mult)
                    v.drain()
                    v.tensor_tensor(out=P(FM2), in0=P(FMEAN), in1=P(FMEAN), op=Alu.mult)
                    v.drain()
                    inc("vector", v.tensor_tensor(out=P(FVAR), in0=P(FEX2), in1=P(FM2), op=Alu.subtract))
                on("vector", yln_v2)
                c_yv = cnt["vector"]

                def yln_s2(s, c_yv=c_yv):
                    s.wait_ge(sems["vector"], c_yv)
                    s.activation(out=sm[:, FSQV:FSQV + 1], in_=sm[:, FVAR:FVAR + 1],
                                 func=AF.Ln, bias=eps_t[:, 0:1], scale=1.0)
                    s.drain()
                    sinc(s.activation(out=sm[:, FRSTD:FRSTD + 1],
                                      in_=sm[:, FSQV:FSQV + 1],
                                      func=AF.Exp, scale=neghalf_t[:, 0:1]))
                c_ysqv = sq(yln_s2)

                def ynorm(v, yt=yt, c_ysqv=c_ysqv):
                    v.wait_ge(sems["scalar"], c_ysqv)
                    P = lambda k: sm[:, k:k + 1]
                    v.tensor_scalar(out=P(FBT), in0=P(FMEAN), scalar1=P(FRSTD),
                                    scalar2=-1.0, op0=Alu.mult, op1=Alu.mult)
                    v.drain()
                    inc("vector", v.tensor_scalar(out=fsq[:], in0=yt[:],
                                                  scalar1=P(FRSTD), scalar2=P(FBT),
                                                  op0=Alu.mult, op1=Alu.add))
                on("vector", ynorm)
                c_yn = cnt["vector"]

                g_x1b_h = [None]

                def ldx1b(sync, j=j, c_yn=c_yn):
                    sync.wait_ge(sems["vector"], c_yn)
                    pool.wait(sync, g_x1w_h[j])
                    g_x1b_h[0] = pool.issue([sync.dma_start(out=x1t[:],
                                                            in_=x1_hbm[128 * j:128 * (j + 1), :])])
                on("sync", ldx1b)
                g_x1b = g_x1b_h[0]

                def yres(v, yt=yt, j=j, g_x1b=g_x1b):
                    pool.wait(v, g_x1b)
                    inc("vector", v.tensor_tensor(out=yt[:], in0=fsq[:], in1=x1t[:], op=Alu.add))
                    if DEBUG and j == 0:
                        inc("vector", v.tensor_copy(out=snap["y"][:], in_=yt[:, 0:1024]))
                on("vector", yres)
                c_out = cnt["vector"]

                def stout(sync, j=j, yt=yt, c_out=c_out):
                    sync.wait_ge(sems["vector"], c_out)
                    inc("out", sync.dma_start(out=out_ext[128 * j:128 * (j + 1), :], in_=yt[:]), 16)
                on("sync", stout)
        sflush()
        es_p5.close()

        if DEBUG:
            def ddump(sync):
                sync.wait_ge(sems["vector"], cnt["vector"])
                for nm_s, nm_d in [("sm", "dbg_sm"), ("xq", "dbg_xq"),
                                   ("ao", "dbg_attnout"),
                                   ("rec", "dbg_sums"), ("oin", "dbg_oin"),
                                   ("x1", "dbg_x1"), ("h", "dbg_h"), ("y", "dbg_y")]:
                    inc("out", sync.dma_start(out=dbg[nm_d], in_=snap[nm_s][:]), 16)
            on("sync", ddump)

        def fin(sync):
            sync.wait_ge(sems["out"], cnt["out"])
        on("sync", fin)

    return nc


_cache = {}


def _host_consts():
    t = np.arange(128)[:, None]
    j = np.arange(896)[None, :]
    cmask = np.where((j - 384) >= t, 0.0, MASKV).astype(np.float32)
    ident = np.eye(128, dtype=ml_dtypes.bfloat16)
    return cmask, ident


def kernel(L0_attn_x, L0_attn_w_q, L0_attn_w_k, L0_attn_w_v, L0_attn_w_o,
           L0_ffn_W_up, L0_ffn_W_gate, L0_ffn_W_down):
    if "nc" not in _cache:
        _cache["nc"] = build_program()
    nc = _cache["nc"]

    x2d = np.ascontiguousarray(np.asarray(L0_attn_x, dtype=np.float32).reshape(TOK, E))
    wq = np.ascontiguousarray(np.asarray(L0_attn_w_q, dtype=np.float32))
    wk = np.ascontiguousarray(np.asarray(L0_attn_w_k, dtype=np.float32))
    wv = np.ascontiguousarray(np.asarray(L0_attn_w_v, dtype=np.float32))
    wo = np.ascontiguousarray(np.asarray(L0_attn_w_o, dtype=np.float32))
    wup = np.ascontiguousarray(np.asarray(L0_ffn_W_up, dtype=np.float32))
    wgate = np.ascontiguousarray(np.asarray(L0_ffn_W_gate, dtype=np.float32))
    wdown = np.ascontiguousarray(np.asarray(L0_ffn_W_down, dtype=np.float32))
    cmask, ident = _host_consts()

    in_maps = []
    for c in range(NC):
        in_maps.append({
            "x_full": x2d,
            "x_mine": np.ascontiguousarray(x2d[TPC * c:TPC * (c + 1)]),
            "wq_s": np.ascontiguousarray(wq[256 * c:256 * (c + 1)]),
            "wk_s": np.ascontiguousarray(wk[256 * c:256 * (c + 1)]),
            "wv_s": np.ascontiguousarray(wv[256 * c:256 * (c + 1)]),
            "wo_s": np.ascontiguousarray(wo[256 * c:256 * (c + 1)]),
            "up_s": np.ascontiguousarray(wup[1024 * c:1024 * (c + 1)]),
            "gate_s": np.ascontiguousarray(wgate[1024 * c:1024 * (c + 1)]),
            "down_s": np.ascontiguousarray(wdown[256 * c:256 * (c + 1)]),
            "cmask_in": cmask,
            "ident_in": ident,
        })
    res = run_bass_kernel_spmd(nc, in_maps, core_ids=list(range(NC)))
    if DEBUG:
        _cache["results"] = res
    out = np.concatenate([np.asarray(res.results[c]["out"]) for c in range(NC)], axis=0)
    return out.reshape(B, S, E).astype(np.float32)


# version 16
# speedup vs baseline: 1.1520x; 1.0087x over previous
import math
from contextlib import ExitStack

import numpy as np
import ml_dtypes

import concourse.bass as bass
import concourse.mybir as mybir
from concourse.alu_op_type import AluOpType
from concourse.bass_utils import run_bass_kernel_spmd
from concourse.aot_env import allow_ici_all_to_all

F32 = mybir.dt.float32
BF16 = mybir.dt.bfloat16
I8 = mybir.dt.int8
AF = mybir.ActivationFunctionType
AX = mybir.AxisListType
Alu = AluOpType

B, S, E, H, D, DE = 2, 4096, 2048, 16, 128, 8192
TOK = B * S
NC = 8
TPC = TOK // NC
NT = TOK // 128
NTB = S // 128
ET = E // 128
XT = DE // 128
MASKV = -1.0e5

RC_Q, RC_K, RC_V, RC_O, RC_UP, RC_GATE, RC_DOWN = range(7)
SC_Q, SC_K, SC_V, SC_O, SC_UP, SC_GATE, SC_DOWN = range(7, 14)

DEBUG = False


class DmaGroup:
    __slots__ = ("sem", "target")

    def __init__(self, sem, target):
        self.sem = sem
        self.target = target


class DmaPool:

    def __init__(self, nc, es, n=24):
        self.sems = [es.enter_context(nc.semaphore(f"dpool{i}")) for i in range(n)]
        self.tot = [0] * n
        self.idx = 0

    def issue(self, instrs):
        i = self.idx % len(self.sems)
        self.idx += 1
        for ins in instrs:
            ins.then_inc(self.sems[i], 16)
            self.tot[i] += 16
        return DmaGroup(self.sems[i], self.tot[i])

    def wait(self, eng, grp):
        eng.wait_ge(grp.sem, grp.target)


def build_program():
    nc = bass.Bass("TRN2", target_bir_lowering=False)

    x_full = nc.dram_tensor("x_full", [TOK, E], F32, kind="ExternalInput").ap()
    x_mine = nc.dram_tensor("x_mine", [TPC, E], F32, kind="ExternalInput").ap()
    wq_s = nc.dram_tensor("wq_s", [256, E], F32, kind="ExternalInput").ap()
    wk_s = nc.dram_tensor("wk_s", [256, E], F32, kind="ExternalInput").ap()
    wv_s = nc.dram_tensor("wv_s", [256, E], F32, kind="ExternalInput").ap()
    wo_s = nc.dram_tensor("wo_s", [256, E], F32, kind="ExternalInput").ap()
    up_s = nc.dram_tensor("up_s", [1024, E], F32, kind="ExternalInput").ap()
    gate_s = nc.dram_tensor("gate_s", [1024, E], F32, kind="ExternalInput").ap()
    down_s = nc.dram_tensor("down_s", [256, DE], F32, kind="ExternalInput").ap()
    cmask_in = nc.dram_tensor("cmask_in", [128, 896], F32, kind="ExternalInput").ap()
    ident_in = nc.dram_tensor("ident_in", [128, 128], BF16, kind="ExternalInput").ap()
    out_ext = nc.dram_tensor("out", [TPC, E], F32, kind="ExternalOutput").ap()

    dbg = {}
    if DEBUG:
        for nm, sh, dt in [("dbg_scales", [1, 16], F32), ("dbg_xq", [128, E], I8),
                           ("dbg_sm", [128, 44], F32), ("dbg_partials", [128, 8], F32),
                           ("dbg_qk", [128, 512], BF16), ("dbg_v", [128, 256], BF16),
                           ("dbg_attnout", [128, 256], BF16), ("dbg_oin", [128, 512], BF16),
                           ("dbg_x1", [128, 1024], BF16), ("dbg_h", [128, 512], BF16),
                           ("dbg_y", [128, 1024], BF16), ("dbg_sums", [128, 32], F32)]:
            dbg[nm] = nc.dram_tensor(nm, sh, dt, kind="ExternalOutput").ap()

    pt_hbm = nc.dram_tensor("pt_hbm", [128, 8], F32).ap()
    ar_in = nc.dram_tensor("ar_in", [1, 8], F32)
    ar_out = nc.dram_tensor("ar_out", [1, 8], F32)
    sc_hbm = nc.dram_tensor("sc_hbm", [1, 16], F32).ap()
    ag_wo_in = nc.dram_tensor("ag_wo_in", [E, 256], BF16)
    ag_wo_out = nc.dram_tensor("ag_wo_out", [NC * E, 256], BF16, addr_space="Shared")
    ag_up_in = nc.dram_tensor("ag_up_in", [E, 1024], BF16)
    ag_up_out = nc.dram_tensor("ag_up_out", [NC * E, 1024], BF16, addr_space="Shared")
    ag_gate_in = nc.dram_tensor("ag_gate_in", [E, 1024], BF16)
    ag_gate_out = nc.dram_tensor("ag_gate_out", [NC * E, 1024], BF16, addr_space="Shared")
    ag_dn_in = nc.dram_tensor("ag_dn_in", [DE, 256], BF16)
    ag_dn_out = nc.dram_tensor("ag_dn_out", [NC * DE, 256], BF16, addr_space="Shared")
    a2a_in = nc.dram_tensor("a2a_in", [TOK, 256], BF16)
    a2a_out = nc.dram_tensor("a2a_out", [TOK, 256], BF16)
    sums_bounce = nc.dram_tensor("sums_bounce", [1, S], BF16).ap()
    x1_hbm = nc.dram_tensor("x1_hbm", [TPC, E], F32).ap()

    rg = [list(range(NC))]

    es = ExitStack()
    with es:
        block = es.enter_context(nc.Block())
        sems = {n: es.enter_context(nc.semaphore("sem_" + n))
                for n in ["vector", "scalar", "tensor", "cc",
                          "ag", "out", "x1w", "a2aw"]}
        cnt = {n: 0 for n in sems}
        pool = DmaPool(nc, es)

        def inc(n, ins, k=1):
            ins.then_inc(sems[n], k)
            cnt[n] += k
            return cnt[n]

        def on(name, f):
            getattr(block, name)(f)

        squeue = []

        def sinc(ins, k=1):
            ins.then_inc(sems["scalar"], k)
            return ins

        def sq(fn, incs=1):
            squeue.append(fn)
            cnt["scalar"] += incs
            return cnt["scalar"]

        def sflush():
            if not squeue:
                return
            fns = list(squeue)
            squeue.clear()

            def run(s):
                for f in fns:
                    f(s)
            on("scalar", run)

        SB = lambda name, shape, dt: es.enter_context(nc.sbuf_tensor(name, shape, dt))

        scales_sb = SB("scales_sb", [128, 16], F32)
        eps_t = SB("eps_t", [128, 1], F32)
        neghalf_t = SB("neghalf_t", [128, 1], F32)
        ones_bf = SB("ones_bf", [128, 1], BF16)
        ident = SB("ident", [128, 128], BF16)
        cmask = SB("cmask", [128, 896], F32)
        sm = SB("smalls", [128, 44], F32)

        if DEBUG:
            snap = {
                "sm": SB("sn_sm", [128, 44], F32), "xq": SB("sn_xq", [128, E], I8),
                "ao": SB("sn_ao", [128, 256], BF16),
                "rec": SB("sn_rec", [128, NTB], F32), "oin": SB("sn_oin", [128, 512], BF16),
                "x1": SB("sn_x1", [128, 1024], BF16), "h": SB("sn_h", [128, 512], BF16),
                "y": SB("sn_y", [128, 1024], BF16),
            }

        es_a = ExitStack()
        wqkvT = es_a.enter_context(nc.sbuf_tensor("wqkvT", [128, ET * 768], BF16))

        g_const_h = [None]

        def p0s(sync):
            i1 = sync.dma_start(out=ident[:], in_=ident_in)
            i2 = sync.dma_start(out=cmask[:], in_=cmask_in)
            g_const_h[0] = pool.issue([i1, i2])
        on("sync", p0s)
        g_const = g_const_h[0]

        def p0v(v):
            v.memset(eps_t[:], 1.0e-5)
            v.memset(neghalf_t[:], -0.5)
            inc("vector", v.memset(ones_bf[:], 1.0))
        on("vector", p0v)

        es_p1 = ExitStack()
        wt0 = es_p1.enter_context(nc.sbuf_tensor("wt0", [128, DE], F32))
        wt1 = es_p1.enter_context(nc.sbuf_tensor("wt1", [128, DE], F32))
        wt2 = es_p1.enter_context(nc.sbuf_tensor("wt2", [128, DE], F32))
        partials = es_p1.enter_context(nc.sbuf_tensor("partials", [128, 8], F32))
        ptmp = es_p1.enter_context(nc.sbuf_tensor("ptmp", [128, 1], F32))
        pt7 = es_p1.enter_context(nc.sbuf_tensor("pt7", [8, 128], F32))
        pt7s = es_p1.enter_context(nc.sbuf_tensor("pt7s", [8, 1], F32))
        srow = es_p1.enter_context(nc.sbuf_tensor("srow", [1, 16], F32))
        q8t0 = es_p1.enter_context(nc.sbuf_tensor("q8t0", [128, DE], I8))
        q8t1 = es_p1.enter_context(nc.sbuf_tensor("q8t1", [128, DE], I8))
        qbft0 = es_p1.enter_context(nc.sbuf_tensor("qbft0", [128, DE], BF16))
        qbft1 = es_p1.enter_context(nc.sbuf_tensor("qbft1", [128, DE], BF16))
        q8t = [q8t0, q8t1]
        qbft = [qbft0, qbft1]
        stage0 = es_p1.enter_context(nc.sbuf_tensor("stage0", [128, DE], BF16))
        stage1 = es_p1.enter_context(nc.sbuf_tensor("stage1", [128, DE], BF16))
        stageb = [stage0, stage1]
        ptpw0 = es_p1.enter_context(nc.psum_tensor("ptpw0", [128, 1024], BF16))
        ptpw1 = es_p1.enter_context(nc.psum_tensor("ptpw1", [128, 1024], BF16))
        wt = [wt0, wt1, wt2]
        ptpw = [ptpw0, ptpw1]

        wslices = [
            (wq_s, 256, E, RC_Q), (wk_s, 256, E, RC_K), (wv_s, 256, E, RC_V),
            (wo_s, 256, E, RC_O), (up_s, 1024, E, RC_UP),
            (gate_s, 1024, E, RC_GATE), (down_s, 256, DE, RC_DOWN),
        ]

        vdone = []
        gload = []
        i = 0
        for (ap, rows, cols, rc) in wslices:
            for r0 in range(0, rows, 128):
                buf = i % 3
                ii = i

                def ldw(q, ap=ap, r0=r0, buf=buf, ii=ii, cols=cols):
                    if ii >= 3:
                        q.wait_ge(sems["vector"], vdone[ii - 3])
                    ins = q.dma_start(out=wt[buf][:, 0:cols], in_=ap[r0:r0 + 128, :])
                    gload.append(pool.issue([ins]))
                on("sync" if ii % 2 == 0 else "scalar", ldw)

                def absr(v, rc=rc, buf=buf, cols=cols, ii=ii, first=(r0 == 0)):
                    pool.wait(v, gload[ii])
                    v.tensor_reduce(out=ptmp[:], in_=wt[buf][:, 0:cols],
                                    axis=AX.X, op=Alu.add, apply_absolute_value=True)
                    v.drain()
                    if first:
                        c = inc("vector", v.tensor_copy(out=partials[:, rc:rc + 1], in_=ptmp[:]))
                    else:
                        c = inc("vector", v.tensor_tensor(out=partials[:, rc:rc + 1],
                                                          in0=partials[:, rc:rc + 1],
                                                          in1=ptmp[:], op=Alu.add))
                    vdone.append(c)
                on("vector", absr)
                i += 1
        c_passA = cnt["vector"]

        g_pt = [None, None]

        def pr1(sync):
            sync.wait_ge(sems["vector"], c_passA)
            g_pt[0] = pool.issue([sync.dma_start(out=pt_hbm, in_=partials[:])])
            pool.wait(sync, g_pt[0])
            g_pt[1] = pool.issue([sync.dma_start(
                out=pt7[:], in_=pt_hbm.rearrange("p t -> t p"),
                allow_slow_non_contiguous=True)])
        on("sync", pr1)

        def pr2(v):
            pool.wait(v, g_pt[1])
            inc("vector", v.tensor_reduce(out=pt7s[:], in_=pt7[:], axis=AX.X, op=Alu.add))
        on("vector", pr2)
        c_pr2 = cnt["vector"]

        g_arin_h = [None]

        def pr3(sync):
            sync.wait_ge(sems["vector"], c_pr2)
            dst = ar_in.ap().rearrange("x (t o) -> (x t) o", o=1)[0:8, :]
            g_arin_h[0] = pool.issue([sync.dma_start(out=dst, in_=pt7s[:],
                                                     allow_slow_non_contiguous=True)])
        on("sync", pr3)
        g_arin = g_arin_h[0]

        def cc1(g):
            pool.wait(g, g_arin)
            inc("cc", g.collective_compute("AllReduce", Alu.add, replica_groups=rg,
                                           ins=[ar_in.ap().opt()], outs=[ar_out.ap().opt()]))
        on("gpsimd", cc1)
        c_ar = cnt["cc"]

        numel = {RC_Q: E * E, RC_K: E * E, RC_V: E * E, RC_O: E * E,
                 RC_UP: DE * E, RC_GATE: DE * E, RC_DOWN: E * DE}
        g_srow_h = [None]

        def sc1(sync):
            sync.wait_ge(sems["cc"], c_ar)
            g_srow_h[0] = pool.issue([sync.dma_start(out=srow[0:1, 0:8],
                                                     in_=ar_out.ap()[0:1, 0:8])])
        on("sync", sc1)
        g_srow = g_srow_h[0]

        if DEBUG:
            def dpart(sync):
                sync.wait_ge(sems["vector"], c_passA)
                inc("out", sync.dma_start(out=dbg["dbg_partials"], in_=partials[:]), 16)
            on("sync", dpart)

        def sc2(v):
            pool.wait(v, g_srow)
            for k in range(7):
                v.tensor_scalar(out=srow[0:1, k:k + 1], in0=srow[0:1, k:k + 1],
                                scalar1=1.0 / numel[k], scalar2=None, op0=Alu.mult)
                v.drain()
            for k in (RC_Q, RC_K, RC_V, RC_O):
                v.tensor_scalar(out=srow[0:1, k:k + 1], in0=srow[0:1, k:k + 1],
                                scalar1=1.0e-5, scalar2=None, op0=Alu.max)
                v.drain()
            for k in (RC_UP, RC_GATE, RC_DOWN):
                v.tensor_scalar(out=srow[0:1, k:k + 1], in0=srow[0:1, k:k + 1],
                                scalar1=1.0e-8, scalar2=None, op0=Alu.add)
                v.drain()
            v.tensor_scalar(out=srow[0:1, SC_Q:SC_Q + 1], in0=srow[0:1, RC_Q:RC_Q + 1],
                            scalar1=1.0 / math.sqrt(D), scalar2=None, op0=Alu.mult)
            v.drain()
            for s_c, d_c in ((RC_K, SC_K), (RC_V, SC_V), (RC_O, SC_O),
                             (RC_UP, SC_UP), (RC_GATE, SC_GATE), (RC_DOWN, SC_DOWN)):
                v.tensor_copy(out=srow[0:1, d_c:d_c + 1], in_=srow[0:1, s_c:s_c + 1])
            v.drain()
            inc("vector", v.reciprocal(out=srow[0:1, 0:7], in_=srow[0:1, 0:7]))
        on("vector", sc2)
        c_sc2 = cnt["vector"]

        g_scales_h = [None]

        def sc3(sync):
            sync.wait_ge(sems["vector"], c_sc2)
            g1 = pool.issue([sync.dma_start(out=sc_hbm, in_=srow[:])])
            pool.wait(sync, g1)
            g_scales_h[0] = pool.issue([sync.dma_start(
                out=scales_sb[:], in_=sc_hbm[0:1, 0:16].partition_broadcast(128))])
            if DEBUG:
                inc("out", sync.dma_start(out=dbg["dbg_scales"], in_=srow[:]), 16)
        on("sync", sc3)
        g_scales = g_scales_h[0]

        qkv_off = {id(wq_s): 0, id(wk_s): 256, id(wv_s): 512}
        wjobs = [
            (wq_s, 256, E, RC_Q, None), (wk_s, 256, E, RC_K, None),
            (wv_s, 256, E, RC_V, None), (wo_s, 256, E, RC_O, ag_wo_in),
            (up_s, 1024, E, RC_UP, ag_up_in), (gate_s, 1024, E, RC_GATE, ag_gate_in),
            (down_s, 256, DE, RC_DOWN, ag_dn_in),
        ]
        ji = 0
        qdone = []
        gloadb = []
        tpfree = [0, 0]
        tpw_hist = [0] * 32
        qcast_hist = [0] * 32
        stg_ag = [0, 0]
        stg_vec = [0, 0]
        for (ap, rows, cols, rc, dest) in wjobs:
            nets = cols // 128
            for j in range(rows // 128):
                buf = ji % 3
                jj = ji

                def ldq(q, ap=ap, j=j, buf=buf, jj=jj, cols=cols):
                    q.wait_ge(sems["vector"], c_passA)
                    if jj >= 3:
                        q.wait_ge(sems["vector"], qdone[jj - 3])
                    ins = q.dma_start(out=wt[buf][:, 0:cols],
                                      in_=ap[j * 128:(j + 1) * 128, :])
                    gloadb.append(pool.issue([ins]))
                on("sync" if jj % 2 == 0 else "scalar", ldq)

                def qn(v, rc=rc, buf=buf, cols=cols, jj=jj):
                    pool.wait(v, gloadb[jj])
                    pool.wait(v, g_scales)
                    if jj > 1:
                        v.wait_ge(sems["scalar"], qcast_hist[jj - 2])
                    v.tensor_scalar(out=wt[buf][:, 0:cols], in0=wt[buf][:, 0:cols],
                                    scalar1=scales_sb[:, rc:rc + 1], scalar2=1.0,
                                    op0=Alu.mult, op1=Alu.min)
                    v.drain()
                    c = inc("vector", v.tensor_scalar(out=q8t[jj % 2][:, 0:cols],
                                                      in0=wt[buf][:, 0:cols],
                                                      scalar1=-1.0, scalar2=None, op0=Alu.max))
                    qdone.append(c)
                on("vector", qn)
                c_q8w = cnt["vector"]

                def qcastw(s, cols=cols, c_q8w=c_q8w, jj=jj):
                    s.wait_ge(sems["vector"], c_q8w)
                    if jj > 1:
                        s.wait_ge(sems["tensor"], tpw_hist[jj - 2])
                    inc("scalar", s.copy(out=qbft[jj % 2][:, 0:cols], in_=q8t[jj % 2][:, 0:cols]))
                on("scalar", qcastw)
                c_q = cnt["scalar"]
                qcast_hist[jj] = c_q

                for g0 in range(0, nets, 8):
                    gN = min(8, nets - g0)
                    pb = (g0 // 8) % 2

                    def tpw(t, g0=g0, gN=gN, pb=pb, c_q=c_q, sb_=ji % 2, qb_=jj % 2):
                        t.wait_ge(sems["scalar"], c_q)
                        t.wait_ge(sems["scalar"], tpfree[pb])
                        t.wait_ge(sems["ag"], stg_ag[sb_])
                        t.wait_ge(sems["vector"], stg_vec[sb_])
                        pool.wait(t, g_const)
                        for k in range(gN):
                            ins = t.transpose(out=ptpw[pb][:, 128 * k:128 * (k + 1)],
                                              in_=qbft[qb_][:, 128 * (g0 + k):128 * (g0 + k + 1)],
                                              identity=ident[:])
                            if k == gN - 1:
                                inc("tensor", ins)
                    on("tensor", tpw)
                    c_tp = cnt["tensor"]

                    def evw(s, g0=g0, gN=gN, pb=pb, c_tp=c_tp, sb_=ji % 2):
                        s.wait_ge(sems["tensor"], c_tp)
                        inc("scalar", s.copy(out=stageb[sb_][:, 128 * g0:128 * (g0 + gN)],
                                             in_=ptpw[pb][:, 0:128 * gN]))
                    on("scalar", evw)
                    tpfree[pb] = cnt["scalar"]
                tpw_hist[jj] = cnt["tensor"]
                c_stage = cnt["scalar"]

                if dest is None:
                    off = qkv_off[id(ap)] + 128 * j

                    def wqkv(v, off=off, nets=nets, c_stage=c_stage, sb_=ji % 2):
                        v.wait_ge(sems["scalar"], c_stage)
                        dst = wqkvT.ap().rearrange("p (et c) -> p et c", c=768)[:, :, off:off + 128]
                        src = stageb[sb_].ap().rearrange("p (et c) -> p et c", c=128)[:, 0:nets, :]
                        c = inc("vector", v.tensor_copy(out=dst, in_=src))
                        stg_vec[sb_] = c
                    on("vector", wqkv)
                else:
                    def wag(sync, dest=dest, j=j, nets=nets, c_stage=c_stage, sb_=ji % 2):
                        sync.wait_ge(sems["scalar"], c_stage)
                        dst = dest.ap().rearrange("(et p) o -> p et o", p=128)[:, :, 128 * j:128 * (j + 1)]
                        src = stageb[sb_].ap().rearrange("p (et c) -> p et c", c=128)[:, 0:nets, :]
                        inc("ag", sync.dma_start(out=dst, in_=src), 16)
                        stg_ag[sb_] = cnt["ag"]
                    on("sync", wag)
                ji += 1
        c_wqkv_rdy = cnt["vector"]
        c_ag_writes = cnt["ag"]

        def cc2(g):
            g.wait_ge(sems["ag"], c_ag_writes)
            for (i_, o_) in ((ag_wo_in, ag_wo_out), (ag_up_in, ag_up_out),
                             (ag_gate_in, ag_gate_out), (ag_dn_in, ag_dn_out)):
                inc("cc", g.collective_compute("AllGather", Alu.bypass, replica_groups=rg,
                                               ins=[i_.ap().opt()], outs=[o_.ap().opt()]))
        on("gpsimd", cc2)
        cc_w = cnt["cc"]
        c_p1_tensor = cnt["tensor"]

        es_p1.close()

        qT_sb = es_a.enter_context(nc.sbuf_tensor("qT_sb", [128, NT * 256], BF16))
        kT_sb = es_a.enter_context(nc.sbuf_tensor("kT_sb", [128, NT * 256], BF16))
        v_sb = es_a.enter_context(nc.sbuf_tensor("v_sb", [128, NT * 256], BF16))
        attnout_sb = es_a.enter_context(nc.sbuf_tensor("attnout_sb", [128, NT * 256], BF16))

        (S1, MEAN, EX2, M2, VAR, SQV, RSTD, MX, MN, T1, T2, AMAX, MAXLN, CC_, SXV,
         INVSX, AQ, BQ) = range(18)

        es_p2 = ExitStack()
        xt0 = es_p2.enter_context(nc.sbuf_tensor("xt0", [128, E], F32))
        xt1 = es_p2.enter_context(nc.sbuf_tensor("xt1", [128, E], F32))
        sqdump = es_p2.enter_context(nc.sbuf_tensor("sqdump", [128, E], F32))
        xq8a = es_p2.enter_context(nc.sbuf_tensor("xq8a", [128, E], I8))
        xq8b = es_p2.enter_context(nc.sbuf_tensor("xq8b", [128, E], I8))
        xqbfa = es_p2.enter_context(nc.sbuf_tensor("xqbfa", [128, E], BF16))
        xqbfb = es_p2.enter_context(nc.sbuf_tensor("xqbfb", [128, E], BF16))
        xq8 = [xq8a, xq8b]
        xqbf = [xqbfa, xqbfb]
        xqT = es_p2.enter_context(nc.sbuf_tensor("xqT", [128, E], BF16))
        qkstage0 = es_p2.enter_context(nc.sbuf_tensor("qkstage0", [128, 512], BF16))
        qkstage1 = es_p2.enter_context(nc.sbuf_tensor("qkstage1", [128, 512], BF16))
        qkstage = [qkstage0, qkstage1]
        ptpx0 = es_p2.enter_context(nc.psum_tensor("ptpx0", [128, 1024], BF16))
        ptpx1 = es_p2.enter_context(nc.psum_tensor("ptpx1", [128, 1024], BF16))
        pq1 = es_p2.enter_context(nc.psum_tensor("pq1", [128, 512], F32))
        pq2 = es_p2.enter_context(nc.psum_tensor("pq2", [128, 256], F32))
        ptpqk = es_p2.enter_context(nc.psum_tensor("ptpqk", [128, 512], BF16))
        xts = [xt0, xt1]
        ptpx = [ptpx0, ptpx1]
        c_tile_done = [0] * NT
        gx = [None] * NT

        c_quant_h = [0] * NT
        c_mmq_h = [0] * NT
        c_evq_h = [0] * NT
        c_tqk_h = [0] * NT
        c_evqk_h = [0] * NT
        c_xqT_h = [0] * NT

        def p2_pre(tt):
            buf = tt % 2
            po = 20 * (tt % 2)

            def ldx(sync, tt=tt, buf=buf):
                if tt >= 2:
                    sync.wait_ge(sems["scalar"], c_tile_done[tt - 2])
                    sync.wait_ge(sems["vector"], c_quant_h[tt - 2] if tt >= 2 else 0)
                else:
                    sync.wait_ge(sems["vector"], c_wqkv_rdy)
                    sync.wait_ge(sems["tensor"], c_p1_tensor)
                gx[tt] = pool.issue([sync.dma_start(out=xts[buf][:],
                                                    in_=x_full[tt * 128:(tt + 1) * 128, :])])
            on("sync", ldx)

            def st_v(v, tt=tt, buf=buf, po=po):
                pool.wait(v, gx[tt])
                v.tensor_reduce(out=sm[:, po + MX:po + MX + 1], in_=xts[buf][:],
                                axis=AX.X, op=Alu.max)
                inc("vector", v.tensor_reduce(out=sm[:, po + MN:po + MN + 1], in_=xts[buf][:],
                                              axis=AX.X, op=Alu.min))
            on("vector", st_v)
            c_sum = cnt["vector"]

            def st_s(s, buf=buf, po=po, c_sum=c_sum, tt=tt):
                pool.wait(s, gx[tt])
                s.activation(out=sqdump[:], in_=xts[buf][:], func=AF.Identity,
                             accum_out=sm[:, po + S1:po + S1 + 1])
                sinc(s.activation(out=sqdump[:], in_=xts[buf][:], func=AF.Square,
                                  accum_out=sm[:, po + EX2:po + EX2 + 1]))
            c_sq = sq(st_s)

            def sml_v(v, po=po, c_sq=c_sq):
                v.wait_ge(sems["scalar"], c_sq)
                v.tensor_scalar(out=sm[:, po + MEAN:po + MEAN + 1], in0=sm[:, po + S1:po + S1 + 1],
                                scalar1=1.0 / E, scalar2=None, op0=Alu.mult)
                v.tensor_scalar(out=sm[:, po + EX2:po + EX2 + 1], in0=sm[:, po + EX2:po + EX2 + 1],
                                scalar1=1.0 / E, scalar2=None, op0=Alu.mult)
                v.drain()
                v.tensor_tensor(out=sm[:, po + M2:po + M2 + 1], in0=sm[:, po + MEAN:po + MEAN + 1],
                                in1=sm[:, po + MEAN:po + MEAN + 1], op=Alu.mult)
                v.drain()
                inc("vector", v.tensor_tensor(out=sm[:, po + VAR:po + VAR + 1],
                                              in0=sm[:, po + EX2:po + EX2 + 1],
                                              in1=sm[:, po + M2:po + M2 + 1], op=Alu.subtract))
            on("vector", sml_v)
            c_var = cnt["vector"]

            def sml_s(s, po=po, c_var=c_var):
                s.wait_ge(sems["vector"], c_var)
                s.activation(out=sm[:, po + SQV:po + SQV + 1],
                             in_=sm[:, po + VAR:po + VAR + 1],
                             func=AF.Ln, bias=eps_t[:, 0:1], scale=1.0)
                s.drain()
                sinc(s.activation(out=sm[:, po + RSTD:po + RSTD + 1],
                                  in_=sm[:, po + SQV:po + SQV + 1],
                                  func=AF.Exp, scale=neghalf_t[:, 0:1]))
            c_sqv = sq(sml_s)

            def qv(v, buf=buf, po=po, tt=tt, c_sqv=c_sqv):
                v.wait_ge(sems["scalar"], c_sqv)
                P = lambda k: sm[:, po + k:po + k + 1]
                v.tensor_tensor(out=P(T1), in0=P(MX), in1=P(MEAN), op=Alu.subtract)
                v.tensor_tensor(out=P(T2), in0=P(MEAN), in1=P(MN), op=Alu.subtract)
                v.drain()
                v.tensor_tensor(out=P(AMAX), in0=P(T1), in1=P(T2), op=Alu.max)
                v.drain()
                v.tensor_tensor(out=P(MAXLN), in0=P(AMAX), in1=P(RSTD), op=Alu.mult)
                v.drain()
                v.tensor_scalar(out=P(CC_), in0=P(MAXLN), scalar1=1.0e-5, scalar2=None, op0=Alu.max)
                v.drain()
                v.reciprocal(out=P(SXV), in_=P(CC_))
                v.drain()
                v.tensor_scalar(out=P(SXV), in0=P(SXV), scalar1=127.0, scalar2=None, op0=Alu.mult)
                v.tensor_scalar(out=P(INVSX), in0=P(CC_), scalar1=1.0 / 127.0, scalar2=None, op0=Alu.mult)
                v.drain()
                v.tensor_tensor(out=P(AQ), in0=P(RSTD), in1=P(SXV), op=Alu.mult)
                v.drain()
                v.tensor_scalar(out=P(BQ), in0=P(MEAN), scalar1=P(AQ), scalar2=-1.0,
                                op0=Alu.mult, op1=Alu.mult)
                v.drain()
                v.tensor_scalar(out=xq8[buf][:], in0=xts[buf][:], scalar1=P(AQ), scalar2=P(BQ),
                                op0=Alu.mult, op1=Alu.add)
                v.drain()
                inc("vector", v.tensor_copy(out=xqbf[buf][:], in_=xq8[buf][:]))
            on("vector", qv)
            c_quant_h[tt] = cnt["vector"]

            if DEBUG and tt == 0:
                def dxq(v):
                    v.tensor_copy(out=snap["xq"][:], in_=xq8[0][:])
                    inc("vector", v.tensor_copy(out=snap["sm"][:], in_=sm[:]))
                on("vector", dxq)
                c_quant_h[tt] = cnt["vector"]

        def p2_pe(tt):
            buf = tt % 2
            po = 20 * (tt % 2)
            c_quant = c_quant_h[tt]
            c_xqT = 0
            for g0 in (0, 8):
                pb = g0 // 8

                def tpx(t, g0=g0, pb=pb, tt=tt, buf=buf, c_quant=c_quant):
                    t.wait_ge(sems["vector"], c_quant)
                    if tt > 0:
                        t.wait_ge(sems["scalar"], c_xqT_h[tt - 1])
                    pool.wait(t, g_const)
                    for k in range(8):
                        ins = t.transpose(out=ptpx[pb][:, 128 * k:128 * (k + 1)],
                                          in_=xqbf[buf][:, 128 * (g0 + k):128 * (g0 + k + 1)],
                                          identity=ident[:])
                        if k == 7:
                            inc("tensor", ins)
                on("tensor", tpx)
                c_tpx = cnt["tensor"]

                def evx(s, g0=g0, pb=pb, c_tpx=c_tpx, tt=tt):
                    s.wait_ge(sems["tensor"], c_tpx)
                    if tt > 0:
                        s.wait_ge(sems["tensor"], c_mmq_h[tt - 1])
                    sinc(s.copy(out=xqT[:, 128 * g0:128 * (g0 + 8)], in_=ptpx[pb][:]))
                c_xqT = sq(evx)
                c_xqT_h[tt] = c_xqT

            def mmq(t, c_xqT=c_xqT, tt=tt):
                t.wait_ge(sems["scalar"], c_xqT)
                t.wait_ge(sems["vector"], c_wqkv_rdy)
                if tt > 0:
                    t.wait_ge(sems["vector"], c_evq_h[tt - 1])
                for et in range(ET):
                    t.matmul(out=pq1[:], lhsT=xqT[:, 128 * et:128 * (et + 1)],
                             rhs=wqkvT[:, 768 * et:768 * et + 512],
                             start=(et == 0), stop=(et == ET - 1))
                for et in range(ET):
                    ins = t.matmul(out=pq2[:], lhsT=xqT[:, 128 * et:128 * (et + 1)],
                                   rhs=wqkvT[:, 768 * et + 512:768 * (et + 1)],
                                   start=(et == 0), stop=(et == ET - 1))
                    if et == ET - 1:
                        inc("tensor", ins)
            on("tensor", mmq)
            c_mmq_h[tt] = cnt["tensor"]
            c_mmq = cnt["tensor"]

            def evq(v, tt=tt, po=po, c_mmq=c_mmq, buf=buf):
                v.wait_ge(sems["tensor"], c_mmq)
                if tt > 1:
                    v.wait_ge(sems["tensor"], c_tqk_h[tt - 2])
                P = lambda k: sm[:, po + k:po + k + 1]
                v.tensor_scalar(out=qkstage[buf][:, 0:256], in0=pq1[:, 0:256],
                                scalar1=P(INVSX), scalar2=scales_sb[:, SC_Q:SC_Q + 1],
                                op0=Alu.mult, op1=Alu.mult)
                v.tensor_scalar(out=qkstage[buf][:, 256:512], in0=pq1[:, 256:512],
                                scalar1=P(INVSX), scalar2=scales_sb[:, SC_K:SC_K + 1],
                                op0=Alu.mult, op1=Alu.mult)
                inc("vector", v.tensor_scalar(out=v_sb[:, 256 * tt:256 * (tt + 1)],
                                              in0=pq2[:], scalar1=P(INVSX),
                                              scalar2=scales_sb[:, SC_V:SC_V + 1],
                                              op0=Alu.mult, op1=Alu.mult))
            on("vector", evq)
            c_evq_h[tt] = cnt["vector"]
            c_evq = cnt["vector"]


        def p2_pe_b(tt):
            buf = tt % 2

            def tqk(t, tt=tt, buf=buf):
                t.wait_ge(sems["vector"], c_evq_h[tt])
                if tt > 0:
                    t.wait_ge(sems["scalar"], c_evqk_h[tt - 1])
                for k in range(4):
                    ins = t.transpose(out=ptpqk[:, 128 * k:128 * (k + 1)],
                                      in_=qkstage[buf][:, 128 * k:128 * (k + 1)],
                                      identity=ident[:])
                    if k == 3:
                        inc("tensor", ins)
            on("tensor", tqk)
            c_tqk_h[tt] = cnt["tensor"]
            c_tqk = cnt["tensor"]

            def evqk(s, tt=tt, c_tqk=c_tqk):
                s.wait_ge(sems["tensor"], c_tqk)
                s.copy(out=qT_sb[:, 256 * tt:256 * (tt + 1)], in_=ptpqk[:, 0:256])
                sinc(s.copy(out=kT_sb[:, 256 * tt:256 * (tt + 1)], in_=ptpqk[:, 256:512]))
            c = sq(evqk)
            c_evqk_h[tt] = c
            c_tile_done[tt] = c

        for step in range(NT + 2):
            if step < NT:
                p2_pre(step)
            if step >= 1 and step - 1 < NT:
                p2_pe(step - 1)
            if step >= 2:
                p2_pe_b(step - 2)

        sflush()
        es_p2.close()

        es_p3 = ExitStack()
        attnT0 = es_p3.enter_context(nc.sbuf_tensor("attnT0", [128, 512], BF16))
        attnT1 = es_p3.enter_context(nc.sbuf_tensor("attnT1", [128, 512], BF16))
        attnT2 = es_p3.enter_context(nc.sbuf_tensor("attnT2", [128, 512], BF16))
        scm0 = es_p3.enter_context(nc.sbuf_tensor("scm0", [128, 512], F32))
        scm1 = es_p3.enter_context(nc.sbuf_tensor("scm1", [128, 512], F32))
        scm2 = es_p3.enter_context(nc.sbuf_tensor("scm2", [128, 512], F32))
        outT = es_p3.enter_context(nc.sbuf_tensor("outT", [128, 4096], BF16))
        sumsrow = es_p3.enter_context(nc.sbuf_tensor("sumsrow", [1, S], F32))
        recips = es_p3.enter_context(nc.sbuf_tensor("recips", [128, NTB], F32))
        sumsbf = es_p3.enter_context(nc.sbuf_tensor("sumsbf", [1, S], BF16))
        sums32 = es_p3.enter_context(nc.sbuf_tensor("sums32", [32, 128], BF16))
        psc0 = es_p3.enter_context(nc.psum_tensor("psc0", [128, 512], F32))
        psc1 = es_p3.enter_context(nc.psum_tensor("psc1", [128, 512], F32))
        psc2 = es_p3.enter_context(nc.psum_tensor("psc2", [128, 512], F32))
        pout = es_p3.enter_context(nc.psum_tensor("pout", [128, 512], F32))
        psum1 = es_p3.enter_context(nc.psum_tensor("psum1", [1, 512], F32))
        ptpo = es_p3.enter_context(nc.psum_tensor("ptpo", [128, 512], BF16))
        ptps = es_p3.enter_context(nc.psum_tensor("ptps", [128, 32], BF16))
        psc = [psc0, psc1, psc2]
        c_qkv_all = cnt["vector"]
        g_sums_prev = [None]

        attnTb = [attnT0, attnT1, attnT2]
        scmb = [scm0, scm1, scm2]
        for bh in range(4):
            b, ds = bh // 2, bh % 2
            tiles = [(sc, tl) for sc in range(8) for tl in range(4 * sc + 4)]
            NTI = len(tiles)
            c_exp_h = [0] * NTI
            c_ov_h = [0] * NTI
            psc_cons = [None, None, None]
            c_ev_sc_map = {}
            c_ev_last = 0

            def emit_mm_sc(idx):
                sc, tl = tiles[idx]
                tg = NTB * b + tl
                pb = idx % 3

                def mm_sc(t, tg=tg, pb=pb, sc=sc, cons=psc_cons[pb]):
                    t.wait_ge(sems["vector"], c_qkv_all)
                    if cons is not None:
                        t.wait_ge(sems[cons[0]], cons[1])
                    qv_ = qT_sb.ap().rearrange("p (t c) -> p t c", c=256)
                    rhs = qv_[:, NTB * b + 4 * sc:NTB * b + 4 * sc + 4,
                              128 * ds:128 * (ds + 1)]
                    inc("tensor", t.matmul(
                        out=psc[pb][:],
                        lhsT=kT_sb[:, 256 * tg + 128 * ds:256 * tg + 128 * (ds + 1)],
                        rhs=rhs, start=True, stop=True))
                on("tensor", mm_sc)
                return cnt["tensor"]

            def emit_exp(idx, c_mmsc):
                sc, tl = tiles[idx]
                pb = idx % 3
                diag = tl >= 4 * sc
                c_at_free = c_ov_h[idx - 3] if idx >= 3 else 0
                if diag:
                    kk = tl - 4 * sc

                    def mskv(v, kk=kk, pb=pb, c_mmsc=c_mmsc):
                        v.wait_ge(sems["tensor"], c_mmsc)
                        if c_exp_h[idx - 3] if idx >= 3 else 0:
                            v.wait_ge(sems["scalar"], c_exp_h[idx - 3])
                        inc("vector", v.tensor_tensor(
                            out=scmb[pb][:], in0=psc[pb][:],
                            in1=cmask[:, 384 - 128 * kk:896 - 128 * kk], op=Alu.add))
                    on("vector", mskv)
                    c_mv = cnt["vector"]
                    psc_cons[pb] = ("vector", c_mv)

                    def expd(s, pb=pb, c_mv=c_mv, c_at_free=c_at_free):
                        s.wait_ge(sems["vector"], c_mv)
                        if c_at_free:
                            s.wait_ge(sems["tensor"], c_at_free)
                        sinc(s.activation(out=attnTb[pb][:], in_=scmb[pb][:], func=AF.Exp))
                    sq(expd)
                else:
                    def expn(s, pb=pb, c_mmsc=c_mmsc, c_at_free=c_at_free):
                        s.wait_ge(sems["tensor"], c_mmsc)
                        if c_at_free:
                            s.wait_ge(sems["tensor"], c_at_free)
                        sinc(s.activation(out=attnTb[pb][:], in_=psc[pb][:], func=AF.Exp))
                    sq(expn)
                    psc_cons[pb] = ("scalar", cnt["scalar"])
                c_exp_h[idx] = cnt["scalar"]

            def emit_mm_ov(idx):
                sc, tl = tiles[idx]
                tg = NTB * b + tl
                pb = idx % 3
                first = tl == 0
                last = tl == 4 * sc + 3
                c_exp = c_exp_h[idx]
                c_prev_ev = c_ev_sc_map.get(sc - 1, 0)

                def mm_ov(t, tg=tg, pb=pb, first=first, last=last,
                          c_exp=c_exp, c_prev_ev=c_prev_ev):
                    t.wait_ge(sems["scalar"], c_exp)
                    if first and c_prev_ev:
                        t.wait_ge(sems["vector"], c_prev_ev)
                    t.matmul(out=pout[:],
                             lhsT=v_sb[:, 256 * tg + 128 * ds:256 * tg + 128 * (ds + 1)],
                             rhs=attnTb[pb][:], start=first, stop=last)
                    inc("tensor", t.matmul(out=psum1[:], lhsT=ones_bf[:], rhs=attnTb[pb][:],
                                           start=first, stop=last))
                on("tensor", mm_ov)
                c_ov_h[idx] = cnt["tensor"]
                if last:
                    def ev_sc(v, sc=sc, c_last_mm=cnt["tensor"]):
                        v.wait_ge(sems["tensor"], c_last_mm)
                        if sc == 0 and g_sums_prev[0] is not None:
                            pool.wait(v, g_sums_prev[0])
                        v.tensor_copy(out=outT[:, 512 * sc:512 * (sc + 1)], in_=pout[:])
                        inc("vector", v.tensor_copy(out=sumsrow[0:1, 512 * sc:512 * (sc + 1)],
                                                    in_=psum1[0:1, :]))
                    on("vector", ev_sc)
                    c_ev_sc_map[sc] = cnt["vector"]

            for idx in range(NTI):
                c_mmsc = emit_mm_sc(idx)
                emit_exp(idx, c_mmsc)
                if idx >= 1:
                    emit_mm_ov(idx - 1)
            emit_mm_ov(NTI - 1)
            sflush()
            c_ev_sc = c_ev_sc_map[7]

            def sums_bf(v, bh=bh):
                v.drain()
                inc("vector", v.tensor_copy(out=sumsbf[0:1, :], in_=sumsrow[0:1, :]))
            on("vector", sums_bf)
            c_sbf = cnt["vector"]

            g_sums_h = [None]

            def sums_mv(sync, c_sbf=c_sbf):
                sync.wait_ge(sems["vector"], c_sbf)
                g1 = pool.issue([sync.dma_start(out=sums_bounce, in_=sumsbf[0:1, :])])
                pool.wait(sync, g1)
                src = sums_bounce.rearrange("x (t p) -> (x t) p", t=32)
                g_sums_h[0] = pool.issue([sync.dma_start(out=sums32[:], in_=src)])
            on("sync", sums_mv)
            g_sums = g_sums_h[0]
            g_sums_prev[0] = g_sums

            def sums_tp(t, g_sums=g_sums):
                pool.wait(t, g_sums)
                inc("tensor", t.transpose(out=ptps[:], in_=sums32[:], identity=ident[0:32, 0:32]))
            on("tensor", sums_tp)
            c_stp = cnt["tensor"]

            def recv(v, c_stp=c_stp, bh=bh):
                v.wait_ge(sems["tensor"], c_stp)
                v.tensor_copy(out=recips[:], in_=ptps[:])
                v.drain()
                inc("vector", v.reciprocal(out=recips[:], in_=recips[:]))
                v.drain()
                if DEBUG and bh == 0:
                    inc("vector", v.tensor_copy(out=snap["rec"][:], in_=recips[:]))
            on("vector", recv)
            c_rec = cnt["vector"]

            c_ev_tp = 0
            for sc in range(8):
                def tpo(t, sc=sc, c_rec=c_rec, c_ev_tp=c_ev_tp):
                    t.wait_ge(sems["vector"], c_rec)
                    if c_ev_tp:
                        t.wait_ge(sems["vector"], c_ev_tp)
                    for j in range(4):
                        ins = t.transpose(out=ptpo[:, 128 * j:128 * (j + 1)],
                                          in_=outT[:, 512 * sc + 128 * j:512 * sc + 128 * (j + 1)],
                                          identity=ident[:])
                        if j == 3:
                            inc("tensor", ins)
                on("tensor", tpo)
                c_tpo = cnt["tensor"]

                def evtp(v, sc=sc, b=b, ds=ds, c_tpo=c_tpo):
                    v.wait_ge(sems["tensor"], c_tpo)
                    for j in range(4):
                        st = 4 * sc + j
                        ins = v.tensor_scalar(
                            out=attnout_sb[:, 256 * (NTB * b + st) + 128 * ds:
                                           256 * (NTB * b + st) + 128 * (ds + 1)],
                            in0=ptpo[:, 128 * j:128 * (j + 1)],
                            scalar1=recips[:, st:st + 1], scalar2=None, op0=Alu.mult)
                        if j == 3:
                            inc("vector", ins)
                on("vector", evtp)
                c_ev_tp = cnt["vector"]

        if DEBUG:
            def dato(v):
                v.wait_ge(sems["vector"], cnt["vector"])
                inc("vector", v.tensor_copy(out=snap["ao"][:], in_=attnout_sb[:, 0:256]))
            on("vector", dato)

        def at_out(sync):
            sync.wait_ge(sems["vector"], cnt["vector"])
            dst = a2a_in.ap().rearrange("(t p) f -> p t f", p=128)
            src_ = attnout_sb.ap().rearrange("p (t f) -> p t f", f=256)
            inc("a2aw", sync.dma_start(out=dst, in_=src_), 16)
        on("sync", at_out)
        c_a2a_in = cnt["a2aw"]

        def cc3(g):
            g.wait_ge(sems["a2aw"], c_a2a_in)
            with allow_ici_all_to_all():
                inc("cc", g.collective_compute("AllToAll", Alu.bypass, replica_groups=rg,
                                               ins=[a2a_in.ap().opt()], outs=[a2a_out.ap().opt()]))
        on("gpsimd", cc3)
        cc_at = cnt["cc"]
        c_p3_v = cnt["vector"]
        c_p3_t = cnt["tensor"]
        c_p3_s = cnt["scalar"]

        es_p3.close()
        es_a.close()

        OS1, OMEAN, OEX2, OM2, OVAR, OSQV, ORSTD, OCC, OSX, OINVSX, OBT = range(11)

        es_p4 = ExitStack()
        woT_sb = es_p4.enter_context(nc.sbuf_tensor("woT_sb", [128, ET * 2048], BF16))
        oin0 = es_p4.enter_context(nc.sbuf_tensor("oin0", [128, E], BF16))
        oin1 = es_p4.enter_context(nc.sbuf_tensor("oin1", [128, E], BF16))
        oinb = [oin0, oin1]
        xres0 = es_p4.enter_context(nc.sbuf_tensor("xres0", [128, E], F32))
        xres1 = es_p4.enter_context(nc.sbuf_tensor("xres1", [128, E], F32))
        xresb = [xres0, xres1]
        oq8a = es_p4.enter_context(nc.sbuf_tensor("oq8a", [128, E], I8))
        oq8b = es_p4.enter_context(nc.sbuf_tensor("oq8b", [128, E], I8))
        oq8b_ = [oq8a, oq8b]
        oqbfa = es_p4.enter_context(nc.sbuf_tensor("oqbfa", [128, E], BF16))
        oqbfb = es_p4.enter_context(nc.sbuf_tensor("oqbfb", [128, E], BF16))
        oqbfb_ = [oqbfa, oqbfb]
        oqT = es_p4.enter_context(nc.sbuf_tensor("oqT", [128, E], BF16))
        oproj0 = es_p4.enter_context(nc.sbuf_tensor("oproj0", [128, E], F32))
        oproj1 = es_p4.enter_context(nc.sbuf_tensor("oproj1", [128, E], F32))
        oprojb = [oproj0, oproj1]
        osq = es_p4.enter_context(nc.sbuf_tensor("osq", [128, E], F32))
        lnres0 = es_p4.enter_context(nc.sbuf_tensor("lnres0", [128, E], F32))
        lnres1 = es_p4.enter_context(nc.sbuf_tensor("lnres1", [128, E], F32))
        lnresb = [lnres0, lnres1]
        ptpo2a = es_p4.enter_context(nc.psum_tensor("ptpo2a", [128, 1024], BF16))
        ptpo2b = es_p4.enter_context(nc.psum_tensor("ptpo2b", [128, 1024], BF16))
        po0 = es_p4.enter_context(nc.psum_tensor("po0", [128, 512], F32))
        po1 = es_p4.enter_context(nc.psum_tensor("po1", [128, 512], F32))
        po2 = es_p4.enter_context(nc.psum_tensor("po2", [128, 512], F32))
        po3 = es_p4.enter_context(nc.psum_tensor("po3", [128, 512], F32))
        pob = [po0, po1, po2, po3]
        ptpo2 = [ptpo2a, ptpo2b]

        g_wo_h = [None]

        def ldwo(sync):
            sync.wait_ge(sems["cc"], cc_w)
            sync.wait_ge(sems["vector"], c_p3_v)
            sync.wait_ge(sems["tensor"], c_p3_t)
            sync.wait_ge(sems["scalar"], c_p3_s)
            ins = []
            for r in range(NC):
                src = ag_wo_out.ap()[E * r:E * (r + 1), :].rearrange("(et p) o -> p et o", p=128)
                dst = woT_sb.ap().rearrange("p (et c) -> p et c", c=2048)[:, :, 256 * r:256 * (r + 1)]
                ins.append(sync.dma_start(out=dst, in_=src))
            g_wo_h[0] = pool.issue(ins)
        on("sync", ldwo)
        g_wo = g_wo_h[0]

        NJ = TPC // 128
        g_x1w_h = [None] * NJ
        c_oq_h = [0] * NJ
        c_evoq_h = [0] * NJ
        c_evo_h = [[0] * 4 for _ in range(NJ)]
        c_x1v_h = [0] * NJ

        def p4_pre(j):
            ob = j % 2
            OP = 11 * (j % 2)
            g_oin_h = [None]

            def ldoin(sync, j=j, ob=ob):
                sync.wait_ge(sems["cc"], cc_at)
                if j >= 2:
                    sync.wait_ge(sems["vector"], c_oq_h[j - 2])
                ins = []
                for r in range(NC):
                    ins.append(sync.dma_start(out=oinb[ob][:, 256 * r:256 * (r + 1)],
                                              in_=a2a_out.ap()[TPC * r + 128 * j:
                                                               TPC * r + 128 * (j + 1), :]))
                g_oin_h[0] = pool.issue(ins)
            on("sync", ldoin)
            g_oin = g_oin_h[0]

            def oquant(v, g_oin=g_oin, j=j, ob=ob, OP=OP):
                pool.wait(v, g_oin)
                oin = oinb[ob]
                if DEBUG and j == 0:
                    v.tensor_copy(out=snap["oin"][:], in_=oin[:, 0:512])
                P = lambda k: sm[:, OP + k:OP + k + 1]
                v.tensor_reduce(out=P(OCC), in_=oin[:], axis=AX.X, op=Alu.max,
                                apply_absolute_value=True)
                v.drain()
                v.tensor_scalar(out=P(OCC), in0=P(OCC), scalar1=1.0e-5, scalar2=None, op0=Alu.max)
                v.drain()
                v.reciprocal(out=P(OSX), in_=P(OCC))
                v.drain()
                v.tensor_scalar(out=P(OSX), in0=P(OSX), scalar1=127.0, scalar2=None, op0=Alu.mult)
                v.tensor_scalar(out=P(OINVSX), in0=P(OCC), scalar1=1.0 / 127.0,
                                scalar2=None, op0=Alu.mult)
                v.drain()
                v.tensor_scalar(out=oq8b_[ob][:], in0=oin[:], scalar1=P(OSX),
                                scalar2=None, op0=Alu.mult)
                v.drain()
                inc("vector", v.tensor_copy(out=oqbfb_[ob][:], in_=oq8b_[ob][:]))
            on("vector", oquant)
            c_oq_h[j] = cnt["vector"]

        def p4_mid(j):
            ob = j % 2
            c_oq = c_oq_h[j]
            for g0 in (0, 8):
                pb = g0 // 8

                def tpoq(t, g0=g0, pb=pb, c_oq=c_oq, j=j, ob=ob):
                    t.wait_ge(sems["vector"], c_oq)
                    if j > 0:
                        t.wait_ge(sems["scalar"], c_evoq_h[j - 1])
                    pool.wait(t, g_const)
                    for k in range(8):
                        ins = t.transpose(out=ptpo2[pb][:, 128 * k:128 * (k + 1)],
                                          in_=oqbfb_[ob][:, 128 * (g0 + k):128 * (g0 + k + 1)],
                                          identity=ident[:])
                        if k == 7:
                            inc("tensor", ins)
                on("tensor", tpoq)
                c_t = cnt["tensor"]

                def evoq(s, g0=g0, pb=pb, c_t=c_t):
                    s.wait_ge(sems["tensor"], c_t)
                    sinc(s.copy(out=oqT[:, 128 * g0:128 * (g0 + 8)], in_=ptpo2[pb][:]))
                c_evoq_h[j] = sq(evoq)
            c_evoq = c_evoq_h[j]

            for ch in range(4):
                def mm_o(t, ch=ch, j=j, c_evoq=c_evoq):
                    t.wait_ge(sems["scalar"], c_evoq)
                    pool.wait(t, g_wo)
                    if j > 0:
                        t.wait_ge(sems["vector"], c_evo_h[j - 1][ch])
                    for et in range(ET):
                        ins = t.matmul(out=pob[ch][:], lhsT=oqT[:, 128 * et:128 * (et + 1)],
                                       rhs=woT_sb[:, 2048 * et + 512 * ch:2048 * et + 512 * (ch + 1)],
                                       start=(et == 0), stop=(et == ET - 1))
                        if et == ET - 1:
                            inc("tensor", ins)
                on("tensor", mm_o)
                c_mo = cnt["tensor"]

                def ev_o(v, ch=ch, j=j, c_mo=c_mo, OP=11 * (j % 2)):
                    v.wait_ge(sems["tensor"], c_mo)
                    inc("vector", v.tensor_scalar(out=oprojb[j % 2][:, 512 * ch:512 * (ch + 1)],
                                                  in0=pob[ch][:],
                                                  scalar1=sm[:, OP + OINVSX:OP + OINVSX + 1],
                                                  scalar2=scales_sb[:, SC_O:SC_O + 1],
                                                  op0=Alu.mult, op1=Alu.mult))
                on("vector", ev_o)
                c_evo_h[j][ch] = cnt["vector"]

        def p4_post(j):
            ob = j % 2
            OP = 11 * (j % 2)
            oproj = oprojb[ob]
            c_op = c_evo_h[j][3]
            g_xr_h = [None]

            def ldxr(sync, j=j, ob=ob):
                if j >= 2:
                    sync.wait_ge(sems["vector"], c_x1v_h[j - 2])
                g_xr_h[0] = pool.issue([sync.dma_start(out=xresb[ob][:],
                                                       in_=x_mine[128 * j:128 * (j + 1), :])])
            on("sync", ldxr)
            g_xr = g_xr_h[0]

            def oln_v(v, c_op=c_op, OP=OP, oproj=oproj):
                v.wait_ge(sems["vector"], c_op)
                inc("vector", v.tensor_reduce(out=sm[:, OP + OS1:OP + OS1 + 1], in_=oproj[:],
                                              axis=AX.X, op=Alu.add))
            on("vector", oln_v)
            c_os1 = cnt["vector"]

            def oln_s(s, c_os1=c_os1, OP=OP, oproj=oproj):
                s.wait_ge(sems["vector"], c_os1)
                sinc(s.activation(out=osq[:], in_=oproj[:], func=AF.Square,
                                  accum_out=sm[:, OP + OEX2:OP + OEX2 + 1]))
            c_osq = sq(oln_s)

            def oln_v2(v, c_osq=c_osq, OP=OP):
                v.wait_ge(sems["scalar"], c_osq)
                P = lambda k: sm[:, OP + k:OP + k + 1]
                v.tensor_scalar(out=P(OMEAN), in0=P(OS1), scalar1=1.0 / E, scalar2=None, op0=Alu.mult)
                v.tensor_scalar(out=P(OEX2), in0=P(OEX2), scalar1=1.0 / E, scalar2=None, op0=Alu.mult)
                v.drain()
                v.tensor_tensor(out=P(OM2), in0=P(OMEAN), in1=P(OMEAN), op=Alu.mult)
                v.drain()
                inc("vector", v.tensor_tensor(out=P(OVAR), in0=P(OEX2), in1=P(OM2), op=Alu.subtract))
            on("vector", oln_v2)
            c_ov = cnt["vector"]

            def oln_s2(s, c_ov=c_ov, OP=OP):
                s.wait_ge(sems["vector"], c_ov)
                s.activation(out=sm[:, OP + OSQV:OP + OSQV + 1], in_=sm[:, OP + OVAR:OP + OVAR + 1],
                             func=AF.Ln, bias=eps_t[:, 0:1], scale=1.0)
                s.drain()
                sinc(s.activation(out=sm[:, OP + ORSTD:OP + ORSTD + 1],
                                  in_=sm[:, OP + OSQV:OP + OSQV + 1],
                                  func=AF.Exp, scale=neghalf_t[:, 0:1]))
            c_osqv = sq(oln_s2)

            def oln_v3(v, j=j, c_osqv=c_osqv, ob=ob, OP=OP, oproj=oproj, g_xr=g_xr):
                v.wait_ge(sems["scalar"], c_osqv)
                if j >= 2:
                    pool.wait(v, g_x1w_h[j - 2])
                pool.wait(v, g_xr)
                P = lambda k: sm[:, OP + k:OP + k + 1]
                v.tensor_scalar(out=P(OBT), in0=P(OMEAN), scalar1=P(ORSTD),
                                scalar2=-1.0, op0=Alu.mult, op1=Alu.mult)
                v.drain()
                v.tensor_scalar(out=lnresb[ob][:], in0=oproj[:],
                                scalar1=P(ORSTD), scalar2=P(OBT),
                                op0=Alu.mult, op1=Alu.add)
                v.drain()
                inc("vector", v.tensor_tensor(out=lnresb[ob][:], in0=lnresb[ob][:],
                                              in1=xresb[ob][:], op=Alu.add))
                if DEBUG and j == 0:
                    inc("vector", v.tensor_copy(out=snap["x1"][:], in_=lnresb[0][:, 0:1024]))
            on("vector", oln_v3)
            c_x1 = cnt["vector"]
            c_x1v_h[j] = c_x1

            def stx1(sync, j=j, c_x1=c_x1, ob=ob):
                sync.wait_ge(sems["vector"], c_x1)
                ins = sync.dma_start(out=x1_hbm[128 * j:128 * (j + 1), :], in_=lnresb[ob][:])
                g_x1w_h[j] = pool.issue([ins])
            on("sync", stx1)

        for step in range(NJ + 2):
            if step < NJ:
                p4_pre(step)
            if 1 <= step <= NJ:
                p4_mid(step - 1)
            if step >= 2:
                p4_post(step - 2)
        sflush()
        c_x1_all = cnt["x1w"]
        c_p4_v = cnt["vector"]
        c_p4_t = cnt["tensor"]
        c_p4_s = cnt["scalar"]
        es_p4.close()

        FS1, FMEAN, FEX2, FM2, FVAR, FSQV, FRSTD, FBT = range(31, 39)

        es_p5 = ExitStack()
        x1t = es_p5.enter_context(nc.sbuf_tensor("x1t", [128, E], F32))
        fsq = es_p5.enter_context(nc.sbuf_tensor("fsq", [128, E], F32))
        hin = es_p5.enter_context(nc.sbuf_tensor("hin", [128, E], BF16))
        hinT = es_p5.enter_context(nc.sbuf_tensor("hinT", [128, ET * 512], BF16))
        wug0 = es_p5.enter_context(nc.sbuf_tensor("wug0", [128, 2 * 2048], BF16))
        wug1 = es_p5.enter_context(nc.sbuf_tensor("wug1", [128, 2 * 2048], BF16))
        silu0 = es_p5.enter_context(nc.sbuf_tensor("silu0", [128, 512], F32))
        silu1 = es_p5.enter_context(nc.sbuf_tensor("silu1", [128, 512], F32))
        silub = [silu0, silu1]
        hT = es_p5.enter_context(nc.sbuf_tensor("hT", [128, 32 * 512], BF16))
        wdn0 = es_p5.enter_context(nc.sbuf_tensor("wdn0", [128, 32 * 512], BF16))
        wdn1 = es_p5.enter_context(nc.sbuf_tensor("wdn1", [128, 32 * 512], BF16))
        y0 = es_p5.enter_context(nc.sbuf_tensor("y0", [128, E], F32))
        y1 = es_p5.enter_context(nc.sbuf_tensor("y1", [128, E], F32))
        y2 = es_p5.enter_context(nc.sbuf_tensor("y2", [128, E], F32))
        y3 = es_p5.enter_context(nc.sbuf_tensor("y3", [128, E], F32))
        ptpf = es_p5.enter_context(nc.psum_tensor("ptpf", [128, 1024], BF16))
        pupa = es_p5.enter_context(nc.psum_tensor("pupa", [128, 512], F32))
        pupb = es_p5.enter_context(nc.psum_tensor("pupb", [128, 512], F32))
        pgta = es_p5.enter_context(nc.psum_tensor("pgta", [128, 512], F32))
        pgtb = es_p5.enter_context(nc.psum_tensor("pgtb", [128, 512], F32))
        pup = [pupa, pupb]
        pgt = [pgta, pgtb]
        pd0 = es_p5.enter_context(nc.psum_tensor("pd0", [128, 512], F32))
        pd1 = es_p5.enter_context(nc.psum_tensor("pd1", [128, 512], F32))
        wug = [wug0, wug1]
        wdn = [wdn0, wdn1]
        ysb = [y0, y1, y2, y3]
        pd = [pd0, pd1]

        for grp in range(2):
            c_hinT = 0
            for j2 in range(4):
                j = 4 * grp + j2
                g_x1_h = [None]

                def ldx1(sync, j=j, j2=j2, grp=grp):
                    pool.wait(sync, g_x1w_h[j])
                    if grp == 0 and j2 == 0:
                        sync.wait_ge(sems["vector"], c_p4_v)
                        sync.wait_ge(sems["tensor"], c_p4_t)
                        sync.wait_ge(sems["scalar"], c_p4_s)
                    else:
                        sync.wait_ge(sems["vector"], cnt["vector"])
                    g_x1_h[0] = pool.issue([sync.dma_start(out=x1t[:],
                                                           in_=x1_hbm[128 * j:128 * (j + 1), :])])
                on("sync", ldx1)
                g_x1 = g_x1_h[0]

                def fst_v(v, g_x1=g_x1):
                    pool.wait(v, g_x1)
                    inc("vector", v.tensor_reduce(out=sm[:, FS1:FS1 + 1], in_=x1t[:],
                                                  axis=AX.X, op=Alu.add))
                on("vector", fst_v)
                c_fs1 = cnt["vector"]

                def fst_s(s, c_fs1=c_fs1):
                    s.wait_ge(sems["vector"], c_fs1)
                    sinc(s.activation(out=fsq[:], in_=x1t[:], func=AF.Square,
                                      accum_out=sm[:, FEX2:FEX2 + 1]))
                c_fsq = sq(fst_s)

                def fsm(v, c_fsq=c_fsq):
                    v.wait_ge(sems["scalar"], c_fsq)
                    P = lambda k: sm[:, k:k + 1]
                    v.tensor_scalar(out=P(FMEAN), in0=P(FS1), scalar1=1.0 / E, scalar2=None, op0=Alu.mult)
                    v.tensor_scalar(out=P(FEX2), in0=P(FEX2), scalar1=1.0 / E, scalar2=None, op0=Alu.mult)
                    v.drain()
                    v.tensor_tensor(out=P(FM2), in0=P(FMEAN), in1=P(FMEAN), op=Alu.mult)
                    v.drain()
                    inc("vector", v.tensor_tensor(out=P(FVAR), in0=P(FEX2), in1=P(FM2), op=Alu.subtract))
                on("vector", fsm)
                c_fv = cnt["vector"]

                def fsm_s(s, c_fv=c_fv):
                    s.wait_ge(sems["vector"], c_fv)
                    s.activation(out=sm[:, FSQV:FSQV + 1], in_=sm[:, FVAR:FVAR + 1],
                                 func=AF.Ln, bias=eps_t[:, 0:1], scale=1.0)
                    s.drain()
                    sinc(s.activation(out=sm[:, FRSTD:FRSTD + 1],
                                      in_=sm[:, FSQV:FSQV + 1],
                                      func=AF.Exp, scale=neghalf_t[:, 0:1]))
                c_fsqv = sq(fsm_s)

                def fnorm(v, c_fsqv=c_fsqv):
                    v.wait_ge(sems["scalar"], c_fsqv)
                    P = lambda k: sm[:, k:k + 1]
                    v.tensor_scalar(out=P(FBT), in0=P(FMEAN), scalar1=P(FRSTD),
                                    scalar2=-1.0, op0=Alu.mult, op1=Alu.mult)
                    v.drain()
                    inc("vector", v.tensor_scalar(out=hin[:], in0=x1t[:],
                                                  scalar1=P(FRSTD), scalar2=P(FBT),
                                                  op0=Alu.mult, op1=Alu.add))
                on("vector", fnorm)
                c_hin = cnt["vector"]

                for g0 in (0, 8):
                    def tph(t, g0=g0, c_hin=c_hin):
                        t.wait_ge(sems["vector"], c_hin)
                        t.wait_ge(sems["vector"], cnt["vector"])
                        pool.wait(t, g_const)
                        for k in range(8):
                            ins = t.transpose(out=ptpf[:, 128 * k:128 * (k + 1)],
                                              in_=hin[:, 128 * (g0 + k):128 * (g0 + k + 1)],
                                              identity=ident[:])
                            if k == 7:
                                inc("tensor", ins)
                    on("tensor", tph)
                    c_tph = cnt["tensor"]

                    def evh(v, g0=g0, j2=j2, c_tph=c_tph):
                        v.wait_ge(sems["tensor"], c_tph)
                        dst = hinT.ap().rearrange("p (et c) -> p et c", c=512)[:, g0:g0 + 8,
                                                                               128 * j2:128 * (j2 + 1)]
                        src = ptpf.ap().rearrange("p (et c) -> p et c", c=128)
                        inc("vector", v.tensor_copy(out=dst, in_=src))
                    on("vector", evh)
                c_hinT = cnt["vector"]

            for xh in range(2):
                c_silu_done = [cnt["scalar"], cnt["scalar"]]
                c_doh_done = [cnt["vector"], cnt["vector"]]
                c_mug_h = [cnt["tensor"], cnt["tensor"]]
                for ebl in range(32):
                    eb = 32 * xh + ebl
                    wb = eb % 2
                    r, ebr = eb // 8, eb % 8
                    g_ug_h = [None]

                    def ldug(sync, r=r, ebr=ebr, wb=wb, eb=eb, ebl=ebl):
                        sync.wait_ge(sems["cc"], cc_w)
                        if c_mug_h[ebl % 2]:
                            sync.wait_ge(sems["tensor"], c_mug_h[ebl % 2])
                        ins = []
                        for idx, agt in enumerate((ag_up_out, ag_gate_out)):
                            src = agt.ap()[E * r:E * (r + 1), 128 * ebr:128 * (ebr + 1)] \
                                .rearrange("(et p) o -> p et o", p=128)
                            dst = wug[wb].ap().rearrange("p (et c) -> p et c", c=128) \
                                [:, 16 * idx:16 * idx + 16, :]
                            ins.append(sync.dma_start(out=dst, in_=src))
                        g_ug_h[0] = pool.issue(ins)
                    on("sync", ldug)
                    g_ug = g_ug_h[0]

                    ps = ebl % 2
                    def mm_ug(t, wb=wb, g_ug=g_ug, c_hinT=c_hinT, eb=eb, ps=ps,
                              c_sfree=c_silu_done[ebl % 2], c_vfree=c_doh_done[ebl % 2]):
                        pool.wait(t, g_ug)
                        t.wait_ge(sems["vector"], c_hinT)
                        if c_sfree:
                            t.wait_ge(sems["scalar"], c_sfree)
                        if c_vfree:
                            t.wait_ge(sems["vector"], c_vfree)
                        for et in range(ET):
                            t.matmul(out=pup[ps][:], lhsT=wug[wb][:, 128 * et:128 * (et + 1)],
                                     rhs=hinT[:, 512 * et:512 * (et + 1)],
                                     start=(et == 0), stop=(et == ET - 1))
                        for et in range(ET):
                            ins = t.matmul(out=pgt[ps][:], lhsT=wug[wb][:, 2048 + 128 * et:2048 + 128 * (et + 1)],
                                           rhs=hinT[:, 512 * et:512 * (et + 1)],
                                           start=(et == 0), stop=(et == ET - 1))
                            if et == ET - 1:
                                inc("tensor", ins)
                    on("tensor", mm_ug)
                    c_mug = cnt["tensor"]
                    c_mug_h[ebl % 2] = c_mug

                    def dosilu(s, c_mug=c_mug, ps=ps, sb_=ebl % 2):
                        s.wait_ge(sems["tensor"], c_mug)
                        sinc(s.activation(out=silub[sb_][:], in_=pup[ps][:], func=AF.Silu,
                                          scale=scales_sb[:, SC_UP:SC_UP + 1]))
                    c_sil = sq(dosilu)
                    c_silu_done[ebl % 2] = c_sil

                    def doh(v, ebl=ebl, c_sil=c_sil, ps=ps, sb_=ebl % 2, grp=grp, eb=eb):
                        v.wait_ge(sems["scalar"], c_sil)
                        inc("vector", v.scalar_tensor_tensor(
                            out=hT[:, 512 * ebl:512 * (ebl + 1)], in0=pgt[ps][:],
                            scalar=scales_sb[:, SC_GATE:SC_GATE + 1], in1=silub[sb_][:],
                            op0=Alu.mult, op1=Alu.mult))
                        if DEBUG and grp == 0 and eb == 0:
                            v.drain()
                            inc("vector", v.tensor_copy(out=snap["h"][:], in_=hT[:, 0:512]))
                    on("vector", doh)
                    c_doh_done[ebl % 2] = cnt["vector"]
                c_h_ready = cnt["vector"]

                c_evd_done = [cnt["vector"], cnt["vector"]]
                c_md_h = [cnt["tensor"], cnt["tensor"]]
                for ch in range(4):
                    wb = ch % 2
                    g_dn_h = [None]

                    def lddn(sync, ch=ch, xh=xh, wb=wb):
                        sync.wait_ge(sems["cc"], cc_w)
                        if c_md_h[ch % 2]:
                            sync.wait_ge(sems["tensor"], c_md_h[ch % 2])
                        ins = []
                        for half in range(2):
                            rr = 2 * ch + half
                            src = ag_dn_out.ap()[DE * rr + 4096 * xh:DE * rr + 4096 * (xh + 1), :] \
                                .rearrange("(xt p) d -> p xt d", p=128)
                            dst = wdn[wb].ap().rearrange("p (xt c) -> p xt c", c=512) \
                                [:, :, 256 * half:256 * (half + 1)]
                            ins.append(sync.dma_start(out=dst, in_=src))
                        g_dn_h[0] = pool.issue(ins)
                    on("sync", lddn)
                    g_dn = g_dn_h[0]

                    for j2 in range(4):
                        pb = j2 % 2

                        def mm_d(t, j2=j2, pb=pb, wb=wb, g_dn=g_dn, c_h_ready=c_h_ready,
                                 c_free=c_evd_done[pb]):
                            pool.wait(t, g_dn)
                            t.wait_ge(sems["vector"], c_h_ready)
                            if c_free:
                                t.wait_ge(sems["vector"], c_free)
                            for xt in range(32):
                                ins = t.matmul(out=pd[pb][:],
                                               lhsT=hT[:, 512 * xt + 128 * j2:512 * xt + 128 * (j2 + 1)],
                                               rhs=wdn[wb][:, 512 * xt:512 * (xt + 1)],
                                               start=(xt == 0), stop=(xt == 31))
                                if xt == 31:
                                    inc("tensor", ins)
                        on("tensor", mm_d)
                        c_md = cnt["tensor"]
                        c_md_h[ch % 2] = c_md

                        def ev_d(v, j2=j2, pb=pb, ch=ch, xh=xh, c_md=c_md):
                            v.wait_ge(sems["tensor"], c_md)
                            if xh == 0:
                                ins = v.tensor_scalar(out=ysb[j2][:, 512 * ch:512 * (ch + 1)],
                                                      in0=pd[pb][:],
                                                      scalar1=scales_sb[:, SC_DOWN:SC_DOWN + 1],
                                                      scalar2=None, op0=Alu.mult)
                            else:
                                ins = v.scalar_tensor_tensor(out=ysb[j2][:, 512 * ch:512 * (ch + 1)],
                                                             in0=pd[pb][:],
                                                             scalar=scales_sb[:, SC_DOWN:SC_DOWN + 1],
                                                             in1=ysb[j2][:, 512 * ch:512 * (ch + 1)],
                                                             op0=Alu.mult, op1=Alu.add)
                            inc("vector", ins)
                        on("vector", ev_d)
                        c_evd_done[pb] = cnt["vector"]
            c_y = cnt["vector"]

            for j2 in range(4):
                j = 4 * grp + j2
                yt = ysb[j2]

                def yln_v(v, yt=yt, c_y=c_y):
                    v.wait_ge(sems["vector"], c_y)
                    inc("vector", v.tensor_reduce(out=sm[:, FS1:FS1 + 1], in_=yt[:],
                                                  axis=AX.X, op=Alu.add))
                on("vector", yln_v)
                c_ys1 = cnt["vector"]

                def yln_s(s, yt=yt, c_ys1=c_ys1):
                    s.wait_ge(sems["vector"], c_ys1)
                    sinc(s.activation(out=fsq[:], in_=yt[:], func=AF.Square,
                                      accum_out=sm[:, FEX2:FEX2 + 1]))
                c_ysq = sq(yln_s)

                def yln_v2(v, c_ysq=c_ysq):
                    v.wait_ge(sems["scalar"], c_ysq)
                    P = lambda k: sm[:, k:k + 1]
                    v.tensor_scalar(out=P(FMEAN), in0=P(FS1), scalar1=1.0 / E, scalar2=None, op0=Alu.mult)
                    v.tensor_scalar(out=P(FEX2), in0=P(FEX2), scalar1=1.0 / E, scalar2=None, op0=Alu.mult)
                    v.drain()
                    v.tensor_tensor(out=P(FM2), in0=P(FMEAN), in1=P(FMEAN), op=Alu.mult)
                    v.drain()
                    inc("vector", v.tensor_tensor(out=P(FVAR), in0=P(FEX2), in1=P(FM2), op=Alu.subtract))
                on("vector", yln_v2)
                c_yv = cnt["vector"]

                def yln_s2(s, c_yv=c_yv):
                    s.wait_ge(sems["vector"], c_yv)
                    s.activation(out=sm[:, FSQV:FSQV + 1], in_=sm[:, FVAR:FVAR + 1],
                                 func=AF.Ln, bias=eps_t[:, 0:1], scale=1.0)
                    s.drain()
                    sinc(s.activation(out=sm[:, FRSTD:FRSTD + 1],
                                      in_=sm[:, FSQV:FSQV + 1],
                                      func=AF.Exp, scale=neghalf_t[:, 0:1]))
                c_ysqv = sq(yln_s2)

                def ynorm(v, yt=yt, c_ysqv=c_ysqv):
                    v.wait_ge(sems["scalar"], c_ysqv)
                    P = lambda k: sm[:, k:k + 1]
                    v.tensor_scalar(out=P(FBT), in0=P(FMEAN), scalar1=P(FRSTD),
                                    scalar2=-1.0, op0=Alu.mult, op1=Alu.mult)
                    v.drain()
                    inc("vector", v.tensor_scalar(out=fsq[:], in0=yt[:],
                                                  scalar1=P(FRSTD), scalar2=P(FBT),
                                                  op0=Alu.mult, op1=Alu.add))
                on("vector", ynorm)
                c_yn = cnt["vector"]

                g_x1b_h = [None]

                def ldx1b(sync, j=j, c_yn=c_yn):
                    sync.wait_ge(sems["vector"], c_yn)
                    pool.wait(sync, g_x1w_h[j])
                    g_x1b_h[0] = pool.issue([sync.dma_start(out=x1t[:],
                                                            in_=x1_hbm[128 * j:128 * (j + 1), :])])
                on("sync", ldx1b)
                g_x1b = g_x1b_h[0]

                def yres(v, yt=yt, j=j, g_x1b=g_x1b):
                    pool.wait(v, g_x1b)
                    inc("vector", v.tensor_tensor(out=yt[:], in0=fsq[:], in1=x1t[:], op=Alu.add))
                    if DEBUG and j == 0:
                        inc("vector", v.tensor_copy(out=snap["y"][:], in_=yt[:, 0:1024]))
                on("vector", yres)
                c_out = cnt["vector"]

                def stout(sync, j=j, yt=yt, c_out=c_out):
                    sync.wait_ge(sems["vector"], c_out)
                    inc("out", sync.dma_start(out=out_ext[128 * j:128 * (j + 1), :], in_=yt[:]), 16)
                on("sync", stout)
        sflush()
        es_p5.close()

        if DEBUG:
            def ddump(sync):
                sync.wait_ge(sems["vector"], cnt["vector"])
                for nm_s, nm_d in [("sm", "dbg_sm"), ("xq", "dbg_xq"),
                                   ("ao", "dbg_attnout"),
                                   ("rec", "dbg_sums"), ("oin", "dbg_oin"),
                                   ("x1", "dbg_x1"), ("h", "dbg_h"), ("y", "dbg_y")]:
                    inc("out", sync.dma_start(out=dbg[nm_d], in_=snap[nm_s][:]), 16)
            on("sync", ddump)

        def fin(sync):
            sync.wait_ge(sems["out"], cnt["out"])
        on("sync", fin)

    return nc


_cache = {}


def _host_consts():
    t = np.arange(128)[:, None]
    j = np.arange(896)[None, :]
    cmask = np.where((j - 384) >= t, 0.0, MASKV).astype(np.float32)
    ident = np.eye(128, dtype=ml_dtypes.bfloat16)
    return cmask, ident


def kernel(L0_attn_x, L0_attn_w_q, L0_attn_w_k, L0_attn_w_v, L0_attn_w_o,
           L0_ffn_W_up, L0_ffn_W_gate, L0_ffn_W_down):
    if "nc" not in _cache:
        _cache["nc"] = build_program()
    nc = _cache["nc"]

    x2d = np.ascontiguousarray(np.asarray(L0_attn_x, dtype=np.float32).reshape(TOK, E))
    wq = np.ascontiguousarray(np.asarray(L0_attn_w_q, dtype=np.float32))
    wk = np.ascontiguousarray(np.asarray(L0_attn_w_k, dtype=np.float32))
    wv = np.ascontiguousarray(np.asarray(L0_attn_w_v, dtype=np.float32))
    wo = np.ascontiguousarray(np.asarray(L0_attn_w_o, dtype=np.float32))
    wup = np.ascontiguousarray(np.asarray(L0_ffn_W_up, dtype=np.float32))
    wgate = np.ascontiguousarray(np.asarray(L0_ffn_W_gate, dtype=np.float32))
    wdown = np.ascontiguousarray(np.asarray(L0_ffn_W_down, dtype=np.float32))
    cmask, ident = _host_consts()

    in_maps = []
    for c in range(NC):
        in_maps.append({
            "x_full": x2d,
            "x_mine": np.ascontiguousarray(x2d[TPC * c:TPC * (c + 1)]),
            "wq_s": np.ascontiguousarray(wq[256 * c:256 * (c + 1)]),
            "wk_s": np.ascontiguousarray(wk[256 * c:256 * (c + 1)]),
            "wv_s": np.ascontiguousarray(wv[256 * c:256 * (c + 1)]),
            "wo_s": np.ascontiguousarray(wo[256 * c:256 * (c + 1)]),
            "up_s": np.ascontiguousarray(wup[1024 * c:1024 * (c + 1)]),
            "gate_s": np.ascontiguousarray(wgate[1024 * c:1024 * (c + 1)]),
            "down_s": np.ascontiguousarray(wdown[256 * c:256 * (c + 1)]),
            "cmask_in": cmask,
            "ident_in": ident,
        })
    res = run_bass_kernel_spmd(nc, in_maps, core_ids=list(range(NC)))
    if DEBUG:
        _cache["results"] = res
    out = np.concatenate([np.asarray(res.results[c]["out"]) for c in range(NC)], axis=0)
    return out.reshape(B, S, E).astype(np.float32)
